# version 07
# baseline (speedup 1.0000x reference)
import contextlib
import numpy as np
import ml_dtypes
import concourse.bass as bass
import concourse.mybir as mybir
from concourse.bass_utils import run_bass_kernel_spmd

F32 = mybir.dt.float32
BF16 = mybir.dt.bfloat16
AF = mybir.ActivationFunctionType
ALU = mybir.AluOpType
AX = mybir.AxisListType
BF = ml_dtypes.bfloat16

NCORES = 8
PPC = 2048
NT = 8
T = 512
H = 256
D = 2048
INTER = 8192
SCALE0 = float(2048 ** -0.5) * 0.0 + float(128 ** -0.5)
EPS = 1e-5


def _hadamard(n):
    Hm = np.array([[1.0]], dtype=np.float64)
    while Hm.shape[0] < n:
        Hm = np.block([[Hm, Hm], [Hm, -Hm]])
    return Hm.astype(np.float32)


def _host_constants():
    H128 = _hadamard(128)
    H64 = _hadamard(64)
    s2 = np.float32((1.0 / np.sqrt(np.float32(2048.0))) / 64.0)
    s8 = np.float32((1.0 / np.sqrt(np.float32(8192.0))) / 64.0)
    c = {}
    c["identb"] = np.eye(128, dtype=np.float32).astype(BF)
    c["identf"] = np.eye(128, dtype=np.float32)
    c["H128a"] = (H128 * s2).astype(BF)
    c["H128s"] = (H128 * s8).astype(BF)
    hdd = np.zeros((128, 128), np.float32)
    hdd[:64, :64] = H64
    hdd[64:, 64:] = H64
    c["Hdd"] = hdd.astype(BF)
    selS = np.zeros((128, 16, 16), np.float32)
    for h in range(16):
        selS[:, h, h] = SCALE0
    c["selS"] = selS.astype(BF)
    selB = np.zeros((16, 16, 128), np.float32)
    for h in range(16):
        selB[h, h, :] = 1.0
    c["selB"] = selB.astype(BF)
    selM = np.zeros((8, 8, 128), np.float32)
    for e in range(8):
        selM[e, e, :] = 1.0
    c["selM"] = selM.astype(BF)
    return c


def _host_weights(inp):
    w = {}
    w["WqT"] = np.ascontiguousarray(inp["L0_attn_Wq"].T)
    w["WkT"] = np.ascontiguousarray(inp["L0_attn_Wk"].T)
    w["WvT"] = np.ascontiguousarray(inp["L0_attn_Wv"].T)
    w["WoT"] = np.ascontiguousarray(inp["L0_attn_Wo"].T)
    w["bq"] = np.ascontiguousarray(inp["L0_attn_bq"])
    w["bk"] = np.ascontiguousarray(inp["L0_attn_bk"])
    w["bv"] = np.ascontiguousarray(inp["L0_attn_bv"])
    w["bo"] = np.ascontiguousarray(inp["L0_attn_bo"])
    w["WupT"] = np.ascontiguousarray(inp["L0_ffn_W_up"].T)
    w["WgateT"] = np.ascontiguousarray(inp["L0_ffn_W_gate"].T)
    w["WdownT"] = np.ascontiguousarray(inp["L0_ffn_W_down"].T)
    w["SU_ug"] = np.ascontiguousarray(inp["L0_ffn_SU_ug"])
    w["SU_dp"] = np.ascontiguousarray(inp["L0_ffn_SU_dp"])
    w["Wscale_ug"] = np.ascontiguousarray(inp["L0_ffn_Wscale_ug"])
    w["Wscale_dp"] = np.ascontiguousarray(inp["L0_ffn_Wscale_dp"])
    Wqkv = inp["L1_attn_weight_qkv"]
    order = []
    for h in range(16):
        g, kk = h // 4, h % 4
        order.extend(range(g * 768 + kk * 128, g * 768 + kk * 128 + 128))
    for g in range(4):
        order.extend(range(g * 768 + 512, g * 768 + 640))
    for g in range(4):
        order.extend(range(g * 768 + 640, g * 768 + 768))
    w["WqkvT"] = np.ascontiguousarray(Wqkv[np.array(order), :].T)
    w["WprojT"] = np.ascontiguousarray(inp["L1_attn_weight_proj"].T)
    w["wg"] = np.ascontiguousarray(inp["L1_ffn_wg"])
    We = inp["L1_ffn_W"]
    WeT = We.transpose(0, 2, 1)
    w["We_blk"] = np.ascontiguousarray(
        WeT.reshape(8, 16, 128, 16, 128).transpose(0, 3, 1, 2, 4))
    w["b_moe"] = np.ascontiguousarray(inp["L1_ffn_b"])
    return w


def _prefix_mask(core):
    m = np.zeros((16, 2), np.float32)
    for cc in range(core):
        m[2 * cc, 0] = 1.0
        m[2 * cc + 1, 1] = 1.0
    return m



class IO:

    def __init__(self, nc, fused):
        self.nc = nc
        self.fused = fused
        self.cache = {}

    def d(self, name, shape, dtype, role):
        if name in self.cache:
            return self.cache[name]
        if role.startswith("scr") and self.fused:
            t = self.nc.dram_tensor(name, shape, dtype)
        else:
            kind = {"in": "ExternalInput", "out": "ExternalOutput",
                    "scr_in": "ExternalInput", "scr_out": "ExternalOutput"}[role]
            t = self.nc.dram_tensor(name, shape, dtype, kind=kind)
        ap = t.ap()
        self.cache[name] = ap
        return ap


NQ = 16


def qtok(j, n=NQ):
    return j % n, 16 * (j // n + 1)


class DQ:

    def __init__(self, nc, es, name, n=NQ):
        self.sems = [es.enter_context(nc.semaphore(name + str(i))) for i in range(n)]
        self.n = n
        self.keymap = {}
        self.count = 0

    def alloc(self, key):
        if key in self.keymap:
            raise KeyError(f"duplicate DMA key {key}")
        self.keymap[key] = self.count
        self.count += 1
        return self.keymap[key]

    def j(self, key):
        return self.keymap[key]

    def inc(self, ins, j):
        ins.then_inc(self.sems[j % self.n], 16)

    def dma(self, eng, j, out, in_, **kw):
        if j >= self.n:
            self.wait_j(eng, j - self.n)
        ins = eng.dma_start(out=out, in_=in_, **kw)
        self.inc(ins, j)
        return ins

    def wait_j(self, eng, j):
        s, v = qtok(j, self.n)
        eng.wait_ge(self.sems[s], v)

    def wait_all_j(self, eng, j_hi):
        for s in range(self.n):
            j = j_hi - ((j_hi % self.n - s) % self.n)
            if j >= 0:
                eng.wait_ge(self.sems[s], 16 * (j // self.n + 1))


class Emitter:

    def __init__(self, stage, name, ordered):
        self.stage = stage
        self.name = name
        self.ordered = ordered
        self.n = 0
        self.m = 0
        self.eng = None
        self.sem = None
        self.lsem = None
        self.last_wait = 0

    def _label(self, label):
        self.m += 1
        if self.eng is None:
            assert label not in self.stage.labels, label
            self.stage.labels[label] = (self.name, self.m)

    def op(self, fn, label=None):
        self.n += 1
        if label is not None:
            self._label(label)
        if self.eng is None:
            return None
        if self.ordered and self.n - 1 > self.last_wait:
            self.eng.wait_ge(self.sem, self.n - 1)
        self.last_wait = max(self.last_wait, self.n - 1)
        ins = fn(self.eng)
        ins.then_inc(self.sem, 1)
        if label is not None:
            self.eng.drain().then_inc(self.lsem, 1)
        return ins

    def unordered_op(self, fn, label=None):
        if label is None:
            if self.eng is not None:
                fn(self.eng)
            return None
        self._label(label)
        if self.eng is None:
            return None
        ins = fn(self.eng)
        ins.then_inc(self.lsem, 1)
        return ins

    def unordered_drain_op(self, fn, label):
        self._label(label)
        if self.eng is None:
            return None
        fn(self.eng)
        self.eng.drain().then_inc(self.lsem, 1)
        return None

    def wait(self, label):
        if self.eng is None:
            return
        ename, pos = self.stage.labels[label]
        self.eng.wait_ge(self.stage.lsems[ename], pos)

    def wait_dma(self, dq, key):
        if self.eng is None:
            return
        dq.wait_j(self.eng, dq.j((self.stage.name,) + tuple(key)))

    def wait_dma_all(self, dq, key_hi):
        if self.eng is None:
            return
        dq.wait_all_j(self.eng, dq.j((self.stage.name,) + tuple(key_hi)))

    def dma(self, dq, key, out, in_, **kw):
        if self.eng is None:
            dq.alloc((self.stage.name,) + tuple(key))
            return
        dq.dma(self.eng, dq.j((self.stage.name,) + tuple(key)), out, in_, **kw)


class GlobalSync:

    ENGINES = [("sync", False), ("scalar", True), ("vector", True),
               ("tensor", False), ("gpsimd", True)]

    def __init__(self, nc, es):
        self.nc = nc
        self.es = es
        self.sems = {}
        self.lsems = {}
        self.counts = {}
        self.lcounts = {}
        for ename, _ in GlobalSync.ENGINES:
            self.sems[ename] = es.enter_context(nc.semaphore(f"g_{ename}"))
            self.lsems[ename] = es.enter_context(nc.semaphore(f"g_l_{ename}"))
            self.counts[ename] = 0
            self.lcounts[ename] = 0
        self.dqs = {}

    def dq(self, name):
        if name not in self.dqs:
            self.dqs[name] = DQ(self.nc, self.es, name)
        return self.dqs[name]


class Stage:

    def __init__(self, nc, es, name, gs=None):
        self.nc = nc
        self.name = name
        self.gs = gs if gs is not None else GlobalSync(nc, es)
        self.labels = {}
        self.sems = self.gs.sems
        self.lsems = self.gs.lsems
        self.progs = {}

    def engine(self, ename):
        def deco(fn):
            self.progs[ename] = fn
            return fn
        return deco

    def run(self):
        order = dict(GlobalSync.ENGINES)
        base_n = dict(self.gs.counts)
        base_m = dict(self.gs.lcounts)
        for ename, fn in self.progs.items():
            em = Emitter(self, ename, order[ename])
            em.n = base_n[ename]
            em.m = base_m[ename]
            em.last_wait = base_n[ename]
            fn(em)
            self.gs.counts[ename] = em.n
            self.gs.lcounts[ename] = em.m
        with self.nc.Block() as blk:
            for ename, fn in self.progs.items():
                em = Emitter(self, ename, order[ename])
                em.n = base_n[ename]
                em.m = base_m[ename]
                em.last_wait = base_n[ename]
                em.sem = self.sems[ename]
                em.lsem = self.lsems[ename]

                def section(eng, em=em, fn=fn):
                    em.eng = eng
                    fn(em)
                getattr(blk, ename)(section)



def build_stageA(nc, io, es, gs=None):
    x = io.d("x", [2, PPC, D], F32, "in")
    WkT = io.d("WkT", [D, D], F32, "in")
    WvT = io.d("WvT", [D, D], F32, "in")
    bk = io.d("bk", [D], F32, "in")
    bv = io.d("bv", [D], F32, "in")
    identb = io.d("identb", [128, 128], BF16, "in")
    s_xn = io.d("s_xn", [16, 128, NT * T], BF16, "scr_out")
    s_kv = io.d("s_kv", [16, 128, NT * T], F32, "scr_out")
    totals = io.d("totals", [2, D], F32, "scr_out")

    E = es.enter_context
    wk_sb = E(nc.sbuf_tensor("wk_sb", [128, 16, 16, 128], BF16))
    wv_sb = E(nc.sbuf_tensor("wv_sb", [128, 16, 16, 128], BF16))
    wstripe = E(nc.sbuf_tensor("wstripe", [128, 2, D], F32))
    bkv = E(nc.sbuf_tensor("bkv", [128, 16, 2], F32))
    idb = E(nc.sbuf_tensor("idb", [128, 128], BF16))
    xbuf = E(nc.sbuf_tensor("xbuf", [128, 2, D], F32))
    stat = E(nc.sbuf_tensor("stat", [128, 4, 8], F32))
    xntok = E(nc.sbuf_tensor("xntok", [128, 2, D], BF16))
    xnf = E(nc.sbuf_tensor("xnf", [128, 16, T], BF16))
    relu_sb = E(nc.sbuf_tensor("relu_sb", [128, T], F32))
    mn_sb = E(nc.sbuf_tensor("mn_sb", [128, T], F32))
    exp_sb = E(nc.sbuf_tensor("exp_sb", [128, T], F32))
    phi_sb = E(nc.sbuf_tensor("phi_sb", [128, T], F32))
    u_sb = E(nc.sbuf_tensor("u_sb", [128, T], F32))
    kvb = E(nc.sbuf_tensor("kvb", [128, 2, T], F32))
    carry = E(nc.sbuf_tensor("carry", [128, 16, 2], F32))
    zeros = E(nc.sbuf_tensor("zeros", [128, H], F32))
    epsc = E(nc.sbuf_tensor("epsc", [128, 1], F32))
    ptr = E(nc.psum_tensor("ptr", [128, 2, 2048], BF16))
    pk = E(nc.psum_tensor("pk", [128, 2, T], F32))
    pv = E(nc.psum_tensor("pv", [128, 2, T], F32))
    stg = Stage(nc, es, "A", gs)
    gs = stg.gs
    ld = gs.dq("q_sync")
    st = gs.dq("q_gpsimd")

    def t_stripe(j):
        return ("stripe", j)
    def t_bias(c, w_):
        return ("bias", c, w_)
    T_IDENT = ("ident",)
    def t_x(tg):
        return ("x", tg)
    def t_xn(i, fc):
        return ("xn", i, fc)
    def t_kv(i, ec):
        return ("kv", i, ec)
    def t_tot(ec, sq):
        return ("tot", ec, sq)


    @stg.engine("sync")
    def _(e):
        for wi, wsrc in enumerate([WkT, WvT]):
            for dc in range(16):
                j = wi * 16 + dc
                if j >= 2:
                    e.wait(("wcast", j - 2))
                e.dma(ld, t_stripe(j), wstripe[:, j % 2, :], wsrc[dc * 128:(dc + 1) * 128, :])
        for c in range(16):
            e.dma(ld, t_bias(c, 0), bkv[:, c, 0:1], bk[c * 128:(c + 1) * 128].rearrange("(p one) -> p one", one=1))
            e.dma(ld, t_bias(c, 1), bkv[:, c, 1:2], bv[c * 128:(c + 1) * 128].rearrange("(p one) -> p one", one=1))
        e.dma(ld, T_IDENT, idb[:, :], identb)
        for i in range(NT):
            for tc in range(4):
                tg = i * 4 + tc
                if tg >= 2:
                    e.wait(("apply", tg - 2))
                    e.wait(("rsum", tg - 2))
                sq = tc // 2
                p0 = i * 256 + (tc % 2) * 128
                e.dma(ld, t_x(tg), xbuf[:, tg % 2, :], x[sq, p0:p0 + 128, :])

    @stg.engine("vector")
    def _(e):
        e.op(lambda v: v.memset(zeros[:, :], 0.0))
        e.op(lambda v: v.memset(carry[:, :, :], 0.0))
        e.op(lambda v: v.memset(epsc[:, :], EPS))
        for wi, wdst in enumerate([wk_sb, wv_sb]):
            for dc in range(16):
                j = wi * 16 + dc
                e.wait_dma(ld, t_stripe(j))
                e.op(lambda v, wdst=wdst, dc=dc, j=j: v.tensor_copy(
                    wdst[:, dc, :, :],
                    wstripe[:, j % 2, :].rearrange("p (e m) -> p e m", m=128)),
                    label=("wcast", j))
        e.wait_dma_all(ld, T_IDENT)
        for i in range(NT):
            for tc in range(4):
                tg = i * 4 + tc
                e.wait_dma(ld, t_x(tg))
                e.op(lambda v, tc=tc, tg=tg: v.reduce_sum(stat[:, tc, 0:1], xbuf[:, tg % 2, :], AX.X),
                     label=("rsum", tg))
                e.wait(("sq", tg))
                e.op(lambda v, tc=tc: v.tensor_scalar(stat[:, tc, 2:3], stat[:, tc, 0:1], 1.0 / D, None, ALU.mult))
                e.op(lambda v, tc=tc: v.tensor_tensor(stat[:, tc, 7:8], stat[:, tc, 2:3], stat[:, tc, 2:3], ALU.mult))
                e.op(lambda v, tc=tc: v.tensor_scalar(stat[:, tc, 3:4], stat[:, tc, 1:2], 1.0 / D, None, ALU.mult))
                e.op(lambda v, tc=tc: v.tensor_tensor(stat[:, tc, 3:4], stat[:, tc, 3:4], stat[:, tc, 7:8], ALU.subtract),
                     label=("var", tg))
                e.wait(("sqrt", tg))
                e.op(lambda v, tc=tc: v.reciprocal(stat[:, tc, 5:6], stat[:, tc, 4:5]))
                e.op(lambda v, tc=tc: v.scalar_tensor_tensor(stat[:, tc, 6:7], stat[:, tc, 2:3], -1.0,
                                                             stat[:, tc, 5:6], ALU.mult, ALU.mult),
                     label=("rstd", tg))
            for tc in range(4):
                tg = i * 4 + tc
                e.wait(("tr", tg))
                if tc == 0 and i >= 1:
                    e.wait_dma_all(st, t_xn(i - 1, 15))
                e.op(lambda v, tc=tc, tg=tg: v.tensor_copy(
                    xnf[:, :, tc * 128:(tc + 1) * 128],
                    ptr[:, tg % 2, :].rearrange("p (c m) -> p c m", m=128)),
                    label=("evac", tg))
            for ec in range(16):
                g = i * 16 + ec
                e.wait(("mk", g))
                e.op(lambda v, ec=ec: v.tensor_scalar(relu_sb[:, :], pk[:, ec % 2, :], bkv[:, ec, 0:1], 0.0,
                                                      ALU.add, ALU.max), label=("relu", g))
                e.op(lambda v, ec=ec: v.tensor_scalar(mn_sb[:, :], pk[:, ec % 2, :], bkv[:, ec, 0:1], 0.0,
                                                      ALU.add, ALU.min), label=("mn", g))
                e.wait(("exp", g))
                e.op(lambda v: v.tensor_tensor(phi_sb[:, :], relu_sb[:, :], exp_sb[:, :], ALU.add),
                     label=("phi", g))
                e.wait(("mv", g))
                e.op(lambda v, ec=ec: v.scalar_tensor_tensor(u_sb[:, :], pv[:, ec % 2, :], bkv[:, ec, 1:2],
                                                             phi_sb[:, :], ALU.add, ALU.mult),
                     label=("u", g))
                if g >= 2:
                    e.wait_dma(st, t_kv((g - 2) // 16, (g - 2) % 16))
                e.op(lambda v, ec=ec, g=g: v.tensor_tensor_scan(kvb[:, g % 2, 0:H], u_sb[:, 0:H], zeros[:, :],
                                                                carry[:, ec, 0:1], ALU.add, ALU.add))
                e.op(lambda v, ec=ec, g=g: v.tensor_tensor_scan(kvb[:, g % 2, H:T], u_sb[:, H:T], zeros[:, :],
                                                                carry[:, ec, 1:2], ALU.add, ALU.add),
                     label=("scan", g))
                e.op(lambda v, ec=ec, g=g: v.tensor_copy(carry[:, ec, 0:1], kvb[:, g % 2, H - 1:H]))
                e.op(lambda v, ec=ec, g=g: v.tensor_copy(carry[:, ec, 1:2], kvb[:, g % 2, T - 1:T]),
                     label=("carry", g))

    @stg.engine("scalar")
    def _(e):
        e.wait_dma_all(ld, T_IDENT)
        for i in range(NT):
            for tc in range(4):
                tg = i * 4 + tc
                e.wait_dma(ld, t_x(tg))
                if tg >= 2:
                    e.wait(("tr", tg - 2))
                e.op(lambda s_, tc=tc, tg=tg: s_.activation(xntok[:, tg % 2, :], xbuf[:, tg % 2, :], AF.Square,
                                                            bias=0.0, scale=1.0, accum_out=stat[:, tc, 1:2]),
                     label=("sq", tg))
                e.wait(("var", tg))
                e.op(lambda s_, tc=tc: s_.activation(stat[:, tc, 4:5], stat[:, tc, 3:4], AF.Sqrt,
                                                     bias=epsc[:, 0:1], scale=1.0), label=("sqrt", tg))
                e.wait(("rstd", tg))
                e.op(lambda s_, tc=tc, tg=tg: s_.activation(xntok[:, tg % 2, :], xbuf[:, tg % 2, :], AF.Identity,
                                                            bias=stat[:, tc, 6:7], scale=stat[:, tc, 5:6]),
                     label=("apply", tg))
            for ec in range(16):
                g = i * 16 + ec
                if g >= 1:
                    e.wait(("phi", g - 1))
                e.wait(("mn", g))
                e.op(lambda s_: s_.activation(exp_sb[:, :], mn_sb[:, :], AF.Exp,
                                              bias=0.0, scale=1.0), label=("exp", g))

    @stg.engine("tensor")
    def _(e):
        e.wait_dma(ld, T_IDENT)
        e.wait(("wcast", 31))
        for i in range(NT):
            for tc in range(4):
                tg = i * 4 + tc
                e.wait(("apply", tg))
                if tg >= 2:
                    e.wait(("evac", tg - 2))
                for fc in range(15):
                    e.unordered_op(
                        lambda t, tg=tg, fc=fc: t.transpose(ptr[:, tg % 2, fc * 128:(fc + 1) * 128],
                                                            xntok[:, tg % 2, fc * 128:(fc + 1) * 128], idb[:, :]))
                e.unordered_drain_op(
                    lambda t, tg=tg: t.transpose(ptr[:, tg % 2, 15 * 128:16 * 128],
                                                 xntok[:, tg % 2, 15 * 128:16 * 128], idb[:, :]),
                    label=("tr", tg))
            e.wait(("evac", i * 4 + 3))
            for ec in range(16):
                g = i * 16 + ec
                if g >= 2:
                    e.wait(("relu", g - 2))
                    e.wait(("u", g - 2))
                for dc in range(16):
                    e.unordered_op(
                        lambda t, dc=dc, ec=ec: t.matmul(pk[:, ec % 2, :], wk_sb[:, dc, ec, :],
                                                         xnf[:, dc, :], start=(dc == 0), stop=(dc == 15)),
                        label=("mk", g) if dc == 15 else None)
                for dc in range(16):
                    e.unordered_op(
                        lambda t, dc=dc, ec=ec: t.matmul(pv[:, ec % 2, :], wv_sb[:, dc, ec, :],
                                                         xnf[:, dc, :], start=(dc == 0), stop=(dc == 15)),
                        label=("mv", g) if dc == 15 else None)

    @stg.engine("gpsimd")
    def _(e):
        for i in range(NT):
            e.wait(("evac", i * 4 + 3))
            for fc in range(16):
                e.dma(st, t_xn(i, fc), s_xn[fc, :, i * T:(i + 1) * T], xnf[:, fc, :])
            for ec in range(16):
                g = i * 16 + ec
                e.wait(("scan", g))
                e.dma(st, t_kv(i, ec), s_kv[ec, :, i * T:(i + 1) * T], kvb[:, g % 2, :])
        e.wait(("carry", NT * 16 - 1))
        for ec in range(16):
            for sq in range(2):
                e.dma(st, t_tot(ec, sq),
                      totals[sq, ec * 128:(ec + 1) * 128].rearrange("(p one) -> p one", one=1),
                      carry[:, ec, sq:sq + 1])
        e.wait_dma_all(st, t_tot(15, 1))

    stg.run()
    return ["s_xn", "s_kv", "totals"]



def build_stageB1(nc, io, es, gs=None):
    s_xn = io.d("s_xn", [16, 128, NT * T], BF16, "scr_in")
    s_kv = io.d("s_kv", [16, 128, NT * T], F32, "scr_in")
    ag = io.d("ag", [16, D], F32, "scr_in")
    pmask = io.d("pmask", [16, 2], F32, "in")
    WqT = io.d("WqT", [D, D], F32, "in")
    WoT = io.d("WoT", [D, D], F32, "in")
    bq = io.d("bq", [D], F32, "in")
    bo = io.d("bo", [D], F32, "in")
    identf = io.d("identf", [128, 128], F32, "in")
    s_ratt = io.d("s_ratt", [16, 128, NT * T], F32, "scr_out")

    E = es.enter_context
    wq_sb = E(nc.sbuf_tensor("wq_sb", [128, 16, 16, 128], BF16))
    wo_sb = E(nc.sbuf_tensor("wo_sb", [128, 16, 16, 128], BF16))
    wstripe = E(nc.sbuf_tensor("B1stripe", [128, 1, D], F32))
    bqo = E(nc.sbuf_tensor("bqo", [128, 16, 2], F32))
    idf = E(nc.sbuf_tensor("B1idf", [128, 128], F32))
    ag_sb = E(nc.sbuf_tensor("ag_sb", [16, D], F32))
    pm_sb = E(nc.sbuf_tensor("pm_sb", [16, 2], F32))
    prow = E(nc.sbuf_tensor("prow", [2, D], F32))
    prefix = E(nc.sbuf_tensor("prefix", [128, 16, 2], F32))
    xn_sb = E(nc.sbuf_tensor("B1xn", [128, 16, T], BF16))
    kv_sb = E(nc.sbuf_tensor("B1kv", [128, 2, T], F32))
    kvp_sb = E(nc.sbuf_tensor("B1kvp", [128, T], F32))
    relu_sb = E(nc.sbuf_tensor("B1relu", [128, T], F32))
    mn_sb = E(nc.sbuf_tensor("B1mn", [128, T], F32))
    exp_sb = E(nc.sbuf_tensor("B1exp", [128, T], F32))
    phi_sb = E(nc.sbuf_tensor("B1phi", [128, T], F32))
    ctx_sb = E(nc.sbuf_tensor("B1ctx", [128, 16, T], BF16))
    ratt_sb = E(nc.sbuf_tensor("B1ratt", [128, 2, T], F32))
    pq = E(nc.psum_tensor("B1pq", [128, 3, T], F32))
    po = E(nc.psum_tensor("B1po", [128, 3, T], F32))
    pp = E(nc.psum_tensor("B1pp", [2, T], F32))
    ppt = E(nc.psum_tensor("B1ppt", [128, 32], F32))
    stg = Stage(nc, es, "B1", gs)
    gs = stg.gs
    ld = gs.dq("q_sync")
    st = gs.dq("q_gpsimd")

    def t_stripe(j):
        return ("stripe", j)
    def t_bias(c, w_):
        return ("bias", c, w_)
    T_IDF = ("idf",)
    T_AG = ("ag",)
    T_PM = ("pm",)
    def t_xn(i):
        return ("xn", i)
    def t_kv(i, ec):
        return ("kv", i, ec)
    def t_ratt(i, oc):
        return ("ratt", i, oc)


    @stg.engine("sync")
    def _(e):
        for wi, wsrc in enumerate([WqT, WoT]):
            for dc in range(16):
                j = wi * 16 + dc
                if j >= 1:
                    e.wait(("wcast", j - 1))
                e.dma(ld, t_stripe(j), wstripe[:, 0, :], wsrc[dc * 128:(dc + 1) * 128, :])
        for c in range(16):
            e.dma(ld, t_bias(c, 0), bqo[:, c, 0:1], bq[c * 128:(c + 1) * 128].rearrange("(p one) -> p one", one=1))
            e.dma(ld, t_bias(c, 1), bqo[:, c, 1:2], bo[c * 128:(c + 1) * 128].rearrange("(p one) -> p one", one=1))
        e.dma(ld, T_IDF, idf[:, :], identf)
        e.dma(ld, T_AG, ag_sb[:, :], ag)
        e.dma(ld, T_PM, pm_sb[:, :], pmask)
        for i in range(NT):
            if i >= 1:
                e.wait(("ctxdone", i - 1))
            e.dma(ld, t_xn(i), xn_sb[:, :, :],
                  s_xn.transpose([1, 0, 2])[:, :, i * T:(i + 1) * T])
            for ec in range(16):
                g = i * 16 + ec
                if g >= 2:
                    e.wait(("kvp", g - 2))
                e.dma(ld, t_kv(i, ec), kv_sb[:, g % 2, :], s_kv[ec, :, i * T:(i + 1) * T])

    @stg.engine("tensor")
    def _(e):
        e.wait_dma(ld, T_AG)
        e.wait_dma(ld, T_PM)
        e.wait_dma(ld, T_IDF)
        for ng in range(4):
            e.unordered_op(lambda t, ng=ng: t.matmul(pp[:, :], pm_sb[:, :], ag_sb[:, ng * 512:(ng + 1) * 512],
                                                     start=True, stop=True),
                           label=("ppm", ng))
            e.wait(("ppev", ng))
        for fc in range(16):
            e.unordered_op(lambda t, fc=fc: t.transpose(ppt[:, fc * 2:(fc + 1) * 2],
                                                        prow[:, fc * 128:(fc + 1) * 128], idf[0:2, 0:2]))
        e.unordered_drain_op(lambda t: t.nop(), label=("ptr",))
        e.wait(("wcast", 31))
        for i in range(NT):
            e.wait_dma(ld, t_xn(i))
            for ec in range(16):
                g = i * 16 + ec
                if g >= 3:
                    e.wait(("mn", g - 3))
                for dc in range(16):
                    e.unordered_op(lambda t, dc=dc, ec=ec, g=g: t.matmul(pq[:, g % 3, :], wq_sb[:, dc, ec, :],
                                                                         xn_sb[:, dc, :], start=(dc == 0), stop=(dc == 15)),
                                   label=("mq", g) if dc == 15 else None)
            e.wait(("ctx", i, 15))
            for oc in range(16):
                go = i * 16 + oc
                if go >= 3:
                    e.wait(("rattev", go - 3))
                for ec in range(16):
                    e.unordered_op(lambda t, ec=ec, oc=oc, go=go: t.matmul(po[:, go % 3, :], wo_sb[:, ec, oc, :],
                                                                           ctx_sb[:, ec, :], start=(ec == 0), stop=(ec == 15)),
                                   label=("mo", i, oc) if ec == 15 else None)

    @stg.engine("vector")
    def _(e):
        for wi, wdst in enumerate([wq_sb, wo_sb]):
            for dc in range(16):
                j = wi * 16 + dc
                e.wait_dma(ld, t_stripe(j))
                e.op(lambda v, wdst=wdst, dc=dc, j=j: v.tensor_copy(
                    wdst[:, dc, :, :],
                    wstripe[:, 0, :].rearrange("p (e m) -> p e m", m=128)),
                    label=("wcast", j))
        for ng in range(4):
            e.wait(("ppm", ng))
            e.op(lambda v, ng=ng: v.tensor_copy(prow[:, ng * 512:(ng + 1) * 512], pp[:, :]),
                 label=("ppev", ng))
        e.wait(("ptr",))
        e.op(lambda v: v.tensor_copy(prefix[:, :, :], ppt[:, :].rearrange("p (c s) -> p c s", s=2)),
             label=("pfx",))
        e.wait_dma_all(ld, T_PM)
        for i in range(NT):
            for ec in range(16):
                g = i * 16 + ec
                e.wait(("mq", g))
                e.op(lambda v, ec=ec, g=g: v.tensor_scalar(relu_sb[:, :], pq[:, g % 3, :], bqo[:, ec, 0:1], 0.0,
                                                           ALU.add, ALU.max), label=("relu", g))
                e.op(lambda v, ec=ec, g=g: v.tensor_scalar(mn_sb[:, :], pq[:, g % 3, :], bqo[:, ec, 0:1], 0.0,
                                                           ALU.add, ALU.min), label=("mn", g))
                e.wait(("exp", g))
                e.op(lambda v: v.tensor_tensor(phi_sb[:, :], relu_sb[:, :], exp_sb[:, :], ALU.add))
                e.wait_dma(ld, t_kv(i, ec))
                e.op(lambda v, ec=ec, g=g: v.tensor_scalar(kvp_sb[:, 0:H], kv_sb[:, g % 2, 0:H],
                                                           prefix[:, ec, 0:1], SCALE0, ALU.add, ALU.mult))
                e.op(lambda v, ec=ec, g=g: v.tensor_scalar(kvp_sb[:, H:T], kv_sb[:, g % 2, H:T],
                                                           prefix[:, ec, 1:2], SCALE0, ALU.add, ALU.mult),
                     label=("kvp", g))
                e.op(lambda v, ec=ec: v.tensor_tensor(ctx_sb[:, ec, :], phi_sb[:, :], kvp_sb[:, :], ALU.mult),
                     label=("ctx", i, ec))
            e.op(lambda v: v.engine_nop(), label=("ctxdone", i))

    @stg.engine("scalar")
    def _(e):
        e.wait_dma_all(ld, T_PM)
        for i in range(NT):
            for ec in range(16):
                g = i * 16 + ec
                e.wait(("mn", g))
                e.op(lambda s_: s_.activation(exp_sb[:, :], mn_sb[:, :], AF.Exp, bias=0.0, scale=1.0),
                     label=("exp", g))
            for oc in range(16):
                go = i * 16 + oc
                e.wait(("mo", i, oc))
                if go >= 2:
                    e.wait_dma(st, t_ratt((go - 2) // 16, (go - 2) % 16))
                e.op(lambda s_, oc=oc, go=go: s_.activation(ratt_sb[:, oc % 2, :], po[:, go % 3, :], AF.Identity,
                                                            bias=bqo[:, oc, 1:2], scale=1.0),
                     label=("rattev", go))

    @stg.engine("gpsimd")
    def _(e):
        for i in range(NT):
            for oc in range(16):
                go = i * 16 + oc
                e.wait(("rattev", go))
                e.dma(st, t_ratt(i, oc), s_ratt[oc, :, i * T:(i + 1) * T], ratt_sb[:, oc % 2, :])
        e.wait_dma_all(st, t_ratt(NT - 1, 15))

    stg.run()
    return ["s_ratt"]



def build_stageB2(nc, io, es, gs=None):
    s_xn = io.d("s_xn", [16, 128, NT * T], BF16, "scr_in")
    SU_ug = io.d("SU_ug", [D], F32, "in")
    H128a = io.d("H128a", [128, 128], BF16, "in")
    s_y = io.d("s_y", [16, 128, NT * T], BF16, "scr_out")

    E = es.enter_context
    h_sb = E(nc.sbuf_tensor("B2h", [128, 128], BF16))
    su_sb = E(nc.sbuf_tensor("B2su", [128, 16], F32))
    xn_sb = E(nc.sbuf_tensor("B2xn", [128, 2, 16, T], BF16))
    z_sb = E(nc.sbuf_tensor("B2z", [128, 2, T], BF16))
    y0 = E(nc.sbuf_tensor("B2y0", [128, 16, T], BF16))
    y1 = E(nc.sbuf_tensor("B2y1", [128, 16, T], BF16))
    py = E(nc.psum_tensor("B2py", [128, 2, T], F32))
    stg = Stage(nc, es, "B2", gs)
    gs = stg.gs
    ld = gs.dq("q_sync")
    st = gs.dq("q_gpsimd")

    T_H = ("h",)
    def t_su(c):
        return ("su", c)
    def t_xn(i):
        return ("xn", i)
    def t_y(i, fc):
        return ("y", i, fc)


    @stg.engine("sync")
    def _(e):
        e.dma(ld, T_H, h_sb[:, :], H128a)
        for c in range(16):
            e.dma(ld, t_su(c), su_sb[:, c:c + 1], SU_ug[c * 128:(c + 1) * 128].rearrange("(p one) -> p one", one=1))
        for i in range(NT):
            if i >= 2:
                e.wait(("z", (i - 2) * 16 + 15))
            e.dma(ld, t_xn(i), xn_sb[:, i % 2, :, :],
                  s_xn.transpose([1, 0, 2])[:, :, i * T:(i + 1) * T])

    @stg.engine("vector")
    def _(e):
        e.wait_dma_all(ld, t_su(15))
        for i in range(NT):
            e.wait_dma(ld, t_xn(i))
            for fc in range(16):
                g = i * 16 + fc
                if g >= 2:
                    e.wait(("my", g - 2))
                e.op(lambda v, fc=fc, i=i, g=g: v.tensor_scalar(z_sb[:, g % 2, :], xn_sb[:, i % 2, fc, :],
                                                                su_sb[:, fc:fc + 1], None, ALU.mult),
                     label=("z", g))
                e.wait(("my", g))
                if g >= 16 and fc == 0:
                    e.wait_dma_all(st, t_y(i - 1, 15))
                e.op(lambda v, fc=fc, g=g: v.tensor_copy(y0[:, fc, :], py[:, g % 2, :]),
                     label=("ev", g))
            for si, b in enumerate((1, 2, 4, 8)):
                src, dst = (y0, y1) if si % 2 == 0 else (y1, y0)
                for c in range(16):
                    if c & b:
                        e.op(lambda v, c=c, b=b, src=src, dst=dst: v.tensor_tensor(
                            dst[:, c, :], src[:, c ^ b, :], src[:, c, :], ALU.subtract))
                    else:
                        e.op(lambda v, c=c, b=b, src=src, dst=dst: v.tensor_tensor(
                            dst[:, c, :], src[:, c, :], src[:, c ^ b, :], ALU.add))
            e.op(lambda v: v.engine_nop(), label=("ydone", i))

    @stg.engine("tensor")
    def _(e):
        e.wait_dma(ld, T_H)
        for i in range(NT):
            for fc in range(16):
                g = i * 16 + fc
                e.wait(("z", g))
                if g >= 2:
                    e.wait(("ev", g - 2))
                e.unordered_op(lambda t, g=g: t.matmul(py[:, g % 2, :], h_sb[:, :], z_sb[:, g % 2, :],
                                                       start=True, stop=True),
                               label=("my", g))

    @stg.engine("gpsimd")
    def _(e):
        for i in range(NT):
            e.wait(("ydone", i))
            for fc in range(16):
                e.dma(st, t_y(i, fc), s_y[fc, :, i * T:(i + 1) * T], y0[:, fc, :])
        e.wait_dma_all(st, t_y(NT - 1, 15))

    stg.run()
    return ["s_y"]



def build_stageD(nc, io, es, gs=None):
    s_y = io.d("s_y", [16, 128, NT * T], BF16, "scr_in")
    WupT = io.d("WupT", [D, INTER], F32, "in")
    WgateT = io.d("WgateT", [D, INTER], F32, "in")
    Wscale_ug = io.d("Wscale_ug", [2 * INTER], F32, "in")
    SU_dp = io.d("SU_dp", [INTER], F32, "in")
    H128s = io.d("H128s", [128, 128], BF16, "in")
    We_blk = io.d("We_blk", [8, 16, 16, 128, 128], F32, "in")
    s_wt = io.d("s_wt", [64, 128, NT * T], BF16, "scr_out")
    s_web = io.d("s_web", [8, 16, 16, 128, 128], BF16, "scr_out")

    E = es.enter_context
    wup_sb = E(nc.sbuf_tensor("Dwup", [128, 16, 16, 128], BF16))
    wgt_sb = E(nc.sbuf_tensor("Dwgt", [128, 16, 16, 128], BF16))
    wstripe = E(nc.sbuf_tensor("Dstripe", [128, 2, D], F32))
    h_sb = E(nc.sbuf_tensor("Dh", [128, 128], BF16))
    wsu_sb = E(nc.sbuf_tensor("Dwsu", [128, 64], F32))
    wsg_sb = E(nc.sbuf_tensor("Dwsg", [128, 64], F32))
    sud_sb = E(nc.sbuf_tensor("Dsud", [128, 64], F32))
    y_sb = E(nc.sbuf_tensor("Dy", [128, 2, 16, T], BF16))
    g_sb = E(nc.sbuf_tensor("Dg", [128, 2, T], F32))
    w_sb = E(nc.sbuf_tensor("Dw", [128, 2, T], BF16))
    wt_sb = E(nc.sbuf_tensor("Dwt", [128, 2, T], BF16))
    wei_f = E(nc.sbuf_tensor("Dweif", [128, 1, 16, 128], F32))
    web_b = E(nc.sbuf_tensor("Dwebb", [128, 1, 16, 128], BF16))
    pup = E(nc.psum_tensor("Dpup", [128, 3, T], F32))
    pgt = E(nc.psum_tensor("Dpgt", [128, 3, T], F32))
    ph = E(nc.psum_tensor("Dph", [128, 2, T], F32))
    stg = Stage(nc, es, "D", gs)
    gs = stg.gs
    ld = gs.dq("q_sync")
    st = gs.dq("q_gpsimd")

    def t_wsu(c):
        return ("wsu", c)
    def t_wsg(c):
        return ("wsg", c)
    def t_sud(c):
        return ("sud", c)
    T_H = ("h",)
    def t_wcast(b):
        return ("wecast", b)
    def t_webst(b):
        return ("webst", b)
    def t_stripe(q, j):
        return ("stripe", q, j)
    def t_y(q, i):
        return ("y", q, i)
    def t_wt(q, i, ic):
        return ("wt", q, i, ic)


    @stg.engine("sync")
    def _(e):
        for c in range(64):
            e.dma(ld, t_wsu(c), wsu_sb[:, c:c + 1], Wscale_ug[c * 128:(c + 1) * 128].rearrange("(p one) -> p one", one=1))
        for c in range(64):
            e.dma(ld, t_wsg(c), wsg_sb[:, c:c + 1], Wscale_ug[INTER + c * 128:INTER + (c + 1) * 128].rearrange("(p one) -> p one", one=1))
        for c in range(64):
            e.dma(ld, t_sud(c), sud_sb[:, c:c + 1], SU_dp[c * 128:(c + 1) * 128].rearrange("(p one) -> p one", one=1))
        e.dma(ld, T_H, h_sb[:, :], H128s)
        for q in range(4):
            for wi, wsrc in enumerate([WupT, WgateT]):
                for dc in range(16):
                    j = wi * 16 + dc
                    if q * 32 + j >= 2:
                        e.wait(("wcast", q * 32 + j - 2))
                    e.dma(ld, t_stripe(q, j), wstripe[:, j % 2, :],
                          wsrc[dc * 128:(dc + 1) * 128, q * D:(q + 1) * D])
            for i in range(NT):
                gt = q * NT + i
                if gt >= 2:
                    e.wait(("mgi", (gt - 2) * 16 + 15))
                e.dma(ld, t_y(q, i), y_sb[:, gt % 2, :, :],
                      s_y.transpose([1, 0, 2])[:, :, i * T:(i + 1) * T])
                for bb in range(4):
                    b = gt * 4 + bb
                    e8b, ocb = b // 16, b % 16
                    if b >= 1:
                        e.wait(("wecast", b - 1))
                    e.dma(ld, t_wcast(b), wei_f[:, 0, :, :], We_blk[e8b, ocb].transpose([1, 0, 2]))

    @stg.engine("vector")
    def _(e):
        e.wait_dma_all(ld, t_sud(63))
        e.op(lambda v: v.tensor_scalar(wsg_sb[:, :], wsg_sb[:, :], 64.0, None, ALU.mult))
        e.op(lambda v: v.tensor_scalar(wsu_sb[:, :], wsu_sb[:, :], 64.0, None, ALU.mult))
        e.op(lambda v: v.tensor_tensor(wsu_sb[:, :], wsu_sb[:, :], sud_sb[:, :], ALU.mult),
             label=("scales",))
        for q in range(4):
            for wi, wdst in enumerate([wup_sb, wgt_sb]):
                for dc in range(16):
                    j = wi * 16 + dc
                    e.wait_dma(ld, t_stripe(q, j))
                    e.op(lambda v, wdst=wdst, dc=dc, j=j: v.tensor_copy(
                        wdst[:, dc, :, :],
                        wstripe[:, j % 2, :].rearrange("p (i m) -> p i m", m=128)),
                        label=("wcast", q * 32 + j))
            for i in range(NT):
                for ic in range(16):
                    gq = (q * NT + i) * 16 + ic
                    ig = q * 16 + ic
                    e.wait(("mu", gq))
                    e.wait(("g", gq))
                    if gq >= 2:
                        e.wait(("mh", gq - 2))
                    e.op(lambda v, gq=gq, ig=ig: v.scalar_tensor_tensor(
                        w_sb[:, gq % 2, :], pup[:, gq % 3, :], wsu_sb[:, ig:ig + 1],
                        g_sb[:, gq % 2, :], ALU.mult, ALU.mult), label=("w", gq))
                    e.wait(("mh", gq))
                    if gq >= 2:
                        pg = gq - 2
                        e.wait_dma(st, t_wt(pg // (NT * 16), (pg // 16) % NT, pg % 16))
                    e.op(lambda v, gq=gq: v.tensor_copy(wt_sb[:, gq % 2, :], ph[:, gq % 2, :]),
                         label=("wtev", gq))
                    if ic in (3, 7, 11, 15):
                        gt_ = q * NT + i
                        b = gt_ * 4 + (ic // 4)
                        e.wait_dma(ld, t_wcast(b))
                        if b >= 1:
                            e.wait_dma(st, t_webst(b - 1))
                        e.op(lambda v, b=b: v.tensor_copy(web_b[:, 0, :, :], wei_f[:, 0, :, :]),
                             label=("wecast", b))

    @stg.engine("scalar")
    def _(e):
        e.wait(("scales",))
        for q in range(4):
            for i in range(NT):
                for ic in range(16):
                    gq = (q * NT + i) * 16 + ic
                    ig = q * 16 + ic
                    e.wait(("mgi", gq))
                    if gq >= 2:
                        e.wait(("w", gq - 2))
                    e.op(lambda s_, gq=gq, ig=ig: s_.activation(g_sb[:, gq % 2, :], pgt[:, gq % 3, :],
                                                                AF.Silu, bias=0.0, scale=wsg_sb[:, ig:ig + 1]),
                         label=("g", gq))

    @stg.engine("tensor")
    def _(e):
        e.wait_dma(ld, T_H)
        for q in range(4):
            e.wait(("wcast", q * 32 + 31))
            for i in range(NT):
                gt = q * NT + i
                e.wait_dma(ld, t_y(q, i))
                for ic in range(16):
                    gq = gt * 16 + ic
                    if gq >= 3:
                        e.wait(("w", gq - 3))
                    for dc in range(16):
                        e.unordered_op(lambda t, dc=dc, ic=ic, gt=gt, gq=gq: t.matmul(
                            pup[:, gq % 3, :], wup_sb[:, dc, ic, :], y_sb[:, gt % 2, dc, :],
                            start=(dc == 0), stop=(dc == 15)),
                            label=("mu", gq) if dc == 15 else None)
                    if gq >= 3:
                        e.wait(("g", gq - 3))
                    for dc in range(16):
                        lab = None
                        if dc == 15:
                            lab = ("mgi", gq)
                        e.unordered_op(lambda t, dc=dc, ic=ic, gt=gt, gq=gq: t.matmul(
                            pgt[:, gq % 3, :], wgt_sb[:, dc, ic, :], y_sb[:, gt % 2, dc, :],
                            start=(dc == 0), stop=(dc == 15)), label=lab)
                    e.wait(("w", gq))
                    if gq >= 2:
                        e.wait(("wtev", gq - 2))
                    e.unordered_op(lambda t, gq=gq: t.matmul(ph[:, gq % 2, :], h_sb[:, :], w_sb[:, gq % 2, :],
                                                             start=True, stop=True),
                                   label=("mh", gq))

    @stg.engine("gpsimd")
    def _(e):
        for q in range(4):
            for i in range(NT):
                for ic in range(16):
                    gq = (q * NT + i) * 16 + ic
                    ig = q * 16 + ic
                    e.wait(("wtev", gq))
                    e.dma(st, t_wt(q, i, ic), s_wt[ig, :, i * T:(i + 1) * T], wt_sb[:, gq % 2, :])
                    if ic in (3, 7, 11, 15):
                        b = (q * NT + i) * 4 + (ic // 4)
                        e8b, ocb = b // 16, b % 16
                        e.wait(("wecast", b))
                        e.dma(st, t_webst(b), s_web[e8b, ocb].transpose([1, 0, 2]), web_b[:, 0, :, :])
        e.wait_dma_all(st, t_wt(3, NT - 1, 15))
        e.wait_dma_all(st, t_webst(127))

    stg.run()
    return ["s_wt"]



def build_stageE(nc, io, es, gs=None):
    s_wt = io.d("s_wt", [64, 128, NT * T], BF16, "scr_in")
    Hdd = io.d("Hdd", [128, 128], BF16, "in")
    s_yp = io.d("s_yp", [64, 128, NT * T], BF16, "scr_out")

    E = es.enter_context
    h_sb = E(nc.sbuf_tensor("Eh", [128, 128], BF16))
    wt_in = E(nc.sbuf_tensor("Ewt", [128, 2, 64, T], BF16))
    stage4 = E(nc.sbuf_tensor("Est", [128, 8, T], BF16))
    ph = E(nc.psum_tensor("Eph", [128, 2, T], F32))
    stg = Stage(nc, es, "E", gs)
    gs = stg.gs
    ld = gs.dq("q_sync")
    st = gs.dq("q_gpsimd")

    T_H = ("h",)
    def t_in(i, a):
        return ("in", i, a)
    def t_out(i, grp, a):
        return ("out", i, grp, a)


    @stg.engine("sync")
    def _(e):
        e.dma(ld, T_H, h_sb[:, :], Hdd)
        for i in range(NT):
            if i >= 2:
                e.wait(("em", (i - 2) * 64 + 63))
            for a in range(2):
                e.dma(ld, t_in(i, a), wt_in[a * 64:(a + 1) * 64, i % 2, :, :],
                      s_wt[:, a::2, i * T:(i + 1) * T])

    @stg.engine("tensor")
    def _(e):
        e.wait_dma(ld, T_H)
        for i in range(NT):
            e.wait_dma(ld, t_in(i, 0))
            e.wait_dma(ld, t_in(i, 1))
            for p2 in range(64):
                g = i * 64 + p2
                if g >= 2:
                    e.wait(("eev", g - 2))
                e.unordered_op(lambda t, i=i, p2=p2, g=g: t.matmul(ph[:, g % 2, :], h_sb[:, :],
                                                                  wt_in[:, i % 2, p2, :],
                                                                  start=True, stop=True),
                               label=("em", g))

    @stg.engine("vector")
    def _(e):
        for i in range(NT):
            for p2 in range(64):
                g = i * 64 + p2
                e.wait(("em", g))
                grp = p2 // 4
                if grp >= 2 and p2 % 4 == 0:
                    for a in range(2):
                        e.wait_dma(st, t_out(i, grp - 2, a))
                elif i >= 1 and grp < 2 and p2 % 4 == 0:
                    for a in range(2):
                        e.wait_dma(st, t_out(i - 1, 14 + grp, a))
                e.op(lambda v, g=g, p2=p2: v.tensor_copy(stage4[:, p2 % 8, :], ph[:, g % 2, :]),
                     label=("eev", g))

    @stg.engine("gpsimd")
    def _(e):
        for i in range(NT):
            for grp in range(16):
                e.wait(("eev", i * 64 + grp * 4 + 3))
                for a in range(2):
                    s0 = (grp % 2) * 4
                    e.dma(st, t_out(i, grp, a),
                          s_yp[:, 8 * grp + a::2, i * T:(i + 1) * T][:, 0:4, :],
                          stage4[a * 64:(a + 1) * 64, s0:s0 + 4, :])
        e.wait_dma_all(st, t_out(NT - 1, 15, 1))

    stg.run()
    return ["s_yp"]



def build_stageF(nc, io, es, gs=None):
    s_yp = io.d("s_yp", [64, 128, NT * T], BF16, "scr_in")
    s_ratt = io.d("s_ratt", [16, 128, NT * T], F32, "scr_in")
    WdownT = io.d("WdownT", [INTER, D], F32, "in")
    Wscale_dp = io.d("Wscale_dp", [D], F32, "in")
    s_dh = io.d("s_dh", [16, 128, NT * T], F32, "scr_out")

    E = es.enter_context
    wd_sb = E(nc.sbuf_tensor("Fwd", [128, 64, 8, 128], BF16))
    wstripe = E(nc.sbuf_tensor("Fstripe", [128, 1, 1024], F32))
    wsd_sb = E(nc.sbuf_tensor("Fwsd", [128, 16], F32))
    yp_sb = E(nc.sbuf_tensor("Fyp", [128, 64, T], BF16))
    ratt_sb = E(nc.sbuf_tensor("Fratt", [128, 2, T], F32))
    dd_sb = E(nc.sbuf_tensor("Fdd", [128, 2, T], F32))
    pd = E(nc.psum_tensor("Fpd", [128, 4, T], F32))
    stg = Stage(nc, es, "F", gs)
    gs = stg.gs
    ld = gs.dq("q_sync")
    st = gs.dq("q_gpsimd")

    def t_wsd(c):
        return ("wsd", c)
    def t_stripe(oh, j):
        return ("stripe", oh, j)
    def t_yp(oh, i):
        return ("yp", oh, i)
    def t_ratt(oh, i, oc):
        return ("ratt", oh, i, oc)
    def t_dh(oh, i, oc):
        return ("dh", oh, i, oc)


    @stg.engine("sync")
    def _(e):
        for c in range(16):
            e.dma(ld, t_wsd(c), wsd_sb[:, c:c + 1], Wscale_dp[c * 128:(c + 1) * 128].rearrange("(p one) -> p one", one=1))
        for oh in range(2):
            for j in range(64):
                if oh * 64 + j >= 1:
                    e.wait(("wcast", oh * 64 + j - 1))
                e.dma(ld, t_stripe(oh, j), wstripe[:, 0, :],
                      WdownT[j * 128:(j + 1) * 128, oh * 1024:(oh + 1) * 1024])
            for i in range(NT):
                gt = oh * NT + i
                if gt >= 1:
                    e.wait(("md", gt - 1, 7))
                e.dma(ld, t_yp(oh, i), yp_sb[:, :, :],
                      s_yp.transpose([1, 0, 2])[:, :, i * T:(i + 1) * T])
                for oc in range(8):
                    g8 = gt * 8 + oc
                    if g8 >= 2:
                        e.wait(("dd", g8 - 2))
                    e.dma(ld, t_ratt(oh, i, oc), ratt_sb[:, g8 % 2, :],
                          s_ratt[oh * 8 + oc, :, i * T:(i + 1) * T])

    @stg.engine("vector")
    def _(e):
        e.wait_dma_all(ld, t_wsd(15))
        e.op(lambda v: v.tensor_scalar(wsd_sb[:, :], wsd_sb[:, :], 64.0, None, ALU.mult),
             label=("scales",))
        for oh in range(2):
            for j in range(64):
                e.wait_dma(ld, t_stripe(oh, j))
                e.op(lambda v, j=j: v.tensor_copy(
                    wd_sb[:, j, :, :],
                    wstripe[:, 0, :].rearrange("p (o m) -> p o m", m=128)),
                    label=("wcast", oh * 64 + j))
            for i in range(NT):
                for oc in range(8):
                    gt = oh * NT + i
                    g8 = gt * 8 + oc
                    ocg = oh * 8 + oc
                    e.wait(("md", gt, oc))
                    e.wait_dma(ld, t_ratt(oh, i, oc))
                    if g8 >= 2:
                        pg = g8 - 2
                        e.wait_dma(st, t_dh(pg // (NT * 8), (pg // 8) % NT, pg % 8))
                    e.op(lambda v, g8=g8, ocg=ocg: v.scalar_tensor_tensor(
                        dd_sb[:, g8 % 2, :], pd[:, g8 % 4, :], wsd_sb[:, ocg:ocg + 1],
                        ratt_sb[:, g8 % 2, :], ALU.mult, ALU.add), label=("dd", g8))

    @stg.engine("tensor")
    def _(e):
        for oh in range(2):
            e.wait(("wcast", oh * 64 + 63))
            e.wait(("scales",))
            for i in range(NT):
                gt = oh * NT + i
                e.wait_dma(ld, t_yp(oh, i))
                for oc in range(8):
                    g8 = gt * 8 + oc
                    if g8 >= 4:
                        e.wait(("dd", g8 - 4))
                    for fc in range(64):
                        e.unordered_op(lambda t, fc=fc, oc=oc, g8=g8: t.matmul(
                            pd[:, g8 % 4, :], wd_sb[:, fc, oc, :], yp_sb[:, fc, :],
                            start=(fc == 0), stop=(fc == 63)),
                            label=("md", gt, oc) if fc == 63 else None)

    @stg.engine("gpsimd")
    def _(e):
        for oh in range(2):
            for i in range(NT):
                for oc in range(8):
                    g8 = (oh * NT + i) * 8 + oc
                    e.wait(("dd", g8))
                    e.dma(st, t_dh(oh, i, oc), s_dh[oh * 8 + oc, :, i * T:(i + 1) * T], dd_sb[:, g8 % 2, :])
        e.wait_dma_all(st, t_dh(1, NT - 1, 7))

    stg.run()
    return ["s_dh"]



def build_stageG(nc, io, es, gs=None):
    x = io.d("x", [2, PPC, D], F32, "in")
    s_dh = io.d("s_dh", [16, 128, NT * T], F32, "scr_in")
    wg = io.d("wg", [D, 8], F32, "in")
    identf = io.d("identf", [128, 128], F32, "in")
    s_x1 = io.d("s_x1", [NT * 4, 128, D], F32, "scr_out")
    s_xn1 = io.d("s_xn1", [16, 128, NT * T], BF16, "scr_out")
    s_lg = io.d("s_lg", [8, NT * T], F32, "scr_out")

    E = es.enter_context
    idf = E(nc.sbuf_tensor("Gidf", [128, 128], F32))
    wg_sb = E(nc.sbuf_tensor("Gwg", [128, 16, 8], F32))
    dh_sb = E(nc.sbuf_tensor("Gdh", [128, 16, T], F32))
    x_sb = E(nc.sbuf_tensor("Gx", [128, 4, D], F32))
    x1_sb = E(nc.sbuf_tensor("Gx1", [128, 4, D], F32))
    xn1tok = E(nc.sbuf_tensor("Gxn1t", [128, 2, D], F32))
    xn1f = E(nc.sbuf_tensor("Gxn1f", [128, 16, T], BF16))
    xf32 = E(nc.sbuf_tensor("Gxf32", [128, 2, 4, 128], F32))
    junk = E(nc.sbuf_tensor("Gjunk", [128, D], F32))
    stat = E(nc.sbuf_tensor("Gstat", [128, 4, 8], F32))
    lg_sb = E(nc.sbuf_tensor("Glg", [8, T], F32))
    epsc = E(nc.sbuf_tensor("Geps", [128, 1], F32))
    pt = E(nc.psum_tensor("Gpt", [128, 2, T], F32))
    pxt = E(nc.psum_tensor("Gpxt", [128, 2, T], F32))
    pl = E(nc.psum_tensor("Gpl", [8, T], F32))
    stg = Stage(nc, es, "G", gs)
    gs = stg.gs
    ld = gs.dq("q_sync")
    st = gs.dq("q_gpsimd")

    T_IDF = ("idf",)
    def t_wg(dc):
        return ("wg", dc)
    def t_dh(i):
        return ("dh", i)
    def t_x(i, tc):
        return ("x", i, tc)
    def t_x1(i, tc):
        return ("x1", i, tc)
    def t_xn1(i, fc):
        return ("xn1", i, fc)
    def t_lg(i):
        return ("lg", i)


    @stg.engine("sync")
    def _(e):
        e.dma(ld, T_IDF, idf[:, :], identf)
        for dc in range(16):
            e.dma(ld, t_wg(dc), wg_sb[:, dc, :], wg[dc * 128:(dc + 1) * 128, :])
        for i in range(NT):
            if i >= 1:
                e.wait(("gtr", i - 1, 15))
            e.dma(ld, t_dh(i), dh_sb[:, :, :],
                  s_dh.transpose([1, 0, 2])[:, :, i * T:(i + 1) * T])
            if i >= 1:
                e.wait(("x1", i - 1, 15))
            for tc in range(4):
                sq = tc // 2
                p0 = i * 256 + (tc % 2) * 128
                e.dma(ld, t_x(i, tc), x_sb[:, tc, :], x[sq, p0:p0 + 128, :])

    @stg.engine("tensor")
    def _(e):
        e.wait_dma(ld, T_IDF)
        for i in range(NT):
            e.wait_dma(ld, t_dh(i))
            for fc in range(16):
                if i * 16 + fc >= 2:
                    pfc = i * 16 + fc - 2
                    e.wait(("x1", pfc // 16, pfc % 16))
                for tc in range(4):
                    e.unordered_op(lambda t, fc=fc, tc=tc: t.transpose(
                        pt[:, fc % 2, tc * 128:(tc + 1) * 128],
                        dh_sb[:, fc, tc * 128:(tc + 1) * 128], idf[:, :]))
                e.unordered_drain_op(lambda t: t.nop(), label=("gtr", i, fc))
            for tc in range(4):
                e.wait(("apply", i, tc))
                for fcg in range(4):
                    gx = (i * 4 + tc) * 4 + fcg
                    if gx >= 2:
                        e.wait(("xev", gx - 2))
                    for j in range(4):
                        fc = fcg * 4 + j
                        e.unordered_op(lambda t, tc=tc, fc=fc, j=j, fcg=fcg, i=i: t.transpose(
                            pxt[:, fcg % 2, j * 128:(j + 1) * 128],
                            xn1tok[:, (i * 4 + tc) % 2, fc * 128:(fc + 1) * 128], idf[:, :]))
                    e.unordered_drain_op(lambda t: t.nop(), label=("xtr", gx))
                    e.wait(("xev", gx))
                    if gx == 0 and i >= 1:
                        e.wait(("lgev", i - 1))
                    for j in range(4):
                        fc = fcg * 4 + j
                        e.unordered_op(lambda t, tc=tc, fc=fc, j=j, gx=gx: t.matmul(
                            pl[:, tc * 128:(tc + 1) * 128], wg_sb[:, fc, :],
                            xf32[:, gx % 2, j, :],
                            start=(tc == 0 and fc == 0), stop=(tc == 3 and fc == 15)),
                            label=("mlg", i) if (tc == 3 and fc == 15) else None)

    @stg.engine("vector")
    def _(e):
        e.op(lambda v: v.memset(epsc[:, :], EPS))
        for i in range(NT):
            for fc in range(16):
                e.wait(("gtr", i, fc))
                e.wait_dma_all(ld, t_x(i, 3))
                if i >= 1 and fc == 0:
                    e.wait_dma_all(st, t_x1(i - 1, 3))
                e.op(lambda v, fc=fc: v.tensor_tensor(
                    x1_sb[:, :, fc * 128:(fc + 1) * 128],
                    pt[:, fc % 2, :].rearrange("p (tc m) -> p tc m", m=128),
                    x_sb[:, :, fc * 128:(fc + 1) * 128], ALU.add), label=("x1", i, fc))
            for tc in range(4):
                tg = i * 4 + tc
                e.op(lambda v, tc=tc: v.reduce_sum(stat[:, tc, 0:1], x1_sb[:, tc, :], AX.X))
                e.wait(("sq", i, tc))
                e.op(lambda v, tc=tc: v.tensor_scalar(stat[:, tc, 2:3], stat[:, tc, 0:1], 1.0 / D, None, ALU.mult))
                e.op(lambda v, tc=tc: v.tensor_tensor(stat[:, tc, 7:8], stat[:, tc, 2:3], stat[:, tc, 2:3], ALU.mult))
                e.op(lambda v, tc=tc: v.tensor_scalar(stat[:, tc, 3:4], stat[:, tc, 1:2], 1.0 / D, None, ALU.mult))
                e.op(lambda v, tc=tc: v.tensor_tensor(stat[:, tc, 3:4], stat[:, tc, 3:4], stat[:, tc, 7:8], ALU.subtract),
                     label=("var", i, tc))
                e.wait(("sqrt", i, tc))
                e.op(lambda v, tc=tc: v.reciprocal(stat[:, tc, 5:6], stat[:, tc, 4:5]))
                e.op(lambda v, tc=tc: v.scalar_tensor_tensor(stat[:, tc, 6:7], stat[:, tc, 2:3], -1.0,
                                                             stat[:, tc, 5:6], ALU.mult, ALU.mult),
                     label=("rstd", i, tc))
                for fcg in range(4):
                    gx = (i * 4 + tc) * 4 + fcg
                    e.wait(("xtr", gx))
                    e.op(lambda v, gx=gx: v.tensor_copy(xf32[:, gx % 2, :, :],
                                                        pxt[:, gx % 2, :].rearrange("p (j m) -> p j m", m=128)))
                    if i >= 1 and tc == 0 and fcg == 0:
                        e.wait_dma_all(st, t_xn1(i - 1, 15))
                    e.op(lambda v, gx=gx, tc=tc, fcg=fcg: v.tensor_copy(
                        xn1f[:, fcg * 4:(fcg + 1) * 4, tc * 128:(tc + 1) * 128],
                        pxt[:, gx % 2, :].rearrange("p (j m) -> p j m", m=128)),
                        label=("xev", gx))
            e.wait(("mlg", i))
            if i >= 1:
                e.wait_dma(st, t_lg(i - 1))
            e.op(lambda v: v.tensor_copy(lg_sb[:, :], pl[:, :]), label=("lgev", i))

    @stg.engine("scalar")
    def _(e):
        for i in range(NT):
            for tc in range(4):
                tg = i * 4 + tc
                e.wait(("x1", i, 15))
                if tg >= 2:
                    e.wait(("xtr", (tg - 2) * 4 + 3))
                e.op(lambda s_, tc=tc: s_.activation(junk[:, :], x1_sb[:, tc, :], AF.Square,
                                                     bias=0.0, scale=1.0, accum_out=stat[:, tc, 1:2]),
                     label=("sq", i, tc))
                e.wait(("var", i, tc))
                e.op(lambda s_, tc=tc: s_.activation(stat[:, tc, 4:5], stat[:, tc, 3:4], AF.Sqrt,
                                                     bias=epsc[:, 0:1], scale=1.0), label=("sqrt", i, tc))
                e.wait(("rstd", i, tc))
                e.op(lambda s_, tc=tc, tg=tg: s_.activation(xn1tok[:, tg % 2, :], x1_sb[:, tc, :], AF.Identity,
                                                            bias=stat[:, tc, 6:7], scale=stat[:, tc, 5:6]),
                     label=("apply", i, tc))

    @stg.engine("gpsimd")
    def _(e):
        for i in range(NT):
            for tc in range(4):
                e.wait(("apply", i, tc))
                e.dma(st, t_x1(i, tc), s_x1[i * 4 + tc, :, :], x1_sb[:, tc, :])
            for fc in range(16):
                e.wait(("xev", (i * 4 + 3) * 4 + 3))
                e.dma(st, t_xn1(i, fc), s_xn1[fc, :, i * T:(i + 1) * T], xn1f[:, fc, :])
            e.wait(("lgev", i))
            e.dma(st, t_lg(i), s_lg[:, i * T:(i + 1) * T], lg_sb[:, :])
        e.wait_dma_all(st, t_lg(NT - 1))

    stg.run()
    return ["s_x1", "s_xn1", "s_lg"]



def build_stageH1(nc, io, es, gs=None):
    s_xn1 = io.d("s_xn1", [16, 128, NT * T], BF16, "scr_in")
    WqkvT = io.d("WqkvT", [D, 3072], F32, "in")
    selS = io.d("selS", [128, 16, 16], BF16, "in")
    selB = io.d("selB", [16, 16, 128], BF16, "in")
    s_attn = io.d("s_attn", [16, 128, NT * T], BF16, "scr_out")

    E = es.enter_context
    wq_sb = E(nc.sbuf_tensor("H1w", [128, 16, 24, 128], BF16))
    wstripe = E(nc.sbuf_tensor("H1stripe", [128, 2, 3072], F32))
    selS_sb = E(nc.sbuf_tensor("H1selS", [128, 16, 16], BF16))
    selB_sb = E(nc.sbuf_tensor("H1selB", [16, 16, 128], BF16))
    xn1_sb = E(nc.sbuf_tensor("H1xn", [128, 16, T], BF16))
    q_sb = E(nc.sbuf_tensor("H1q", [128, 16, T], BF16))
    k_sb = E(nc.sbuf_tensor("H1k", [128, 4, T], BF16))
    v_sb = E(nc.sbuf_tensor("H1v", [128, 4, T], BF16))
    pr_sb = E(nc.sbuf_tensor("H1pr", [128, 2, T], BF16))
    s_sb = E(nc.sbuf_tensor("H1s", [16, T], F32))
    sm_sb = E(nc.sbuf_tensor("H1sm", [16, 6, H], F32))
    wts_sb = E(nc.sbuf_tensor("H1wts", [16, T], BF16))
    wb_sb = E(nc.sbuf_tensor("H1wb", [128, 2, T], BF16))
    t0_sb = E(nc.sbuf_tensor("H1t0", [128, 2, T], BF16))
    attn_sb = E(nc.sbuf_tensor("H1attn", [128, 16, T], BF16))
    pq = E(nc.psum_tensor("H1pq", [128, 2, T], F32))
    ps = E(nc.psum_tensor("H1ps", [16, T], F32))
    pw = E(nc.psum_tensor("H1pw", [128, 2, T], F32))
    stg = Stage(nc, es, "H1", gs)
    gs = stg.gs
    ld = gs.dq("q_sync")
    st = gs.dq("q_gpsimd")

    def t_stripe(j):
        return ("stripe", j)
    T_SELS = ("sels",)
    T_SELB = ("selb",)
    def t_xn(i):
        return ("xn", i)
    def t_attn(i, fc):
        return ("attn", i, fc)


    @stg.engine("sync")
    def _(e):
        for dc in range(16):
            if dc >= 2:
                e.wait(("wcast", dc - 2))
            e.dma(ld, t_stripe(dc), wstripe[:, dc % 2, :], WqkvT[dc * 128:(dc + 1) * 128, :])
        e.dma(ld, T_SELS, selS_sb[:, :, :], selS)
        e.dma(ld, T_SELB, selB_sb[:, :, :], selB)
        for i in range(NT):
            if i >= 1:
                e.wait(("mq", i - 1, 23))
            e.dma(ld, t_xn(i), xn1_sb[:, :, :],
                  s_xn1.transpose([1, 0, 2])[:, :, i * T:(i + 1) * T])

    @stg.engine("tensor")
    def _(e):
        e.wait_dma(ld, T_SELS)
        e.wait_dma(ld, T_SELB)
        e.wait(("wcast", 15))
        for i in range(NT):
            e.wait_dma(ld, t_xn(i))
            for j in range(24):
                gj = i * 24 + j
                if gj >= 2:
                    e.wait(("qev", gj - 2))
                for dc in range(16):
                    e.unordered_op(lambda t, dc=dc, j=j: t.matmul(pq[:, j % 2, :], wq_sb[:, dc, j, :],
                                                                  xn1_sb[:, dc, :], start=(dc == 0), stop=(dc == 15)),
                                   label=("mq", i, j) if dc == 15 else None)
            if i >= 1:
                e.wait(("sev", i - 1))
            for h in range(16):
                e.wait(("pr", i, h))
                e.unordered_op(lambda t, h=h: t.matmul(ps[:, 0:H], selS_sb[:, h, :], pr_sb[:, h % 2, 0:H],
                                                       start=(h == 0), stop=False))
                e.unordered_op(lambda t, h=h: t.matmul(ps[:, H:T], selS_sb[:, h, :], pr_sb[:, h % 2, H:T],
                                                       start=False, stop=(h == 15)),
                               label=("msh", i, h))
            for h in range(16):
                e.wait(("wts", i))
                if i * 16 + h >= 2:
                    ph_ = i * 16 + h - 2
                    e.wait(("ao", ph_ // 16, ph_ % 16))
                e.unordered_op(lambda t, h=h: t.matmul(pw[:, h % 2, :], selB_sb[:, h, :], wts_sb[:, :],
                                                       start=True, stop=True),
                               label=("mw", i, h))

    @stg.engine("vector")
    def _(e):
        for dc in range(16):
            e.wait_dma(ld, t_stripe(dc))
            e.op(lambda v, dc=dc: v.tensor_copy(wq_sb[:, dc, :, :],
                                                wstripe[:, dc % 2, :].rearrange("p (j m) -> p j m", m=128)),
                 label=("wcast", dc))
        for i in range(NT):
            for j in range(24):
                gj = i * 24 + j
                e.wait(("mq", i, j))
                dst = (lambda j: q_sb[:, j, :] if j < 16 else (k_sb[:, j - 16, :] if j < 20 else v_sb[:, j - 20, :]))
                e.op(lambda v, j=j: v.tensor_copy(dst(j), pq[:, j % 2, :]), label=("qev", gj))
            for h in range(16):
                gh = i * 16 + h
                if gh >= 2:
                    e.wait(("msh", (gh - 2) // 16, (gh - 2) % 16))
                e.op(lambda v, h=h: v.tensor_tensor(pr_sb[:, h % 2, 0:H], q_sb[:, h, H:T],
                                                    k_sb[:, h // 4, 0:H], ALU.mult))
                e.op(lambda v, h=h: v.tensor_tensor(pr_sb[:, h % 2, H:T], q_sb[:, h, H:T],
                                                    k_sb[:, h // 4, H:T], ALU.mult), label=("pr", i, h))
            e.wait(("msh", i, 15))
            e.op(lambda v: v.tensor_copy(s_sb[:, :], ps[:, :]), label=("sev", i))
            e.op(lambda v: v.tensor_tensor(sm_sb[:, 0, :], s_sb[:, 0:H], s_sb[:, H:T], ALU.max))
            e.op(lambda v: v.tensor_tensor(sm_sb[:, 1, :], s_sb[:, 0:H], sm_sb[:, 0, :], ALU.subtract),
                 label=("d0", i))
            e.op(lambda v: v.tensor_tensor(sm_sb[:, 2, :], s_sb[:, H:T], sm_sb[:, 0, :], ALU.subtract),
                 label=("d1", i))
            e.wait(("e1", i))
            e.op(lambda v: v.tensor_tensor(sm_sb[:, 0, :], sm_sb[:, 4, :], sm_sb[:, 5, :], ALU.add))
            e.op(lambda v: v.reciprocal(sm_sb[:, 3, :], sm_sb[:, 0, :]))
            e.op(lambda v: v.tensor_tensor(wts_sb[:, 0:H], sm_sb[:, 4, :], sm_sb[:, 3, :], ALU.mult))
            e.op(lambda v: v.tensor_tensor(wts_sb[:, H:T], sm_sb[:, 5, :], sm_sb[:, 3, :], ALU.mult),
                 label=("wts", i))
            for h in range(16):
                gh = i * 16 + h
                e.wait(("mw", i, h))
                if i >= 1 and h == 0:
                    e.wait_dma_all(st, t_attn(i - 1, 15))
                e.op(lambda v, h=h: v.tensor_copy(wb_sb[:, h % 2, :], pw[:, h % 2, :]))
                e.op(lambda v, h=h: v.tensor_tensor(t0_sb[:, h % 2, 0:H], wb_sb[:, h % 2, 0:H],
                                                    v_sb[:, h // 4, 0:H], ALU.mult))
                e.op(lambda v, h=h: v.tensor_tensor(t0_sb[:, h % 2, H:T], wb_sb[:, h % 2, H:T],
                                                    v_sb[:, h // 4, H:T], ALU.mult))
                e.op(lambda v, h=h: v.tensor_tensor(attn_sb[:, h, H:T], t0_sb[:, h % 2, 0:H],
                                                    t0_sb[:, h % 2, H:T], ALU.add))
                e.op(lambda v, h=h: v.tensor_copy(attn_sb[:, h, 0:H], v_sb[:, h // 4, 0:H]),
                     label=("ao", i, h))

    @stg.engine("scalar")
    def _(e):
        for i in range(NT):
            e.wait(("d0", i))
            e.op(lambda s_: s_.activation(sm_sb[:, 4, :], sm_sb[:, 1, :], AF.Exp, bias=0.0, scale=1.0))
            e.wait(("d1", i))
            e.op(lambda s_: s_.activation(sm_sb[:, 5, :], sm_sb[:, 2, :], AF.Exp, bias=0.0, scale=1.0),
                 label=("e1", i))

    @stg.engine("gpsimd")
    def _(e):
        for i in range(NT):
            for fc in range(16):
                e.wait(("ao", i, fc))
                e.dma(st, t_attn(i, fc), s_attn[fc, :, i * T:(i + 1) * T], attn_sb[:, fc, :])
        e.wait_dma_all(st, t_attn(NT - 1, 15))

    stg.run()
    return ["s_attn"]



def build_stageH2(nc, io, es, gs=None):
    s_attn = io.d("s_attn", [16, 128, NT * T], BF16, "scr_in")
    WprojT = io.d("WprojT", [D, D], F32, "in")
    s_r1 = io.d("s_r1", [16, 128, NT * T], F32, "scr_out")

    E = es.enter_context
    wp_sb = E(nc.sbuf_tensor("H2w", [128, 16, 16, 128], BF16))
    wstripe = E(nc.sbuf_tensor("H2stripe", [128, 2, D], F32))
    at_sb = E(nc.sbuf_tensor("H2at", [128, 16, T], BF16))
    r1_sb = E(nc.sbuf_tensor("H2r1", [128, 4, T], F32))
    po = E(nc.psum_tensor("H2po", [128, 4, T], F32))
    stg = Stage(nc, es, "H2", gs)
    gs = stg.gs
    ld = gs.dq("q_sync")
    st = gs.dq("q_gpsimd")

    def t_stripe(j):
        return ("stripe", j)
    def t_at(i):
        return ("at", i)
    def t_r1(i, oc):
        return ("r1", i, oc)


    @stg.engine("sync")
    def _(e):
        for dc in range(16):
            if dc >= 2:
                e.wait(("wcast", dc - 2))
            e.dma(ld, t_stripe(dc), wstripe[:, dc % 2, :], WprojT[dc * 128:(dc + 1) * 128, :])
        for i in range(NT):
            if i >= 1:
                e.wait(("mo", i - 1, 15))
            e.dma(ld, t_at(i), at_sb[:, :, :],
                  s_attn.transpose([1, 0, 2])[:, :, i * T:(i + 1) * T])

    @stg.engine("tensor")
    def _(e):
        e.wait(("wcast", 15))
        for i in range(NT):
            e.wait_dma(ld, t_at(i))
            for oc in range(16):
                go = i * 16 + oc
                if go >= 4:
                    e.wait(("r1ev", go - 4))
                for ec in range(16):
                    e.unordered_op(lambda t, ec=ec, oc=oc, go=go: t.matmul(po[:, go % 4, :], wp_sb[:, ec, oc, :],
                                                                           at_sb[:, ec, :], start=(ec == 0), stop=(ec == 15)),
                                   label=("mo", i, oc) if ec == 15 else None)

    @stg.engine("vector")
    def _(e):
        for dc in range(16):
            e.wait_dma(ld, t_stripe(dc))
            e.op(lambda v, dc=dc: v.tensor_copy(wp_sb[:, dc, :, :],
                                                wstripe[:, dc % 2, :].rearrange("p (o m) -> p o m", m=128)),
                 label=("wcast", dc))
        for i in range(NT):
            for oc in range(16):
                go = i * 16 + oc
                e.wait(("mo", i, oc))
                if go >= 4:
                    e.wait_dma(st, t_r1((go - 4) // 16, (go - 4) % 16))
                e.op(lambda v, oc=oc, go=go: v.tensor_copy(r1_sb[:, go % 4, :], po[:, go % 4, :]),
                     label=("r1ev", go))

    @stg.engine("gpsimd")
    def _(e):
        for i in range(NT):
            for oc in range(16):
                go = i * 16 + oc
                e.wait(("r1ev", go))
                e.dma(st, t_r1(i, oc), s_r1[oc, :, i * T:(i + 1) * T], r1_sb[:, go % 4, :])
        e.wait_dma_all(st, t_r1(NT - 1, 15))

    stg.run()
    return ["s_r1"]



def build_stageI(nc, io, es, gs=None):
    s_xn1 = io.d("s_xn1", [16, 128, NT * T], BF16, "scr_in")
    s_lg = io.d("s_lg", [8, NT * T], F32, "scr_in")
    s_r1 = io.d("s_r1", [16, 128, NT * T], F32, "scr_in")
    s_x1 = io.d("s_x1", [NT * 4, 128, D], F32, "scr_in")
    s_web_ap = io.d("s_web", [8, 16, 16, 128, 128], BF16, "scr_in")
    b_moe = io.d("b_moe", [8, D], F32, "in")
    selM = io.d("selM", [8, 8, 128], BF16, "in")
    identf = io.d("identf", [128, 128], F32, "in")
    identb = io.d("identb", [128, 128], BF16, "in")
    out = io.d("out", [2, PPC, D], F32, "out")

    E = es.enter_context
    idf = E(nc.sbuf_tensor("Iidf", [128, 128], F32))
    idb = E(nc.sbuf_tensor("Iidb", [128, 128], BF16))
    selM_sb = E(nc.sbuf_tensor("IselM", [8, 8, 128], BF16))
    bm_f = E(nc.sbuf_tensor("Ibmf", [8, D], F32))
    bm_sb = E(nc.sbuf_tensor("Ibm", [8, 16, 128], BF16))
    we_sb = E(nc.sbuf_tensor("Iwe", [128, 3, 16, 128], BF16))
    xn1_sb = E(nc.sbuf_tensor("Ixn", [128, 16, T], BF16))
    lg_sb = E(nc.sbuf_tensor("Ilg", [8, T], F32))
    lt_sb = E(nc.sbuf_tensor("Ilt", [128, 4, 8], F32))
    et_sb = E(nc.sbuf_tensor("Iet", [128, 4, 8], F32))
    rm_sb = E(nc.sbuf_tensor("Irm", [128, 4, 4], F32))
    m_sb = E(nc.sbuf_tensor("Im", [128, 4, 8], BF16))
    mrows = E(nc.sbuf_tensor("Imrows", [8, T], BF16))
    mb_sb = E(nc.sbuf_tensor("Imb", [128, 2, T], BF16))
    z_sb = E(nc.sbuf_tensor("Iz", [128, 8, 16, T], BF16))
    r1_sb = E(nc.sbuf_tensor("Ir1", [128, 2, T], F32))
    sum1_sb = E(nc.sbuf_tensor("Isum1", [128, 2, T], F32))
    x1p_sb = E(nc.sbuf_tensor("Ix1p", [128, 2, 4, 128], F32))
    x2p_sb = E(nc.sbuf_tensor("Ix2p", [128, 2, 4, 128], F32))
    pltr = E(nc.psum_tensor("Ipltr", [128, 32], F32))
    pmr = E(nc.psum_tensor("Ipmr", [8, T], BF16))
    pb = E(nc.psum_tensor("Ipb", [128, 2, T], F32))
    pm = E(nc.psum_tensor("Ipm", [128, 2, T], F32))
    pxt = E(nc.psum_tensor("Ipxt", [128, 2, T], F32))
    stg = Stage(nc, es, "I", gs)
    gs = stg.gs
    ld = gs.dq("q_sync")
    st = gs.dq("q_gpsimd")
    so = gs.dq("q_scalar")
    wq = gs.dq("q_gpsimd")

    T_IDF = ("idf",)
    T_IDB = ("idb",)
    T_SELM = ("selm",)
    T_BM = ("bm",)
    def t_xn(i):
        return ("xn", i)
    def t_lg(i):
        return ("lg", i)
    def t_r1(i, oc):
        return ("r1", i, oc)
    def t_x1p(i, oc):
        return ("x1p", i, oc)
    def t_out(i, oc, tc):
        return ("out", i, oc, tc)
    def t_we(i, oc, e8):
        return ("we", i, oc, e8)


    @stg.engine("sync")
    def _(e):
        e.dma(ld, T_IDF, idf[:, :], identf)
        e.dma(ld, T_IDB, idb[:, :], identb)
        e.dma(ld, T_SELM, selM_sb[:, :, :], selM)
        e.dma(ld, T_BM, bm_f[:, :], b_moe)
        for i in range(NT):
            if i >= 1:
                e.wait(("mmb", (i - 1) * 128 + 127))
            e.dma(ld, t_xn(i), xn1_sb[:, :, :],
                  s_xn1.transpose([1, 0, 2])[:, :, i * T:(i + 1) * T])
            if i >= 1:
                e.wait(("lgread", i - 1))
            e.dma(ld, t_lg(i), lg_sb[:, :], s_lg[:, i * T:(i + 1) * T])
            for oc in range(16):
                go = i * 16 + oc
                if go >= 2:
                    e.wait(("s1", go - 2))
                e.dma(ld, t_r1(i, oc), r1_sb[:, go % 2, :], s_r1[oc, :, i * T:(i + 1) * T])
                if go >= 2:
                    e.wait(("x2", go - 2))
                e.dma(ld, t_x1p(i, oc), x1p_sb[:, go % 2, :, :],
                      s_x1[i * 4:(i + 1) * 4, :, oc * 128:(oc + 1) * 128].transpose([1, 0, 2]))

    @stg.engine("gpsimd")
    def _(e):
        for i in range(NT):
            for oc in range(16):
                for e8 in range(8):
                    bk = (i * 16 + oc) * 8 + e8
                    if bk >= 3:
                        e.wait(("mmb", bk - 3))
                    e.dma(wq, t_we(i, oc, e8), we_sb[:, bk % 3, :, :],
                          s_web_ap[e8, oc].transpose([1, 0, 2]))
        e.wait(("done",))

    @stg.engine("tensor")
    def _(e):
        e.wait_dma(ld, T_IDF)
        e.wait_dma(ld, T_IDB)
        e.wait_dma(ld, T_SELM)
        e.wait(("bmcast",))
        for i in range(NT):
            e.wait_dma(ld, t_lg(i))
            if i >= 1:
                e.wait(("lgread", i - 1))
            for tc in range(4):
                e.unordered_op(lambda t, tc=tc: t.transpose(pltr[:, tc * 8:(tc + 1) * 8],
                                                            lg_sb[0:8, tc * 128:(tc + 1) * 128],
                                                            idf[0:8, 0:8]))
            e.unordered_drain_op(lambda t: t.nop(), label=("ltr", i))
            if i >= 1:
                e.wait(("mrev", i - 1))
            for tc in range(4):
                e.wait(("mask", i, tc))
                e.unordered_op(lambda t, tc=tc: t.transpose(pmr[:, tc * 128:(tc + 1) * 128],
                                                            m_sb[:, tc, :], idb[:, :]))
            e.unordered_drain_op(lambda t: t.nop(), label=("mtr", i))
            e.wait_dma(ld, t_xn(i))
            for e8 in range(8):
                e.wait(("mrev", i))
                if i * 8 + e8 >= 2:
                    pz = i * 8 + e8 - 2
                    e.wait(("mbev", pz // 8, pz % 8))
                e.unordered_op(lambda t, e8=e8: t.matmul(pb[:, e8 % 2, :], selM_sb[:, e8, :], mrows[:, :],
                                                         start=True, stop=True),
                               label=("mbb", i, e8))
            for oc in range(16):
                go = i * 16 + oc
                if go >= 2:
                    e.wait(("s1", go - 2))
                e.wait(("z", i, 7))
                e.unordered_op(lambda t, oc=oc: t.matmul(pm[:, oc % 2, :], bm_sb[:, oc, :], mrows[:, :],
                                                         start=True, stop=False))
                for e8 in range(8):
                    bk = go * 8 + e8
                    e.wait_dma(wq, t_we(i, oc, e8))
                    for kc in range(16):
                        e.unordered_op(lambda t, kc=kc, e8=e8, oc=oc, bk=bk: t.matmul(
                            pm[:, oc % 2, :], we_sb[:, bk % 3, kc, :], z_sb[:, e8, kc, :],
                            start=False, stop=(e8 == 7 and kc == 15)),
                            label=("mmb", bk) if kc == 15 else None)
                e.wait(("s1", go))
                if go >= 2:
                    e.wait(("x2", go - 2))
                for tc in range(4):
                    e.unordered_op(lambda t, tc=tc, go=go: t.transpose(
                        pxt[:, go % 2, tc * 128:(tc + 1) * 128], sum1_sb[:, go % 2, tc * 128:(tc + 1) * 128],
                        idf[:, :]))
                e.unordered_drain_op(lambda t: t.nop(), label=("str", go))

    @stg.engine("vector")
    def _(e):
        e.wait_dma(ld, T_BM)
        e.op(lambda v: v.tensor_copy(bm_sb[:, :, :], bm_f.rearrange("p (o m) -> p o m", m=128)),
             label=("bmcast",))
        for i in range(NT):
            e.wait(("ltr", i))
            e.op(lambda v: v.tensor_copy(lt_sb[:, :, :], pltr[:, :].rearrange("p (tc j) -> p tc j", j=8)),
                 label=("lgread", i))
            for tc in range(4):
                e.op(lambda v, tc=tc: v.reduce_max(rm_sb[:, tc, 0:1], lt_sb[:, tc, :], AX.X))
                e.op(lambda v, tc=tc: v.tensor_scalar(rm_sb[:, tc, 1:2], rm_sb[:, tc, 0:1], -1.0, None, ALU.mult),
                     label=("nm", i, tc))
                e.wait(("et", i, tc))
                e.op(lambda v, tc=tc: v.reduce_sum(rm_sb[:, tc, 2:3], et_sb[:, tc, :], AX.X))
                e.op(lambda v, tc=tc: v.reciprocal(rm_sb[:, tc, 3:4], rm_sb[:, tc, 2:3]))
                e.op(lambda v, tc=tc: v.tensor_scalar(m_sb[:, tc, :], et_sb[:, tc, :], 1.0, rm_sb[:, tc, 3:4],
                                                      ALU.is_ge, ALU.mult), label=("mask", i, tc))
            e.wait(("mtr", i))
            e.op(lambda v: v.tensor_copy(mrows[:, :], pmr[:, :]), label=("mrev", i))
            for e8 in range(8):
                e.wait(("mbb", i, e8))
                e.op(lambda v, e8=e8: v.tensor_copy(mb_sb[:, e8 % 2, :], pb[:, e8 % 2, :]),
                     label=("mbev", i, e8))
                if i >= 1 and e8 == 0:
                    e.wait(("mmb", (i - 1) * 128 + 127))
                for kc in range(16):
                    ins = e.op(lambda v, e8=e8, kc=kc: v.tensor_tensor(z_sb[:, e8, kc, :], xn1_sb[:, kc, :],
                                                                       mb_sb[:, e8 % 2, :], ALU.mult),
                               label=("z", i, e8) if kc == 15 else None)
            for oc in range(16):
                go = i * 16 + oc
                e.wait(("mmb", go * 8 + 7))
                e.wait_dma(ld, t_r1(i, oc))
                if go >= 2:
                    e.wait(("str", go - 2))
                e.op(lambda v, go=go: v.tensor_tensor(sum1_sb[:, go % 2, :], pm[:, go % 2, :],
                                                      r1_sb[:, go % 2, :], ALU.add), label=("s1", go))
                e.wait(("str", go))
                e.wait_dma(ld, t_x1p(i, oc))
                if go >= 2:
                    for tc in range(4):
                        e.wait_dma(so, t_out((go - 2) // 16, (go - 2) % 16, tc))
                e.op(lambda v, go=go: v.tensor_tensor(x2p_sb[:, go % 2, :, :],
                                                      pxt[:, go % 2, :].rearrange("p (tc m) -> p tc m", m=128),
                                                      x1p_sb[:, go % 2, :, :], ALU.add), label=("x2", go))

    @stg.engine("scalar")
    def _(e):
        for i in range(NT):
            for tc in range(4):
                e.wait(("nm", i, tc))
                e.op(lambda s_, tc=tc: s_.activation(et_sb[:, tc, :], lt_sb[:, tc, :], AF.Exp,
                                                     bias=rm_sb[:, tc, 1:2], scale=1.0), label=("et", i, tc))
            for oc in range(16):
                go = i * 16 + oc
                e.wait(("x2", go))
                for tc in range(4):
                    sq = tc // 2
                    p0 = i * 256 + (tc % 2) * 128
                    e.dma(so, t_out(i, oc, tc), out[sq, p0:p0 + 128, oc * 128:(oc + 1) * 128],
                          x2p_sb[:, go % 2, tc, :])
        e.wait_dma_all(so, t_out(NT - 1, 15, 3))
        e.op(lambda s_: s_.activation(et_sb[:, 0, :], et_sb[:, 0, :], AF.Identity, bias=0.0, scale=1.0),
             label=("done",))

    stg.run()
    return ["out"]



def _shard(x, c):
    return np.ascontiguousarray(x[:, c * PPC:(c + 1) * PPC, :])


def build_AG(nc, io, ges):
    totals = io.d("totals", [2, D], F32, "scr_out")
    ag = io.d("ag", [16, D], F32, "scr_out")
    ccsem = ges.enter_context(nc.semaphore("ccsem"))
    with nc.Block() as blk:
        @blk.gpsimd
        def _(g):
            g.collective_compute(
                "AllGather", ALU.bypass,
                replica_groups=[list(range(NCORES))],
                ins=[totals.opt()], outs=[ag.opt()],
            ).then_inc(ccsem, 1)
            g.wait_ge(ccsem, 1)


def _in_maps(inputs, w, c, shards):
    maps = []
    for ci in range(NCORES):
        maps.append({
            "x": shards[ci], "pmask": _prefix_mask(ci),
            "WkT": w["WkT"], "WvT": w["WvT"], "bk": w["bk"], "bv": w["bv"],
            "WqT": w["WqT"], "WoT": w["WoT"], "bq": w["bq"], "bo": w["bo"],
            "SU_ug": w["SU_ug"], "SU_dp": w["SU_dp"],
            "Wscale_ug": w["Wscale_ug"], "Wscale_dp": w["Wscale_dp"],
            "WupT": w["WupT"], "WgateT": w["WgateT"], "WdownT": w["WdownT"],
            "wg": w["wg"], "WqkvT": w["WqkvT"], "WprojT": w["WprojT"],
            "We_blk": w["We_blk"], "b_moe": w["b_moe"],
            "identb": c["identb"], "identf": c["identf"],
            "H128a": c["H128a"], "H128s": c["H128s"], "Hdd": c["Hdd"],
            "selS": c["selS"], "selB": c["selB"], "selM": c["selM"],
        })
    return maps


def kernel(**inputs):
    inputs = {k: np.asarray(v) for k, v in inputs.items()}
    w = _host_weights(inputs)
    c = _host_constants()
    x = inputs["L0_attn_x"]
    shards = [_shard(x, ci) for ci in range(NCORES)]

    nc = bass.Bass("TRN2", target_bir_lowering=False)
    io = IO(nc, fused=True)
    with contextlib.ExitStack() as ges:
        gs = GlobalSync(nc, ges)
        for builder in [build_stageA]:
            with contextlib.ExitStack() as es:
                builder(nc, io, es, gs)
        build_AG(nc, io, ges)
        for builder in [build_stageB1, build_stageB2, build_stageD, build_stageE,
                        build_stageF, build_stageG, build_stageH1, build_stageH2,
                        build_stageI]:
            with contextlib.ExitStack() as es:
                builder(nc, io, es, gs)
        res = run_bass_kernel_spmd(nc, _in_maps(inputs, w, c, shards),
                                   core_ids=list(range(NCORES)))
    return np.ascontiguousarray(
        np.concatenate([np.asarray(res.results[ci]["out"], np.float32)
                        for ci in range(NCORES)], axis=1))


def kernel_unfused(**inputs):
    inputs = {k: np.asarray(v) for k, v in inputs.items()}
    w = _host_weights(inputs)
    c = _host_constants()
    x = inputs["L0_attn_x"]
    shards = [_shard(x, ci) for ci in range(NCORES)]

    def launch(builder, per_core_maps):
        nc = bass.Bass("TRN2", target_bir_lowering=False)
        io = IO(nc, fused=False)
        with contextlib.ExitStack() as es:
            builder(nc, io, es)
            res = run_bass_kernel_spmd(nc, per_core_maps, core_ids=list(range(NCORES)))
        return res.results

    rA = launch(build_stageA, [
        {"x": shards[ci], "WkT": w["WkT"], "WvT": w["WvT"], "bk": w["bk"], "bv": w["bv"],
         "identb": c["identb"]} for ci in range(NCORES)])
    ag = np.concatenate([np.asarray(rA[ci]["totals"]) for ci in range(NCORES)], axis=0)

    rB1 = launch(build_stageB1, [
        {"s_xn": rA[ci]["s_xn"], "s_kv": rA[ci]["s_kv"], "ag": ag, "pmask": _prefix_mask(ci),
         "WqT": w["WqT"], "WoT": w["WoT"], "bq": w["bq"], "bo": w["bo"], "identf": c["identf"]}
        for ci in range(NCORES)])

    rB2 = launch(build_stageB2, [
        {"s_xn": rA[ci]["s_xn"], "SU_ug": w["SU_ug"], "H128a": c["H128a"]}
        for ci in range(NCORES)])

    rD = launch(build_stageD, [
        {"s_y": rB2[ci]["s_y"], "WupT": w["WupT"], "WgateT": w["WgateT"],
         "Wscale_ug": w["Wscale_ug"], "SU_dp": w["SU_dp"], "H128s": c["H128s"],
         "We_blk": w["We_blk"]}
        for ci in range(NCORES)])

    rE = launch(build_stageE, [
        {"s_wt": rD[ci]["s_wt"], "Hdd": c["Hdd"]} for ci in range(NCORES)])

    rF = launch(build_stageF, [
        {"s_yp": rE[ci]["s_yp"], "s_ratt": rB1[ci]["s_ratt"], "WdownT": w["WdownT"],
         "Wscale_dp": w["Wscale_dp"]} for ci in range(NCORES)])

    rG = launch(build_stageG, [
        {"x": shards[ci], "s_dh": rF[ci]["s_dh"], "wg": w["wg"], "identf": c["identf"]}
        for ci in range(NCORES)])

    rH1 = launch(build_stageH1, [
        {"s_xn1": rG[ci]["s_xn1"], "WqkvT": w["WqkvT"], "selS": c["selS"], "selB": c["selB"]}
        for ci in range(NCORES)])

    rH2 = launch(build_stageH2, [
        {"s_attn": rH1[ci]["s_attn"], "WprojT": w["WprojT"]} for ci in range(NCORES)])

    rI = launch(build_stageI, [
        {"s_xn1": rG[ci]["s_xn1"], "s_lg": rG[ci]["s_lg"], "s_r1": rH2[ci]["s_r1"],
         "s_x1": rG[ci]["s_x1"], "s_web": rD[ci]["s_web"], "b_moe": w["b_moe"],
         "selM": c["selM"], "identf": c["identf"], "identb": c["identb"]}
        for ci in range(NCORES)])

    return np.ascontiguousarray(
        np.concatenate([np.asarray(rI[ci]["out"], np.float32) for ci in range(NCORES)], axis=1))


# version 09 (final)
# speedup vs baseline: 1.0125x; 1.0125x over previous
import contextlib
import numpy as np
import ml_dtypes
import concourse.bass as bass
import concourse.mybir as mybir
from concourse.bass_utils import run_bass_kernel_spmd

F32 = mybir.dt.float32
BF16 = mybir.dt.bfloat16
AF = mybir.ActivationFunctionType
ALU = mybir.AluOpType
AX = mybir.AxisListType
BF = ml_dtypes.bfloat16

NCORES = 8
PPC = 2048
NT = 8
T = 512
H = 256
D = 2048
INTER = 8192
SCALE0 = float(2048 ** -0.5) * 0.0 + float(128 ** -0.5)
EPS = 1e-5


def _hadamard(n):
    Hm = np.array([[1.0]], dtype=np.float64)
    while Hm.shape[0] < n:
        Hm = np.block([[Hm, Hm], [Hm, -Hm]])
    return Hm.astype(np.float32)


def _host_constants():
    H128 = _hadamard(128)
    H64 = _hadamard(64)
    s2 = np.float32((1.0 / np.sqrt(np.float32(2048.0))) / 64.0)
    s8 = np.float32((1.0 / np.sqrt(np.float32(8192.0))) / 64.0)
    c = {}
    c["identb"] = np.eye(128, dtype=np.float32).astype(BF)
    c["identf"] = np.eye(128, dtype=np.float32)
    c["H128a"] = (H128 * s2).astype(BF)
    c["H128s"] = (H128 * s8).astype(BF)
    hdd = np.zeros((128, 128), np.float32)
    hdd[:64, :64] = H64
    hdd[64:, 64:] = H64
    c["Hdd"] = hdd.astype(BF)
    selS = np.zeros((128, 16, 16), np.float32)
    for h in range(16):
        selS[:, h, h] = SCALE0
    c["selS"] = selS.astype(BF)
    selB = np.zeros((16, 16, 128), np.float32)
    for h in range(16):
        selB[h, h, :] = 1.0
    c["selB"] = selB.astype(BF)
    selM = np.zeros((8, 8, 128), np.float32)
    for e in range(8):
        selM[e, e, :] = 1.0
    c["selM"] = selM.astype(BF)
    return c


def _host_weights(inp):
    w = {}
    w["WqT"] = np.ascontiguousarray(inp["L0_attn_Wq"].T)
    w["WkT"] = np.ascontiguousarray(inp["L0_attn_Wk"].T)
    w["WvT"] = np.ascontiguousarray(inp["L0_attn_Wv"].T)
    w["WoT"] = np.ascontiguousarray(inp["L0_attn_Wo"].T)
    w["bq"] = np.ascontiguousarray(inp["L0_attn_bq"])
    w["bk"] = np.ascontiguousarray(inp["L0_attn_bk"])
    w["bv"] = np.ascontiguousarray(inp["L0_attn_bv"])
    w["bo"] = np.ascontiguousarray(inp["L0_attn_bo"])
    w["WupT"] = np.ascontiguousarray(inp["L0_ffn_W_up"].T)
    w["WgateT"] = np.ascontiguousarray(inp["L0_ffn_W_gate"].T)
    w["WdownT"] = np.ascontiguousarray(inp["L0_ffn_W_down"].T)
    w["SU_ug"] = np.ascontiguousarray(inp["L0_ffn_SU_ug"])
    w["SU_dp"] = np.ascontiguousarray(inp["L0_ffn_SU_dp"])
    w["Wscale_ug"] = np.ascontiguousarray(inp["L0_ffn_Wscale_ug"])
    w["Wscale_dp"] = np.ascontiguousarray(inp["L0_ffn_Wscale_dp"])
    Wqkv = inp["L1_attn_weight_qkv"]
    order = []
    for h in range(16):
        g, kk = h // 4, h % 4
        order.extend(range(g * 768 + kk * 128, g * 768 + kk * 128 + 128))
    for g in range(4):
        order.extend(range(g * 768 + 512, g * 768 + 640))
    for g in range(4):
        order.extend(range(g * 768 + 640, g * 768 + 768))
    w["WqkvT"] = np.ascontiguousarray(Wqkv[np.array(order), :].T)
    w["WprojT"] = np.ascontiguousarray(inp["L1_attn_weight_proj"].T)
    w["wg"] = np.ascontiguousarray(inp["L1_ffn_wg"])
    We = inp["L1_ffn_W"]
    WeT = We.transpose(0, 2, 1)
    w["We_blk"] = np.ascontiguousarray(
        WeT.reshape(8, 16, 128, 16, 128).transpose(0, 3, 1, 2, 4))
    w["b_moe"] = np.ascontiguousarray(inp["L1_ffn_b"])
    return w


def _prefix_mask(core):
    m = np.zeros((16, 2), np.float32)
    for cc in range(core):
        m[2 * cc, 0] = 1.0
        m[2 * cc + 1, 1] = 1.0
    return m



class IO:

    def __init__(self, nc, fused):
        self.nc = nc
        self.fused = fused
        self.cache = {}

    def d(self, name, shape, dtype, role):
        if name in self.cache:
            return self.cache[name]
        if role.startswith("scr") and self.fused:
            t = self.nc.dram_tensor(name, shape, dtype)
        else:
            kind = {"in": "ExternalInput", "out": "ExternalOutput",
                    "scr_in": "ExternalInput", "scr_out": "ExternalOutput"}[role]
            t = self.nc.dram_tensor(name, shape, dtype, kind=kind)
        ap = t.ap()
        self.cache[name] = ap
        return ap


NQ = 8


def qtok(j, n=NQ):
    return j % n, 16 * (j // n + 1)


class DQ:

    def __init__(self, nc, es, name, n=NQ):
        self.sems = [es.enter_context(nc.semaphore(name + str(i))) for i in range(n)]
        self.n = n
        self.keymap = {}
        self.count = 0

    def alloc(self, key):
        if key in self.keymap:
            raise KeyError(f"duplicate DMA key {key}")
        self.keymap[key] = self.count
        self.count += 1
        return self.keymap[key]

    def j(self, key):
        return self.keymap[key]

    def inc(self, ins, j):
        ins.then_inc(self.sems[j % self.n], 16)

    def dma(self, eng, j, out, in_, **kw):
        if j >= self.n:
            self.wait_j(eng, j - self.n)
        ins = eng.dma_start(out=out, in_=in_, **kw)
        self.inc(ins, j)
        return ins

    def wait_j(self, eng, j):
        s, v = qtok(j, self.n)
        eng.wait_ge(self.sems[s], v)

    def wait_all_j(self, eng, j_hi):
        for s in range(self.n):
            j = j_hi - ((j_hi % self.n - s) % self.n)
            if j >= 0:
                eng.wait_ge(self.sems[s], 16 * (j // self.n + 1))


class Emitter:

    def __init__(self, stage, name, ordered):
        self.stage = stage
        self.name = name
        self.ordered = ordered
        self.n = 0
        self.m = 0
        self.eng = None
        self.sem = None
        self.lsem = None
        self.last_wait = 0

    def _label(self, label):
        self.m += 1
        if self.eng is None:
            assert label not in self.stage.labels, label
            self.stage.labels[label] = (self.name, self.m)

    def op(self, fn, label=None):
        self.n += 1
        if label is not None:
            self._label(label)
        if self.eng is None:
            return None
        if self.ordered and self.n - 1 > self.last_wait:
            self.eng.wait_ge(self.sem, self.n - 1)
        self.last_wait = max(self.last_wait, self.n - 1)
        ins = fn(self.eng)
        ins.then_inc(self.sem, 1)
        if label is not None:
            self.eng.drain().then_inc(self.lsem, 1)
        return ins

    def unordered_op(self, fn, label=None):
        if label is None:
            if self.eng is not None:
                fn(self.eng)
            return None
        self._label(label)
        if self.eng is None:
            return None
        ins = fn(self.eng)
        ins.then_inc(self.lsem, 1)
        return ins

    def unordered_drain_op(self, fn, label):
        self._label(label)
        if self.eng is None:
            return None
        fn(self.eng)
        self.eng.drain().then_inc(self.lsem, 1)
        return None

    def wait(self, label):
        if self.eng is None:
            return
        ename, pos = self.stage.labels[label]
        self.eng.wait_ge(self.stage.lsems[ename], pos)

    def wait_dma(self, dq, key):
        if self.eng is None:
            return
        dq.wait_j(self.eng, dq.j((self.stage.name,) + tuple(key)))

    def wait_dma_all(self, dq, key_hi):
        if self.eng is None:
            return
        dq.wait_all_j(self.eng, dq.j((self.stage.name,) + tuple(key_hi)))

    def dma(self, dq, key, out, in_, **kw):
        if self.eng is None:
            dq.alloc((self.stage.name,) + tuple(key))
            return
        dq.dma(self.eng, dq.j((self.stage.name,) + tuple(key)), out, in_, **kw)


class GlobalSync:

    ENGINES = [("sync", False), ("scalar", True), ("vector", True),
               ("tensor", False), ("gpsimd", True)]

    def __init__(self, nc, es):
        self.nc = nc
        self.es = es
        self.sems = {}
        self.lsems = {}
        self.counts = {}
        self.lcounts = {}
        for ename, _ in GlobalSync.ENGINES:
            self.sems[ename] = es.enter_context(nc.semaphore(f"g_{ename}"))
            self.lsems[ename] = es.enter_context(nc.semaphore(f"g_l_{ename}"))
            self.counts[ename] = 0
            self.lcounts[ename] = 0
        self.dqs = {}

    def dq(self, name):
        if name not in self.dqs:
            self.dqs[name] = DQ(self.nc, self.es, name)
        return self.dqs[name]


class Stage:

    def __init__(self, nc, es, name, gs=None):
        self.nc = nc
        self.name = name
        self.gs = gs if gs is not None else GlobalSync(nc, es)
        self.labels = {}
        self.sems = self.gs.sems
        self.lsems = self.gs.lsems
        self.progs = {}

    def engine(self, ename):
        def deco(fn):
            self.progs[ename] = fn
            return fn
        return deco

    def run(self):
        order = dict(GlobalSync.ENGINES)
        base_n = dict(self.gs.counts)
        base_m = dict(self.gs.lcounts)
        for ename, fn in self.progs.items():
            em = Emitter(self, ename, order[ename])
            em.n = base_n[ename]
            em.m = base_m[ename]
            em.last_wait = base_n[ename]
            fn(em)
            self.gs.counts[ename] = em.n
            self.gs.lcounts[ename] = em.m
        with self.nc.Block() as blk:
            for ename, fn in self.progs.items():
                em = Emitter(self, ename, order[ename])
                em.n = base_n[ename]
                em.m = base_m[ename]
                em.last_wait = base_n[ename]
                em.sem = self.sems[ename]
                em.lsem = self.lsems[ename]

                def section(eng, em=em, fn=fn):
                    em.eng = eng
                    fn(em)
                getattr(blk, ename)(section)



def build_stageA(nc, io, es, gs=None):
    x = io.d("x", [2, PPC, D], F32, "in")
    WkT = io.d("WkT", [D, D], F32, "in")
    WvT = io.d("WvT", [D, D], F32, "in")
    bk = io.d("bk", [D], F32, "in")
    bv = io.d("bv", [D], F32, "in")
    identb = io.d("identb", [128, 128], BF16, "in")
    s_xn = io.d("s_xn", [16, 128, NT * T], BF16, "scr_out")
    s_kv = io.d("s_kv", [16, 128, NT * T], F32, "scr_out")
    totals = io.d("totals", [2, D], F32, "scr_out")

    E = es.enter_context
    wk_sb = E(nc.sbuf_tensor("wk_sb", [128, 16, 16, 128], BF16))
    wv_sb = E(nc.sbuf_tensor("wv_sb", [128, 16, 16, 128], BF16))
    wstripe = E(nc.sbuf_tensor("wstripe", [128, 2, D], F32))
    bkv = E(nc.sbuf_tensor("bkv", [128, 16, 2], F32))
    idb = E(nc.sbuf_tensor("idb", [128, 128], BF16))
    xbuf = E(nc.sbuf_tensor("xbuf", [128, 2, D], F32))
    stat = E(nc.sbuf_tensor("stat", [128, 4, 8], F32))
    xntok = E(nc.sbuf_tensor("xntok", [128, 2, D], BF16))
    xnf = E(nc.sbuf_tensor("xnf", [128, 16, T], BF16))
    relu_sb = E(nc.sbuf_tensor("relu_sb", [128, T], F32))
    mn_sb = E(nc.sbuf_tensor("mn_sb", [128, T], F32))
    exp_sb = E(nc.sbuf_tensor("exp_sb", [128, T], F32))
    phi_sb = E(nc.sbuf_tensor("phi_sb", [128, T], F32))
    u_sb = E(nc.sbuf_tensor("u_sb", [128, T], F32))
    kvb = E(nc.sbuf_tensor("kvb", [128, 2, T], F32))
    carry = E(nc.sbuf_tensor("carry", [128, 16, 2], F32))
    zeros = E(nc.sbuf_tensor("zeros", [128, H], F32))
    epsc = E(nc.sbuf_tensor("epsc", [128, 1], F32))
    ptr = E(nc.psum_tensor("ptr", [128, 2, 2048], BF16))
    pk = E(nc.psum_tensor("pk", [128, 2, T], F32))
    pv = E(nc.psum_tensor("pv", [128, 2, T], F32))
    stg = Stage(nc, es, "A", gs)
    gs = stg.gs
    ld = gs.dq("q_sync")
    st = gs.dq("q_gpsimd")

    def t_stripe(j):
        return ("stripe", j)
    def t_bias(c, w_):
        return ("bias", c, w_)
    T_IDENT = ("ident",)
    def t_x(tg):
        return ("x", tg)
    def t_xn(i, fc):
        return ("xn", i, fc)
    def t_kv(i, ec):
        return ("kv", i, ec)
    def t_tot(ec, sq):
        return ("tot", ec, sq)


    @stg.engine("sync")
    def _(e):
        for wi, wsrc in enumerate([WkT, WvT]):
            for dc in range(16):
                j = wi * 16 + dc
                if j >= 2:
                    e.wait(("wcast", j - 2))
                e.dma(ld, t_stripe(j), wstripe[:, j % 2, :], wsrc[dc * 128:(dc + 1) * 128, :])
        for c in range(16):
            e.dma(ld, t_bias(c, 0), bkv[:, c, 0:1], bk[c * 128:(c + 1) * 128].rearrange("(p one) -> p one", one=1))
            e.dma(ld, t_bias(c, 1), bkv[:, c, 1:2], bv[c * 128:(c + 1) * 128].rearrange("(p one) -> p one", one=1))
        e.dma(ld, T_IDENT, idb[:, :], identb)
        for i in range(NT):
            for tc in range(4):
                tg = i * 4 + tc
                if tg >= 2:
                    e.wait(("apply", tg - 2))
                    e.wait(("rsum", tg - 2))
                sq = tc // 2
                p0 = i * 256 + (tc % 2) * 128
                e.dma(ld, t_x(tg), xbuf[:, tg % 2, :], x[sq, p0:p0 + 128, :])

    @stg.engine("vector")
    def _(e):
        e.op(lambda v: v.memset(zeros[:, :], 0.0))
        e.op(lambda v: v.memset(carry[:, :, :], 0.0))
        e.op(lambda v: v.memset(epsc[:, :], EPS))
        for wi, wdst in enumerate([wk_sb, wv_sb]):
            for dc in range(16):
                j = wi * 16 + dc
                e.wait_dma(ld, t_stripe(j))
                e.op(lambda v, wdst=wdst, dc=dc, j=j: v.tensor_copy(
                    wdst[:, dc, :, :],
                    wstripe[:, j % 2, :].rearrange("p (e m) -> p e m", m=128)),
                    label=("wcast", j))
        e.wait_dma_all(ld, T_IDENT)
        for i in range(NT):
            for tc in range(4):
                tg = i * 4 + tc
                e.wait_dma(ld, t_x(tg))
                e.op(lambda v, tc=tc, tg=tg: v.reduce_sum(stat[:, tc, 0:1], xbuf[:, tg % 2, :], AX.X),
                     label=("rsum", tg))
                e.wait(("sq", tg))
                e.op(lambda v, tc=tc: v.tensor_scalar(stat[:, tc, 2:3], stat[:, tc, 0:1], 1.0 / D, None, ALU.mult))
                e.op(lambda v, tc=tc: v.tensor_tensor(stat[:, tc, 7:8], stat[:, tc, 2:3], stat[:, tc, 2:3], ALU.mult))
                e.op(lambda v, tc=tc: v.tensor_scalar(stat[:, tc, 3:4], stat[:, tc, 1:2], 1.0 / D, None, ALU.mult))
                e.op(lambda v, tc=tc: v.tensor_tensor(stat[:, tc, 3:4], stat[:, tc, 3:4], stat[:, tc, 7:8], ALU.subtract),
                     label=("var", tg))
                e.wait(("sqrt", tg))
                e.op(lambda v, tc=tc: v.reciprocal(stat[:, tc, 5:6], stat[:, tc, 4:5]))
                e.op(lambda v, tc=tc: v.scalar_tensor_tensor(stat[:, tc, 6:7], stat[:, tc, 2:3], -1.0,
                                                             stat[:, tc, 5:6], ALU.mult, ALU.mult),
                     label=("rstd", tg))
            for tc in range(4):
                tg = i * 4 + tc
                e.wait(("tr", tg))
                if tc == 0 and i >= 1:
                    e.wait_dma_all(st, t_xn(i - 1, 15))
                e.op(lambda v, tc=tc, tg=tg: v.tensor_copy(
                    xnf[:, :, tc * 128:(tc + 1) * 128],
                    ptr[:, tg % 2, :].rearrange("p (c m) -> p c m", m=128)),
                    label=("evac", tg))
            for ec in range(16):
                g = i * 16 + ec
                e.wait(("mk", g))
                e.op(lambda v, ec=ec: v.tensor_scalar(relu_sb[:, :], pk[:, ec % 2, :], bkv[:, ec, 0:1], 0.0,
                                                      ALU.add, ALU.max), label=("relu", g))
                e.op(lambda v, ec=ec: v.tensor_scalar(mn_sb[:, :], pk[:, ec % 2, :], bkv[:, ec, 0:1], 0.0,
                                                      ALU.add, ALU.min), label=("mn", g))
                e.wait(("exp", g))
                e.op(lambda v: v.tensor_tensor(phi_sb[:, :], relu_sb[:, :], exp_sb[:, :], ALU.add),
                     label=("phi", g))
                e.wait(("mv", g))
                e.op(lambda v, ec=ec: v.scalar_tensor_tensor(u_sb[:, :], pv[:, ec % 2, :], bkv[:, ec, 1:2],
                                                             phi_sb[:, :], ALU.add, ALU.mult),
                     label=("u", g))
                if g >= 2:
                    e.wait_dma(st, t_kv((g - 2) // 16, (g - 2) % 16))
                e.op(lambda v, ec=ec, g=g: v.tensor_tensor_scan(kvb[:, g % 2, 0:H], u_sb[:, 0:H], zeros[:, :],
                                                                carry[:, ec, 0:1], ALU.add, ALU.add))
                e.op(lambda v, ec=ec, g=g: v.tensor_tensor_scan(kvb[:, g % 2, H:T], u_sb[:, H:T], zeros[:, :],
                                                                carry[:, ec, 1:2], ALU.add, ALU.add),
                     label=("scan", g))
                e.op(lambda v, ec=ec, g=g: v.tensor_copy(carry[:, ec, 0:1], kvb[:, g % 2, H - 1:H]))
                e.op(lambda v, ec=ec, g=g: v.tensor_copy(carry[:, ec, 1:2], kvb[:, g % 2, T - 1:T]),
                     label=("carry", g))

    @stg.engine("scalar")
    def _(e):
        e.wait_dma_all(ld, T_IDENT)
        for i in range(NT):
            for tc in range(4):
                tg = i * 4 + tc
                e.wait_dma(ld, t_x(tg))
                if tg >= 2:
                    e.wait(("tr", tg - 2))
                e.op(lambda s_, tc=tc, tg=tg: s_.activation(xntok[:, tg % 2, :], xbuf[:, tg % 2, :], AF.Square,
                                                            bias=0.0, scale=1.0, accum_out=stat[:, tc, 1:2]),
                     label=("sq", tg))
                e.wait(("var", tg))
                e.op(lambda s_, tc=tc: s_.activation(stat[:, tc, 4:5], stat[:, tc, 3:4], AF.Sqrt,
                                                     bias=epsc[:, 0:1], scale=1.0), label=("sqrt", tg))
                e.wait(("rstd", tg))
                e.op(lambda s_, tc=tc, tg=tg: s_.activation(xntok[:, tg % 2, :], xbuf[:, tg % 2, :], AF.Identity,
                                                            bias=stat[:, tc, 6:7], scale=stat[:, tc, 5:6]),
                     label=("apply", tg))
            for ec in range(16):
                g = i * 16 + ec
                if g >= 1:
                    e.wait(("phi", g - 1))
                e.wait(("mn", g))
                e.op(lambda s_: s_.activation(exp_sb[:, :], mn_sb[:, :], AF.Exp,
                                              bias=0.0, scale=1.0), label=("exp", g))

    @stg.engine("tensor")
    def _(e):
        e.wait_dma(ld, T_IDENT)
        e.wait(("wcast", 31))
        for i in range(NT):
            for tc in range(4):
                tg = i * 4 + tc
                e.wait(("apply", tg))
                if tg >= 2:
                    e.wait(("evac", tg - 2))
                for fc in range(15):
                    e.unordered_op(
                        lambda t, tg=tg, fc=fc: t.transpose(ptr[:, tg % 2, fc * 128:(fc + 1) * 128],
                                                            xntok[:, tg % 2, fc * 128:(fc + 1) * 128], idb[:, :]))
                e.unordered_drain_op(
                    lambda t, tg=tg: t.transpose(ptr[:, tg % 2, 15 * 128:16 * 128],
                                                 xntok[:, tg % 2, 15 * 128:16 * 128], idb[:, :]),
                    label=("tr", tg))
            e.wait(("evac", i * 4 + 3))
            for ec in range(16):
                g = i * 16 + ec
                if g >= 2:
                    e.wait(("relu", g - 2))
                    e.wait(("u", g - 2))
                for dc in range(16):
                    e.unordered_op(
                        lambda t, dc=dc, ec=ec: t.matmul(pk[:, ec % 2, :], wk_sb[:, dc, ec, :],
                                                         xnf[:, dc, :], start=(dc == 0), stop=(dc == 15)),
                        label=("mk", g) if dc == 15 else None)
                for dc in range(16):
                    e.unordered_op(
                        lambda t, dc=dc, ec=ec: t.matmul(pv[:, ec % 2, :], wv_sb[:, dc, ec, :],
                                                         xnf[:, dc, :], start=(dc == 0), stop=(dc == 15)),
                        label=("mv", g) if dc == 15 else None)

    @stg.engine("gpsimd")
    def _(e):
        for i in range(NT):
            e.wait(("evac", i * 4 + 3))
            for fc in range(16):
                e.dma(st, t_xn(i, fc), s_xn[fc, :, i * T:(i + 1) * T], xnf[:, fc, :])
            for ec in range(16):
                g = i * 16 + ec
                e.wait(("scan", g))
                e.dma(st, t_kv(i, ec), s_kv[ec, :, i * T:(i + 1) * T], kvb[:, g % 2, :])
        e.wait(("carry", NT * 16 - 1))
        for ec in range(16):
            for sq in range(2):
                e.dma(st, t_tot(ec, sq),
                      totals[sq, ec * 128:(ec + 1) * 128].rearrange("(p one) -> p one", one=1),
                      carry[:, ec, sq:sq + 1])
        e.wait_dma_all(st, t_tot(15, 1))

    stg.run()
    return ["s_xn", "s_kv", "totals"]



def build_stageB1(nc, io, es, gs=None):
    s_xn = io.d("s_xn", [16, 128, NT * T], BF16, "scr_in")
    s_kv = io.d("s_kv", [16, 128, NT * T], F32, "scr_in")
    ag = io.d("ag", [16, D], F32, "scr_in")
    pmask = io.d("pmask", [16, 2], F32, "in")
    WqT = io.d("WqT", [D, D], F32, "in")
    WoT = io.d("WoT", [D, D], F32, "in")
    bq = io.d("bq", [D], F32, "in")
    bo = io.d("bo", [D], F32, "in")
    identf = io.d("identf", [128, 128], F32, "in")
    s_ratt = io.d("s_ratt", [16, 128, NT * T], F32, "scr_out")

    E = es.enter_context
    wq_sb = E(nc.sbuf_tensor("wq_sb", [128, 16, 16, 128], BF16))
    wo_sb = E(nc.sbuf_tensor("wo_sb", [128, 16, 16, 128], BF16))
    wstripe = E(nc.sbuf_tensor("B1stripe", [128, 1, D], F32))
    bqo = E(nc.sbuf_tensor("bqo", [128, 16, 2], F32))
    idf = E(nc.sbuf_tensor("B1idf", [128, 128], F32))
    ag_sb = E(nc.sbuf_tensor("ag_sb", [16, D], F32))
    pm_sb = E(nc.sbuf_tensor("pm_sb", [16, 2], F32))
    prow = E(nc.sbuf_tensor("prow", [2, D], F32))
    prefix = E(nc.sbuf_tensor("prefix", [128, 16, 2], F32))
    xn_sb = E(nc.sbuf_tensor("B1xn", [128, 16, T], BF16))
    kv_sb = E(nc.sbuf_tensor("B1kv", [128, 2, T], F32))
    kvp_sb = E(nc.sbuf_tensor("B1kvp", [128, T], F32))
    relu_sb = E(nc.sbuf_tensor("B1relu", [128, T], F32))
    mn_sb = E(nc.sbuf_tensor("B1mn", [128, T], F32))
    exp_sb = E(nc.sbuf_tensor("B1exp", [128, T], F32))
    phi_sb = E(nc.sbuf_tensor("B1phi", [128, T], F32))
    ctx_sb = E(nc.sbuf_tensor("B1ctx", [128, 16, T], BF16))
    ratt_sb = E(nc.sbuf_tensor("B1ratt", [128, 2, T], F32))
    pq = E(nc.psum_tensor("B1pq", [128, 3, T], F32))
    po = E(nc.psum_tensor("B1po", [128, 3, T], F32))
    pp = E(nc.psum_tensor("B1pp", [2, T], F32))
    ppt = E(nc.psum_tensor("B1ppt", [128, 32], F32))
    stg = Stage(nc, es, "B1", gs)
    gs = stg.gs
    ld = gs.dq("q_sync")
    st = gs.dq("q_gpsimd")

    def t_stripe(j):
        return ("stripe", j)
    def t_bias(c, w_):
        return ("bias", c, w_)
    T_IDF = ("idf",)
    T_AG = ("ag",)
    T_PM = ("pm",)
    def t_xn(i):
        return ("xn", i)
    def t_kv(i, ec):
        return ("kv", i, ec)
    def t_ratt(i, oc):
        return ("ratt", i, oc)


    @stg.engine("sync")
    def _(e):
        for wi, wsrc in enumerate([WqT, WoT]):
            for dc in range(16):
                j = wi * 16 + dc
                if j >= 1:
                    e.wait(("wcast", j - 1))
                e.dma(ld, t_stripe(j), wstripe[:, 0, :], wsrc[dc * 128:(dc + 1) * 128, :])
        for c in range(16):
            e.dma(ld, t_bias(c, 0), bqo[:, c, 0:1], bq[c * 128:(c + 1) * 128].rearrange("(p one) -> p one", one=1))
            e.dma(ld, t_bias(c, 1), bqo[:, c, 1:2], bo[c * 128:(c + 1) * 128].rearrange("(p one) -> p one", one=1))
        e.dma(ld, T_IDF, idf[:, :], identf)
        e.dma(ld, T_AG, ag_sb[:, :], ag)
        e.dma(ld, T_PM, pm_sb[:, :], pmask)
        for i in range(NT):
            if i >= 1:
                e.wait(("ctxdone", i - 1))
            e.dma(ld, t_xn(i), xn_sb[:, :, :],
                  s_xn.transpose([1, 0, 2])[:, :, i * T:(i + 1) * T])
            for ec in range(16):
                g = i * 16 + ec
                if g >= 2:
                    e.wait(("kvp", g - 2))
                e.dma(ld, t_kv(i, ec), kv_sb[:, g % 2, :], s_kv[ec, :, i * T:(i + 1) * T])

    @stg.engine("tensor")
    def _(e):
        e.wait_dma(ld, T_AG)
        e.wait_dma(ld, T_PM)
        e.wait_dma(ld, T_IDF)
        for ng in range(4):
            e.unordered_op(lambda t, ng=ng: t.matmul(pp[:, :], pm_sb[:, :], ag_sb[:, ng * 512:(ng + 1) * 512],
                                                     start=True, stop=True),
                           label=("ppm", ng))
            e.wait(("ppev", ng))
        for fc in range(16):
            e.unordered_op(lambda t, fc=fc: t.transpose(ppt[:, fc * 2:(fc + 1) * 2],
                                                        prow[:, fc * 128:(fc + 1) * 128], idf[0:2, 0:2]))
        e.unordered_drain_op(lambda t: t.nop(), label=("ptr",))
        e.wait(("wcast", 31))
        for i in range(NT):
            e.wait_dma(ld, t_xn(i))
            for ec in range(16):
                g = i * 16 + ec
                if g >= 3:
                    e.wait(("mn", g - 3))
                for dc in range(16):
                    e.unordered_op(lambda t, dc=dc, ec=ec, g=g: t.matmul(pq[:, g % 3, :], wq_sb[:, dc, ec, :],
                                                                         xn_sb[:, dc, :], start=(dc == 0), stop=(dc == 15)),
                                   label=("mq", g) if dc == 15 else None)
            e.wait(("ctx", i, 15))
            for oc in range(16):
                go = i * 16 + oc
                if go >= 3:
                    e.wait(("rattev", go - 3))
                for ec in range(16):
                    e.unordered_op(lambda t, ec=ec, oc=oc, go=go: t.matmul(po[:, go % 3, :], wo_sb[:, ec, oc, :],
                                                                           ctx_sb[:, ec, :], start=(ec == 0), stop=(ec == 15)),
                                   label=("mo", i, oc) if ec == 15 else None)

    @stg.engine("vector")
    def _(e):
        for wi, wdst in enumerate([wq_sb, wo_sb]):
            for dc in range(16):
                j = wi * 16 + dc
                e.wait_dma(ld, t_stripe(j))
                e.op(lambda v, wdst=wdst, dc=dc, j=j: v.tensor_copy(
                    wdst[:, dc, :, :],
                    wstripe[:, 0, :].rearrange("p (e m) -> p e m", m=128)),
                    label=("wcast", j))
        for ng in range(4):
            e.wait(("ppm", ng))
            e.op(lambda v, ng=ng: v.tensor_copy(prow[:, ng * 512:(ng + 1) * 512], pp[:, :]),
                 label=("ppev", ng))
        e.wait(("ptr",))
        e.op(lambda v: v.tensor_copy(prefix[:, :, :], ppt[:, :].rearrange("p (c s) -> p c s", s=2)),
             label=("pfx",))
        e.wait_dma_all(ld, T_PM)
        for i in range(NT):
            for ec in range(16):
                g = i * 16 + ec
                e.wait(("mq", g))
                e.op(lambda v, ec=ec, g=g: v.tensor_scalar(relu_sb[:, :], pq[:, g % 3, :], bqo[:, ec, 0:1], 0.0,
                                                           ALU.add, ALU.max), label=("relu", g))
                e.op(lambda v, ec=ec, g=g: v.tensor_scalar(mn_sb[:, :], pq[:, g % 3, :], bqo[:, ec, 0:1], 0.0,
                                                           ALU.add, ALU.min), label=("mn", g))
                e.wait(("exp", g))
                e.op(lambda v: v.tensor_tensor(phi_sb[:, :], relu_sb[:, :], exp_sb[:, :], ALU.add))
                e.wait_dma(ld, t_kv(i, ec))
                e.op(lambda v, ec=ec, g=g: v.tensor_scalar(kvp_sb[:, 0:H], kv_sb[:, g % 2, 0:H],
                                                           prefix[:, ec, 0:1], SCALE0, ALU.add, ALU.mult))
                e.op(lambda v, ec=ec, g=g: v.tensor_scalar(kvp_sb[:, H:T], kv_sb[:, g % 2, H:T],
                                                           prefix[:, ec, 1:2], SCALE0, ALU.add, ALU.mult),
                     label=("kvp", g))
                e.op(lambda v, ec=ec: v.tensor_tensor(ctx_sb[:, ec, :], phi_sb[:, :], kvp_sb[:, :], ALU.mult),
                     label=("ctx", i, ec))
            e.op(lambda v: v.engine_nop(), label=("ctxdone", i))

    @stg.engine("scalar")
    def _(e):
        e.wait_dma_all(ld, T_PM)
        for i in range(NT):
            for ec in range(16):
                g = i * 16 + ec
                e.wait(("mn", g))
                e.op(lambda s_: s_.activation(exp_sb[:, :], mn_sb[:, :], AF.Exp, bias=0.0, scale=1.0),
                     label=("exp", g))
            for oc in range(16):
                go = i * 16 + oc
                e.wait(("mo", i, oc))
                if go >= 2:
                    e.wait_dma(st, t_ratt((go - 2) // 16, (go - 2) % 16))
                e.op(lambda s_, oc=oc, go=go: s_.activation(ratt_sb[:, oc % 2, :], po[:, go % 3, :], AF.Identity,
                                                            bias=bqo[:, oc, 1:2], scale=1.0),
                     label=("rattev", go))

    @stg.engine("gpsimd")
    def _(e):
        for i in range(NT):
            for oc in range(16):
                go = i * 16 + oc
                e.wait(("rattev", go))
                e.dma(st, t_ratt(i, oc), s_ratt[oc, :, i * T:(i + 1) * T], ratt_sb[:, oc % 2, :])
        e.wait_dma_all(st, t_ratt(NT - 1, 15))

    stg.run()
    return ["s_ratt"]



def build_stageB2(nc, io, es, gs=None):
    s_xn = io.d("s_xn", [16, 128, NT * T], BF16, "scr_in")
    SU_ug = io.d("SU_ug", [D], F32, "in")
    H128a = io.d("H128a", [128, 128], BF16, "in")
    s_y = io.d("s_y", [16, 128, NT * T], BF16, "scr_out")

    E = es.enter_context
    h_sb = E(nc.sbuf_tensor("B2h", [128, 128], BF16))
    su_sb = E(nc.sbuf_tensor("B2su", [128, 16], F32))
    xn_sb = E(nc.sbuf_tensor("B2xn", [128, 2, 16, T], BF16))
    z_sb = E(nc.sbuf_tensor("B2z", [128, 2, T], BF16))
    y0 = E(nc.sbuf_tensor("B2y0", [128, 16, T], BF16))
    y1 = E(nc.sbuf_tensor("B2y1", [128, 16, T], BF16))
    py = E(nc.psum_tensor("B2py", [128, 2, T], F32))
    stg = Stage(nc, es, "B2", gs)
    gs = stg.gs
    ld = gs.dq("q_sync")
    st = gs.dq("q_gpsimd")

    T_H = ("h",)
    def t_su(c):
        return ("su", c)
    def t_xn(i):
        return ("xn", i)
    def t_y(i, fc):
        return ("y", i, fc)


    @stg.engine("sync")
    def _(e):
        e.dma(ld, T_H, h_sb[:, :], H128a)
        for c in range(16):
            e.dma(ld, t_su(c), su_sb[:, c:c + 1], SU_ug[c * 128:(c + 1) * 128].rearrange("(p one) -> p one", one=1))
        for i in range(NT):
            if i >= 2:
                e.wait(("z", (i - 2) * 16 + 15))
            e.dma(ld, t_xn(i), xn_sb[:, i % 2, :, :],
                  s_xn.transpose([1, 0, 2])[:, :, i * T:(i + 1) * T])

    @stg.engine("vector")
    def _(e):
        e.wait_dma_all(ld, t_su(15))
        for i in range(NT):
            e.wait_dma(ld, t_xn(i))
            for fc in range(16):
                g = i * 16 + fc
                if g >= 2:
                    e.wait(("my", g - 2))
                e.op(lambda v, fc=fc, i=i, g=g: v.tensor_scalar(z_sb[:, g % 2, :], xn_sb[:, i % 2, fc, :],
                                                                su_sb[:, fc:fc + 1], None, ALU.mult),
                     label=("z", g))
                e.wait(("my", g))
                if g >= 16 and fc == 0:
                    e.wait_dma_all(st, t_y(i - 1, 15))
                e.op(lambda v, fc=fc, g=g: v.tensor_copy(y0[:, fc, :], py[:, g % 2, :]),
                     label=("ev", g))
            for si, b in enumerate((1, 2, 4, 8)):
                src, dst = (y0, y1) if si % 2 == 0 else (y1, y0)
                for c in range(16):
                    if c & b:
                        e.op(lambda v, c=c, b=b, src=src, dst=dst: v.tensor_tensor(
                            dst[:, c, :], src[:, c ^ b, :], src[:, c, :], ALU.subtract))
                    else:
                        e.op(lambda v, c=c, b=b, src=src, dst=dst: v.tensor_tensor(
                            dst[:, c, :], src[:, c, :], src[:, c ^ b, :], ALU.add))
            e.op(lambda v: v.engine_nop(), label=("ydone", i))

    @stg.engine("tensor")
    def _(e):
        e.wait_dma(ld, T_H)
        for i in range(NT):
            for fc in range(16):
                g = i * 16 + fc
                e.wait(("z", g))
                if g >= 2:
                    e.wait(("ev", g - 2))
                e.unordered_op(lambda t, g=g: t.matmul(py[:, g % 2, :], h_sb[:, :], z_sb[:, g % 2, :],
                                                       start=True, stop=True),
                               label=("my", g))

    @stg.engine("gpsimd")
    def _(e):
        for i in range(NT):
            e.wait(("ydone", i))
            for fc in range(16):
                e.dma(st, t_y(i, fc), s_y[fc, :, i * T:(i + 1) * T], y0[:, fc, :])
        e.wait_dma_all(st, t_y(NT - 1, 15))

    stg.run()
    return ["s_y"]



def build_stageD(nc, io, es, gs=None):
    s_y = io.d("s_y", [16, 128, NT * T], BF16, "scr_in")
    WupT = io.d("WupT", [D, INTER], F32, "in")
    WgateT = io.d("WgateT", [D, INTER], F32, "in")
    Wscale_ug = io.d("Wscale_ug", [2 * INTER], F32, "in")
    SU_dp = io.d("SU_dp", [INTER], F32, "in")
    H128s = io.d("H128s", [128, 128], BF16, "in")
    We_blk = io.d("We_blk", [8, 16, 16, 128, 128], F32, "in")
    s_wt = io.d("s_wt", [64, 128, NT * T], BF16, "scr_out")
    s_web = io.d("s_web", [8, 16, 16, 128, 128], BF16, "scr_out")

    E = es.enter_context
    wup_sb = E(nc.sbuf_tensor("Dwup", [128, 16, 16, 128], BF16))
    wgt_sb = E(nc.sbuf_tensor("Dwgt", [128, 16, 16, 128], BF16))
    wstripe = E(nc.sbuf_tensor("Dstripe", [128, 2, D], F32))
    h_sb = E(nc.sbuf_tensor("Dh", [128, 128], BF16))
    wsu_sb = E(nc.sbuf_tensor("Dwsu", [128, 64], F32))
    wsg_sb = E(nc.sbuf_tensor("Dwsg", [128, 64], F32))
    sud_sb = E(nc.sbuf_tensor("Dsud", [128, 64], F32))
    y_sb = E(nc.sbuf_tensor("Dy", [128, 2, 16, T], BF16))
    g_sb = E(nc.sbuf_tensor("Dg", [128, 2, T], F32))
    w_sb = E(nc.sbuf_tensor("Dw", [128, 2, T], BF16))
    wt_sb = E(nc.sbuf_tensor("Dwt", [128, 2, T], BF16))
    wei_f = E(nc.sbuf_tensor("Dweif", [128, 1, 16, 128], F32))
    web_b = E(nc.sbuf_tensor("Dwebb", [128, 1, 16, 128], BF16))
    pup = E(nc.psum_tensor("Dpup", [128, 3, T], F32))
    pgt = E(nc.psum_tensor("Dpgt", [128, 3, T], F32))
    ph = E(nc.psum_tensor("Dph", [128, 2, T], F32))
    stg = Stage(nc, es, "D", gs)
    gs = stg.gs
    ld = gs.dq("q_sync")
    st = gs.dq("q_gpsimd")

    def t_wsu(c):
        return ("wsu", c)
    def t_wsg(c):
        return ("wsg", c)
    def t_sud(c):
        return ("sud", c)
    T_H = ("h",)
    def t_wcast(b):
        return ("wecast", b)
    def t_webst(b):
        return ("webst", b)
    def t_stripe(q, j):
        return ("stripe", q, j)
    def t_y(q, i):
        return ("y", q, i)
    def t_wt(q, i, ic):
        return ("wt", q, i, ic)


    @stg.engine("sync")
    def _(e):
        for c in range(64):
            e.dma(ld, t_wsu(c), wsu_sb[:, c:c + 1], Wscale_ug[c * 128:(c + 1) * 128].rearrange("(p one) -> p one", one=1))
        for c in range(64):
            e.dma(ld, t_wsg(c), wsg_sb[:, c:c + 1], Wscale_ug[INTER + c * 128:INTER + (c + 1) * 128].rearrange("(p one) -> p one", one=1))
        for c in range(64):
            e.dma(ld, t_sud(c), sud_sb[:, c:c + 1], SU_dp[c * 128:(c + 1) * 128].rearrange("(p one) -> p one", one=1))
        e.dma(ld, T_H, h_sb[:, :], H128s)
        for q in range(4):
            for wi, wsrc in enumerate([WupT, WgateT]):
                for dc in range(16):
                    j = wi * 16 + dc
                    if q * 32 + j >= 2:
                        e.wait(("wcast", q * 32 + j - 2))
                    e.dma(ld, t_stripe(q, j), wstripe[:, j % 2, :],
                          wsrc[dc * 128:(dc + 1) * 128, q * D:(q + 1) * D])
            for i in range(NT):
                gt = q * NT + i
                if gt >= 2:
                    e.wait(("mgi", (gt - 2) * 16 + 15))
                e.dma(ld, t_y(q, i), y_sb[:, gt % 2, :, :],
                      s_y.transpose([1, 0, 2])[:, :, i * T:(i + 1) * T])
                for bb in range(4):
                    b = gt * 4 + bb
                    e8b, ocb = b // 16, b % 16
                    if b >= 1:
                        e.wait(("wecast", b - 1))
                    e.dma(ld, t_wcast(b), wei_f[:, 0, :, :], We_blk[e8b, ocb].transpose([1, 0, 2]))

    @stg.engine("vector")
    def _(e):
        e.wait_dma_all(ld, t_sud(63))
        e.op(lambda v: v.tensor_scalar(wsg_sb[:, :], wsg_sb[:, :], 64.0, None, ALU.mult))
        e.op(lambda v: v.tensor_scalar(wsu_sb[:, :], wsu_sb[:, :], 64.0, None, ALU.mult))
        e.op(lambda v: v.tensor_tensor(wsu_sb[:, :], wsu_sb[:, :], sud_sb[:, :], ALU.mult),
             label=("scales",))
        for q in range(4):
            for wi, wdst in enumerate([wup_sb, wgt_sb]):
                for dc in range(16):
                    j = wi * 16 + dc
                    e.wait_dma(ld, t_stripe(q, j))
                    e.op(lambda v, wdst=wdst, dc=dc, j=j: v.tensor_copy(
                        wdst[:, dc, :, :],
                        wstripe[:, j % 2, :].rearrange("p (i m) -> p i m", m=128)),
                        label=("wcast", q * 32 + j))
            for i in range(NT):
                for ic in range(16):
                    gq = (q * NT + i) * 16 + ic
                    ig = q * 16 + ic
                    e.wait(("mu", gq))
                    e.wait(("g", gq))
                    if gq >= 2:
                        e.wait(("mh", gq - 2))
                    e.op(lambda v, gq=gq, ig=ig: v.scalar_tensor_tensor(
                        w_sb[:, gq % 2, :], pup[:, gq % 3, :], wsu_sb[:, ig:ig + 1],
                        g_sb[:, gq % 2, :], ALU.mult, ALU.mult), label=("w", gq))
                    e.wait(("mh", gq))
                    if gq >= 2:
                        pg = gq - 2
                        e.wait_dma(st, t_wt(pg // (NT * 16), (pg // 16) % NT, pg % 16))
                    e.op(lambda v, gq=gq: v.tensor_copy(wt_sb[:, gq % 2, :], ph[:, gq % 2, :]),
                         label=("wtev", gq))
                    if ic in (3, 7, 11, 15):
                        gt_ = q * NT + i
                        b = gt_ * 4 + (ic // 4)
                        e.wait_dma(ld, t_wcast(b))
                        if b >= 1:
                            e.wait_dma(st, t_webst(b - 1))
                        e.op(lambda v, b=b: v.tensor_copy(web_b[:, 0, :, :], wei_f[:, 0, :, :]),
                             label=("wecast", b))

    @stg.engine("scalar")
    def _(e):
        e.wait(("scales",))
        for q in range(4):
            for i in range(NT):
                for ic in range(16):
                    gq = (q * NT + i) * 16 + ic
                    ig = q * 16 + ic
                    e.wait(("mgi", gq))
                    if gq >= 2:
                        e.wait(("w", gq - 2))
                    e.op(lambda s_, gq=gq, ig=ig: s_.activation(g_sb[:, gq % 2, :], pgt[:, gq % 3, :],
                                                                AF.Silu, bias=0.0, scale=wsg_sb[:, ig:ig + 1]),
                         label=("g", gq))

    @stg.engine("tensor")
    def _(e):
        def emit_mh(j2):
            e.wait(("w", j2))
            if j2 >= 2:
                e.wait(("wtev", j2 - 2))
            e.unordered_op(lambda t, j2=j2: t.matmul(ph[:, j2 % 2, :], h_sb[:, :], w_sb[:, j2 % 2, :],
                                                     start=True, stop=True),
                           label=("mh", j2))
        e.wait_dma(ld, T_H)
        for q in range(4):
            if q > 0:
                emit_mh(q * 128 - 2)
                emit_mh(q * 128 - 1)
            e.wait(("wcast", q * 32 + 31))
            for i in range(NT):
                gt = q * NT + i
                e.wait_dma(ld, t_y(q, i))
                for ic in range(16):
                    gq = gt * 16 + ic
                    if gq >= 3:
                        e.wait(("w", gq - 3))
                    for dc in range(16):
                        e.unordered_op(lambda t, dc=dc, ic=ic, gt=gt, gq=gq: t.matmul(
                            pup[:, gq % 3, :], wup_sb[:, dc, ic, :], y_sb[:, gt % 2, dc, :],
                            start=(dc == 0), stop=(dc == 15)),
                            label=("mu", gq) if dc == 15 else None)
                    if gq >= 3:
                        e.wait(("g", gq - 3))
                    for dc in range(16):
                        lab = None
                        if dc == 15:
                            lab = ("mgi", gq)
                        e.unordered_op(lambda t, dc=dc, ic=ic, gt=gt, gq=gq: t.matmul(
                            pgt[:, gq % 3, :], wgt_sb[:, dc, ic, :], y_sb[:, gt % 2, dc, :],
                            start=(dc == 0), stop=(dc == 15)), label=lab)
                    if gq - 2 >= q * 128:
                        emit_mh(gq - 2)

        emit_mh(NT * 64 - 2)
        emit_mh(NT * 64 - 1)

    @stg.engine("gpsimd")
    def _(e):
        for q in range(4):
            for i in range(NT):
                for ic in range(16):
                    gq = (q * NT + i) * 16 + ic
                    ig = q * 16 + ic
                    e.wait(("wtev", gq))
                    e.dma(st, t_wt(q, i, ic), s_wt[ig, :, i * T:(i + 1) * T], wt_sb[:, gq % 2, :])
                    if ic in (3, 7, 11, 15):
                        b = (q * NT + i) * 4 + (ic // 4)
                        e8b, ocb = b // 16, b % 16
                        e.wait(("wecast", b))
                        e.dma(st, t_webst(b), s_web[e8b, ocb].transpose([1, 0, 2]), web_b[:, 0, :, :])
        e.wait_dma_all(st, t_wt(3, NT - 1, 15))
        e.wait_dma_all(st, t_webst(127))

    stg.run()
    return ["s_wt"]



def build_stageE(nc, io, es, gs=None):
    s_wt = io.d("s_wt", [64, 128, NT * T], BF16, "scr_in")
    Hdd = io.d("Hdd", [128, 128], BF16, "in")
    s_yp = io.d("s_yp", [64, 128, NT * T], BF16, "scr_out")

    E = es.enter_context
    h_sb = E(nc.sbuf_tensor("Eh", [128, 128], BF16))
    wt_in = E(nc.sbuf_tensor("Ewt", [128, 2, 64, T], BF16))
    stage4 = E(nc.sbuf_tensor("Est", [128, 8, T], BF16))
    ph = E(nc.psum_tensor("Eph", [128, 2, T], F32))
    stg = Stage(nc, es, "E", gs)
    gs = stg.gs
    ld = gs.dq("q_sync")
    st = gs.dq("q_gpsimd")

    T_H = ("h",)
    def t_in(i, a):
        return ("in", i, a)
    def t_out(i, grp, a):
        return ("out", i, grp, a)


    @stg.engine("sync")
    def _(e):
        e.dma(ld, T_H, h_sb[:, :], Hdd)
        for i in range(NT):
            if i >= 2:
                e.wait(("em", (i - 2) * 64 + 63))
            for a in range(2):
                e.dma(ld, t_in(i, a), wt_in[a * 64:(a + 1) * 64, i % 2, :, :],
                      s_wt[:, a::2, i * T:(i + 1) * T])

    @stg.engine("tensor")
    def _(e):
        e.wait_dma(ld, T_H)
        for i in range(NT):
            e.wait_dma(ld, t_in(i, 0))
            e.wait_dma(ld, t_in(i, 1))
            for p2 in range(64):
                g = i * 64 + p2
                if g >= 2:
                    e.wait(("eev", g - 2))
                e.unordered_op(lambda t, i=i, p2=p2, g=g: t.matmul(ph[:, g % 2, :], h_sb[:, :],
                                                                  wt_in[:, i % 2, p2, :],
                                                                  start=True, stop=True),
                               label=("em", g))

    @stg.engine("vector")
    def _(e):
        for i in range(NT):
            for p2 in range(64):
                g = i * 64 + p2
                e.wait(("em", g))
                grp = p2 // 4
                if grp >= 2 and p2 % 4 == 0:
                    for a in range(2):
                        e.wait_dma(st, t_out(i, grp - 2, a))
                elif i >= 1 and grp < 2 and p2 % 4 == 0:
                    for a in range(2):
                        e.wait_dma(st, t_out(i - 1, 14 + grp, a))
                e.op(lambda v, g=g, p2=p2: v.tensor_copy(stage4[:, p2 % 8, :], ph[:, g % 2, :]),
                     label=("eev", g))

    @stg.engine("gpsimd")
    def _(e):
        for i in range(NT):
            for grp in range(16):
                e.wait(("eev", i * 64 + grp * 4 + 3))
                for a in range(2):
                    s0 = (grp % 2) * 4
                    e.dma(st, t_out(i, grp, a),
                          s_yp[:, 8 * grp + a::2, i * T:(i + 1) * T][:, 0:4, :],
                          stage4[a * 64:(a + 1) * 64, s0:s0 + 4, :])
        e.wait_dma_all(st, t_out(NT - 1, 15, 1))

    stg.run()
    return ["s_yp"]



def build_stageF(nc, io, es, gs=None):
    s_yp = io.d("s_yp", [64, 128, NT * T], BF16, "scr_in")
    s_ratt = io.d("s_ratt", [16, 128, NT * T], F32, "scr_in")
    WdownT = io.d("WdownT", [INTER, D], F32, "in")
    Wscale_dp = io.d("Wscale_dp", [D], F32, "in")
    s_dh = io.d("s_dh", [16, 128, NT * T], F32, "scr_out")

    E = es.enter_context
    wd_sb = E(nc.sbuf_tensor("Fwd", [128, 64, 8, 128], BF16))
    wstripe = E(nc.sbuf_tensor("Fstripe", [128, 1, 1024], F32))
    wsd_sb = E(nc.sbuf_tensor("Fwsd", [128, 16], F32))
    yp_sb = E(nc.sbuf_tensor("Fyp", [128, 64, T], BF16))
    ratt_sb = E(nc.sbuf_tensor("Fratt", [128, 2, T], F32))
    dd_sb = E(nc.sbuf_tensor("Fdd", [128, 2, T], F32))
    pd = E(nc.psum_tensor("Fpd", [128, 4, T], F32))
    stg = Stage(nc, es, "F", gs)
    gs = stg.gs
    ld = gs.dq("q_sync")
    st = gs.dq("q_gpsimd")

    def t_wsd(c):
        return ("wsd", c)
    def t_stripe(oh, j):
        return ("stripe", oh, j)
    def t_yp(oh, i, gch):
        return ("yp", oh, i, gch)
    def t_ratt(oh, i, oc):
        return ("ratt", oh, i, oc)
    def t_dh(oh, i, oc):
        return ("dh", oh, i, oc)


    @stg.engine("sync")
    def _(e):
        for c in range(16):
            e.dma(ld, t_wsd(c), wsd_sb[:, c:c + 1], Wscale_dp[c * 128:(c + 1) * 128].rearrange("(p one) -> p one", one=1))
        for oh in range(2):
            for j in range(64):
                if oh * 64 + j >= 1:
                    e.wait(("wcast", oh * 64 + j - 1))
                e.dma(ld, t_stripe(oh, j), wstripe[:, 0, :],
                      WdownT[j * 128:(j + 1) * 128, oh * 1024:(oh + 1) * 1024])
            for i in range(NT):
                gt = oh * NT + i
                if gt >= 1:
                    e.wait(("md", gt - 1, 7))
                for gch in range(8):
                    e.dma(ld, t_yp(oh, i, gch), yp_sb[:, gch * 8:(gch + 1) * 8, :],
                          s_yp.transpose([1, 0, 2])[:, gch * 8:(gch + 1) * 8, i * T:(i + 1) * T])
                for oc in range(8):
                    g8 = gt * 8 + oc
                    if g8 >= 2:
                        e.wait(("dd", g8 - 2))
                    e.dma(ld, t_ratt(oh, i, oc), ratt_sb[:, g8 % 2, :],
                          s_ratt[oh * 8 + oc, :, i * T:(i + 1) * T])

    @stg.engine("vector")
    def _(e):
        e.wait_dma_all(ld, t_wsd(15))
        e.op(lambda v: v.tensor_scalar(wsd_sb[:, :], wsd_sb[:, :], 64.0, None, ALU.mult),
             label=("scales",))
        for oh in range(2):
            for j in range(64):
                e.wait_dma(ld, t_stripe(oh, j))
                e.op(lambda v, j=j: v.tensor_copy(
                    wd_sb[:, j, :, :],
                    wstripe[:, 0, :].rearrange("p (o m) -> p o m", m=128)),
                    label=("wcast", oh * 64 + j))
            for i in range(NT):
                for oc in range(8):
                    gt = oh * NT + i
                    g8 = gt * 8 + oc
                    ocg = oh * 8 + oc
                    e.wait(("md", gt, oc))
                    e.wait_dma(ld, t_ratt(oh, i, oc))
                    if g8 >= 2:
                        pg = g8 - 2
                        e.wait_dma(st, t_dh(pg // (NT * 8), (pg // 8) % NT, pg % 8))
                    e.op(lambda v, g8=g8, ocg=ocg: v.scalar_tensor_tensor(
                        dd_sb[:, g8 % 2, :], pd[:, g8 % 4, :], wsd_sb[:, ocg:ocg + 1],
                        ratt_sb[:, g8 % 2, :], ALU.mult, ALU.add), label=("dd", g8))

    @stg.engine("tensor")
    def _(e):
        for oh in range(2):
            e.wait(("wcast", oh * 64 + 63))
            e.wait(("scales",))
            for i in range(NT):
                gt = oh * NT + i
                for oc in range(8):
                    g8 = gt * 8 + oc
                    if g8 >= 4:
                        e.wait(("dd", g8 - 4))
                    for fc in range(64):
                        if oc == 0 and fc % 8 == 0:
                            e.wait_dma(ld, t_yp(oh, i, fc // 8))
                        e.unordered_op(lambda t, fc=fc, oc=oc, g8=g8: t.matmul(
                            pd[:, g8 % 4, :], wd_sb[:, fc, oc, :], yp_sb[:, fc, :],
                            start=(fc == 0), stop=(fc == 63)),
                            label=("md", gt, oc) if fc == 63 else None)

    @stg.engine("gpsimd")
    def _(e):
        for oh in range(2):
            for i in range(NT):
                for oc in range(8):
                    g8 = (oh * NT + i) * 8 + oc
                    e.wait(("dd", g8))
                    e.dma(st, t_dh(oh, i, oc), s_dh[oh * 8 + oc, :, i * T:(i + 1) * T], dd_sb[:, g8 % 2, :])
        e.wait_dma_all(st, t_dh(1, NT - 1, 7))

    stg.run()
    return ["s_dh"]



def build_stageG(nc, io, es, gs=None):
    x = io.d("x", [2, PPC, D], F32, "in")
    s_dh = io.d("s_dh", [16, 128, NT * T], F32, "scr_in")
    wg = io.d("wg", [D, 8], F32, "in")
    identf = io.d("identf", [128, 128], F32, "in")
    s_x1 = io.d("s_x1", [NT * 4, 128, D], F32, "scr_out")
    s_xn1 = io.d("s_xn1", [16, 128, NT * T], BF16, "scr_out")
    s_lg = io.d("s_lg", [8, NT * T], F32, "scr_out")

    E = es.enter_context
    idf = E(nc.sbuf_tensor("Gidf", [128, 128], F32))
    wg_sb = E(nc.sbuf_tensor("Gwg", [128, 16, 8], F32))
    dh_sb = E(nc.sbuf_tensor("Gdh", [128, 16, T], F32))
    x_sb = E(nc.sbuf_tensor("Gx", [128, 4, D], F32))
    x1_sb = E(nc.sbuf_tensor("Gx1", [128, 4, D], F32))
    xn1tok = E(nc.sbuf_tensor("Gxn1t", [128, 2, D], F32))
    xn1f = E(nc.sbuf_tensor("Gxn1f", [128, 16, T], BF16))
    xf32 = E(nc.sbuf_tensor("Gxf32", [128, 2, 4, 128], F32))
    junk = E(nc.sbuf_tensor("Gjunk", [128, D], F32))
    stat = E(nc.sbuf_tensor("Gstat", [128, 4, 8], F32))
    lg_sb = E(nc.sbuf_tensor("Glg", [8, T], F32))
    epsc = E(nc.sbuf_tensor("Geps", [128, 1], F32))
    pt = E(nc.psum_tensor("Gpt", [128, 2, T], F32))
    pxt = E(nc.psum_tensor("Gpxt", [128, 2, T], F32))
    pl = E(nc.psum_tensor("Gpl", [8, T], F32))
    stg = Stage(nc, es, "G", gs)
    gs = stg.gs
    ld = gs.dq("q_sync")
    st = gs.dq("q_gpsimd")

    T_IDF = ("idf",)
    def t_wg(dc):
        return ("wg", dc)
    def t_dh(i):
        return ("dh", i)
    def t_x(i, tc):
        return ("x", i, tc)
    def t_x1(i, tc):
        return ("x1", i, tc)
    def t_xn1(i, fc):
        return ("xn1", i, fc)
    def t_lg(i):
        return ("lg", i)


    @stg.engine("sync")
    def _(e):
        e.dma(ld, T_IDF, idf[:, :], identf)
        for dc in range(16):
            e.dma(ld, t_wg(dc), wg_sb[:, dc, :], wg[dc * 128:(dc + 1) * 128, :])
        for i in range(NT):
            if i >= 1:
                e.wait(("gtr", i - 1, 15))
            e.dma(ld, t_dh(i), dh_sb[:, :, :],
                  s_dh.transpose([1, 0, 2])[:, :, i * T:(i + 1) * T])
            if i >= 1:
                e.wait(("x1", i - 1, 15))
            for tc in range(4):
                sq = tc // 2
                p0 = i * 256 + (tc % 2) * 128
                e.dma(ld, t_x(i, tc), x_sb[:, tc, :], x[sq, p0:p0 + 128, :])

    @stg.engine("tensor")
    def _(e):
        e.wait_dma(ld, T_IDF)
        for i in range(NT):
            e.wait_dma(ld, t_dh(i))
            for fc in range(16):
                if i * 16 + fc >= 2:
                    pfc = i * 16 + fc - 2
                    e.wait(("x1", pfc // 16, pfc % 16))
                for tc in range(4):
                    e.unordered_op(lambda t, fc=fc, tc=tc: t.transpose(
                        pt[:, fc % 2, tc * 128:(tc + 1) * 128],
                        dh_sb[:, fc, tc * 128:(tc + 1) * 128], idf[:, :]))
                e.unordered_drain_op(lambda t: t.nop(), label=("gtr", i, fc))
            for tc in range(4):
                e.wait(("apply", i, tc))
                for fcg in range(4):
                    gx = (i * 4 + tc) * 4 + fcg
                    if gx >= 2:
                        e.wait(("xev", gx - 2))
                    for j in range(4):
                        fc = fcg * 4 + j
                        e.unordered_op(lambda t, tc=tc, fc=fc, j=j, fcg=fcg, i=i: t.transpose(
                            pxt[:, fcg % 2, j * 128:(j + 1) * 128],
                            xn1tok[:, (i * 4 + tc) % 2, fc * 128:(fc + 1) * 128], idf[:, :]))
                    e.unordered_drain_op(lambda t: t.nop(), label=("xtr", gx))
                    e.wait(("xev", gx))
                    if gx == 0 and i >= 1:
                        e.wait(("lgev", i - 1))
                    for j in range(4):
                        fc = fcg * 4 + j
                        e.unordered_op(lambda t, tc=tc, fc=fc, j=j, gx=gx: t.matmul(
                            pl[:, tc * 128:(tc + 1) * 128], wg_sb[:, fc, :],
                            xf32[:, gx % 2, j, :],
                            start=(tc == 0 and fc == 0), stop=(tc == 3 and fc == 15)),
                            label=("mlg", i) if (tc == 3 and fc == 15) else None)

    @stg.engine("vector")
    def _(e):
        e.op(lambda v: v.memset(epsc[:, :], EPS))
        for i in range(NT):
            for fc in range(16):
                e.wait(("gtr", i, fc))
                e.wait_dma_all(ld, t_x(i, 3))
                if i >= 1 and fc == 0:
                    e.wait_dma_all(st, t_x1(i - 1, 3))
                e.op(lambda v, fc=fc: v.tensor_tensor(
                    x1_sb[:, :, fc * 128:(fc + 1) * 128],
                    pt[:, fc % 2, :].rearrange("p (tc m) -> p tc m", m=128),
                    x_sb[:, :, fc * 128:(fc + 1) * 128], ALU.add), label=("x1", i, fc))
            for tc in range(4):
                tg = i * 4 + tc
                e.op(lambda v, tc=tc: v.reduce_sum(stat[:, tc, 0:1], x1_sb[:, tc, :], AX.X))
                e.wait(("sq", i, tc))
                e.op(lambda v, tc=tc: v.tensor_scalar(stat[:, tc, 2:3], stat[:, tc, 0:1], 1.0 / D, None, ALU.mult))
                e.op(lambda v, tc=tc: v.tensor_tensor(stat[:, tc, 7:8], stat[:, tc, 2:3], stat[:, tc, 2:3], ALU.mult))
                e.op(lambda v, tc=tc: v.tensor_scalar(stat[:, tc, 3:4], stat[:, tc, 1:2], 1.0 / D, None, ALU.mult))
                e.op(lambda v, tc=tc: v.tensor_tensor(stat[:, tc, 3:4], stat[:, tc, 3:4], stat[:, tc, 7:8], ALU.subtract),
                     label=("var", i, tc))
                e.wait(("sqrt", i, tc))
                e.op(lambda v, tc=tc: v.reciprocal(stat[:, tc, 5:6], stat[:, tc, 4:5]))
                e.op(lambda v, tc=tc: v.scalar_tensor_tensor(stat[:, tc, 6:7], stat[:, tc, 2:3], -1.0,
                                                             stat[:, tc, 5:6], ALU.mult, ALU.mult),
                     label=("rstd", i, tc))
                for fcg in range(4):
                    gx = (i * 4 + tc) * 4 + fcg
                    e.wait(("xtr", gx))
                    e.op(lambda v, gx=gx: v.tensor_copy(xf32[:, gx % 2, :, :],
                                                        pxt[:, gx % 2, :].rearrange("p (j m) -> p j m", m=128)))
                    if i >= 1 and tc == 0 and fcg == 0:
                        e.wait_dma_all(st, t_xn1(i - 1, 15))
                    e.op(lambda v, gx=gx, tc=tc, fcg=fcg: v.tensor_copy(
                        xn1f[:, fcg * 4:(fcg + 1) * 4, tc * 128:(tc + 1) * 128],
                        pxt[:, gx % 2, :].rearrange("p (j m) -> p j m", m=128)),
                        label=("xev", gx))
            e.wait(("mlg", i))
            if i >= 1:
                e.wait_dma(st, t_lg(i - 1))
            e.op(lambda v: v.tensor_copy(lg_sb[:, :], pl[:, :]), label=("lgev", i))

    @stg.engine("scalar")
    def _(e):
        for i in range(NT):
            for tc in range(4):
                tg = i * 4 + tc
                e.wait(("x1", i, 15))
                if tg >= 2:
                    e.wait(("xtr", (tg - 2) * 4 + 3))
                e.op(lambda s_, tc=tc: s_.activation(junk[:, :], x1_sb[:, tc, :], AF.Square,
                                                     bias=0.0, scale=1.0, accum_out=stat[:, tc, 1:2]),
                     label=("sq", i, tc))
                e.wait(("var", i, tc))
                e.op(lambda s_, tc=tc: s_.activation(stat[:, tc, 4:5], stat[:, tc, 3:4], AF.Sqrt,
                                                     bias=epsc[:, 0:1], scale=1.0), label=("sqrt", i, tc))
                e.wait(("rstd", i, tc))
                e.op(lambda s_, tc=tc, tg=tg: s_.activation(xn1tok[:, tg % 2, :], x1_sb[:, tc, :], AF.Identity,
                                                            bias=stat[:, tc, 6:7], scale=stat[:, tc, 5:6]),
                     label=("apply", i, tc))

    @stg.engine("gpsimd")
    def _(e):
        for i in range(NT):
            for tc in range(4):
                e.wait(("apply", i, tc))
                e.dma(st, t_x1(i, tc), s_x1[i * 4 + tc, :, :], x1_sb[:, tc, :])
            for fc in range(16):
                e.wait(("xev", (i * 4 + 3) * 4 + 3))
                e.dma(st, t_xn1(i, fc), s_xn1[fc, :, i * T:(i + 1) * T], xn1f[:, fc, :])
            e.wait(("lgev", i))
            e.dma(st, t_lg(i), s_lg[:, i * T:(i + 1) * T], lg_sb[:, :])
        e.wait_dma_all(st, t_lg(NT - 1))

    stg.run()
    return ["s_x1", "s_xn1", "s_lg"]



def build_stageH1(nc, io, es, gs=None):
    s_xn1 = io.d("s_xn1", [16, 128, NT * T], BF16, "scr_in")
    WqkvT = io.d("WqkvT", [D, 3072], F32, "in")
    selS = io.d("selS", [128, 16, 16], BF16, "in")
    selB = io.d("selB", [16, 16, 128], BF16, "in")
    s_attn = io.d("s_attn", [16, 128, NT * T], BF16, "scr_out")

    E = es.enter_context
    wq_sb = E(nc.sbuf_tensor("H1w", [128, 16, 24, 128], BF16))
    wstripe = E(nc.sbuf_tensor("H1stripe", [128, 2, 3072], F32))
    selS_sb = E(nc.sbuf_tensor("H1selS", [128, 16, 16], BF16))
    selB_sb = E(nc.sbuf_tensor("H1selB", [16, 16, 128], BF16))
    xn1_sb = E(nc.sbuf_tensor("H1xn", [128, 16, T], BF16))
    q_sb = E(nc.sbuf_tensor("H1q", [128, 16, T], BF16))
    k_sb = E(nc.sbuf_tensor("H1k", [128, 4, T], BF16))
    v_sb = E(nc.sbuf_tensor("H1v", [128, 4, T], BF16))
    pr_sb = E(nc.sbuf_tensor("H1pr", [128, 2, T], BF16))
    s_sb = E(nc.sbuf_tensor("H1s", [16, T], F32))
    sm_sb = E(nc.sbuf_tensor("H1sm", [16, 6, H], F32))
    wts_sb = E(nc.sbuf_tensor("H1wts", [16, T], BF16))
    wb_sb = E(nc.sbuf_tensor("H1wb", [128, 2, T], BF16))
    t0_sb = E(nc.sbuf_tensor("H1t0", [128, 2, T], BF16))
    attn_sb = E(nc.sbuf_tensor("H1attn", [128, 16, T], BF16))
    pq = E(nc.psum_tensor("H1pq", [128, 2, T], F32))
    ps = E(nc.psum_tensor("H1ps", [16, T], F32))
    pw = E(nc.psum_tensor("H1pw", [128, 2, T], F32))
    stg = Stage(nc, es, "H1", gs)
    gs = stg.gs
    ld = gs.dq("q_sync")
    st = gs.dq("q_gpsimd")

    def t_stripe(j):
        return ("stripe", j)
    T_SELS = ("sels",)
    T_SELB = ("selb",)
    def t_xn(i):
        return ("xn", i)
    def t_attn(i, fc):
        return ("attn", i, fc)


    @stg.engine("sync")
    def _(e):
        for dc in range(16):
            if dc >= 2:
                e.wait(("wcast", dc - 2))
            e.dma(ld, t_stripe(dc), wstripe[:, dc % 2, :], WqkvT[dc * 128:(dc + 1) * 128, :])
        e.dma(ld, T_SELS, selS_sb[:, :, :], selS)
        e.dma(ld, T_SELB, selB_sb[:, :, :], selB)
        for i in range(NT):
            if i >= 1:
                e.wait(("mq", i - 1, 23))
            e.dma(ld, t_xn(i), xn1_sb[:, :, :],
                  s_xn1.transpose([1, 0, 2])[:, :, i * T:(i + 1) * T])

    @stg.engine("tensor")
    def _(e):
        e.wait_dma(ld, T_SELS)
        e.wait_dma(ld, T_SELB)
        e.wait(("wcast", 15))
        for i in range(NT):
            e.wait_dma(ld, t_xn(i))
            for j in range(24):
                gj = i * 24 + j
                if gj >= 2:
                    e.wait(("qev", gj - 2))
                for dc in range(16):
                    e.unordered_op(lambda t, dc=dc, j=j: t.matmul(pq[:, j % 2, :], wq_sb[:, dc, j, :],
                                                                  xn1_sb[:, dc, :], start=(dc == 0), stop=(dc == 15)),
                                   label=("mq", i, j) if dc == 15 else None)
            if i >= 1:
                e.wait(("sev", i - 1))
            for h in range(16):
                e.wait(("pr", i, h))
                e.unordered_op(lambda t, h=h: t.matmul(ps[:, 0:H], selS_sb[:, h, :], pr_sb[:, h % 2, 0:H],
                                                       start=(h == 0), stop=False))
                e.unordered_op(lambda t, h=h: t.matmul(ps[:, H:T], selS_sb[:, h, :], pr_sb[:, h % 2, H:T],
                                                       start=False, stop=(h == 15)),
                               label=("msh", i, h))
            for h in range(16):
                e.wait(("wts", i))
                if i * 16 + h >= 2:
                    ph_ = i * 16 + h - 2
                    e.wait(("ao", ph_ // 16, ph_ % 16))
                e.unordered_op(lambda t, h=h: t.matmul(pw[:, h % 2, :], selB_sb[:, h, :], wts_sb[:, :],
                                                       start=True, stop=True),
                               label=("mw", i, h))

    @stg.engine("vector")
    def _(e):
        for dc in range(16):
            e.wait_dma(ld, t_stripe(dc))
            e.op(lambda v, dc=dc: v.tensor_copy(wq_sb[:, dc, :, :],
                                                wstripe[:, dc % 2, :].rearrange("p (j m) -> p j m", m=128)),
                 label=("wcast", dc))
        for i in range(NT):
            for j in range(24):
                gj = i * 24 + j
                e.wait(("mq", i, j))
                dst = (lambda j: q_sb[:, j, :] if j < 16 else (k_sb[:, j - 16, :] if j < 20 else v_sb[:, j - 20, :]))
                e.op(lambda v, j=j: v.tensor_copy(dst(j), pq[:, j % 2, :]), label=("qev", gj))
            for h in range(16):
                gh = i * 16 + h
                if gh >= 2:
                    e.wait(("msh", (gh - 2) // 16, (gh - 2) % 16))
                e.op(lambda v, h=h: v.tensor_tensor(pr_sb[:, h % 2, 0:H], q_sb[:, h, H:T],
                                                    k_sb[:, h // 4, 0:H], ALU.mult))
                e.op(lambda v, h=h: v.tensor_tensor(pr_sb[:, h % 2, H:T], q_sb[:, h, H:T],
                                                    k_sb[:, h // 4, H:T], ALU.mult), label=("pr", i, h))
            e.wait(("msh", i, 15))
            e.op(lambda v: v.tensor_copy(s_sb[:, :], ps[:, :]), label=("sev", i))
            e.op(lambda v: v.tensor_tensor(sm_sb[:, 0, :], s_sb[:, 0:H], s_sb[:, H:T], ALU.max))
            e.op(lambda v: v.tensor_tensor(sm_sb[:, 1, :], s_sb[:, 0:H], sm_sb[:, 0, :], ALU.subtract),
                 label=("d0", i))
            e.op(lambda v: v.tensor_tensor(sm_sb[:, 2, :], s_sb[:, H:T], sm_sb[:, 0, :], ALU.subtract),
                 label=("d1", i))
            e.wait(("e1", i))
            e.op(lambda v: v.tensor_tensor(sm_sb[:, 0, :], sm_sb[:, 4, :], sm_sb[:, 5, :], ALU.add))
            e.op(lambda v: v.reciprocal(sm_sb[:, 3, :], sm_sb[:, 0, :]))
            e.op(lambda v: v.tensor_tensor(wts_sb[:, 0:H], sm_sb[:, 4, :], sm_sb[:, 3, :], ALU.mult))
            e.op(lambda v: v.tensor_tensor(wts_sb[:, H:T], sm_sb[:, 5, :], sm_sb[:, 3, :], ALU.mult),
                 label=("wts", i))
            for h in range(16):
                gh = i * 16 + h
                e.wait(("mw", i, h))
                if i >= 1 and h == 0:
                    e.wait_dma_all(st, t_attn(i - 1, 15))
                e.op(lambda v, h=h: v.tensor_copy(wb_sb[:, h % 2, :], pw[:, h % 2, :]))
                e.op(lambda v, h=h: v.tensor_tensor(t0_sb[:, h % 2, 0:H], wb_sb[:, h % 2, 0:H],
                                                    v_sb[:, h // 4, 0:H], ALU.mult))
                e.op(lambda v, h=h: v.tensor_tensor(t0_sb[:, h % 2, H:T], wb_sb[:, h % 2, H:T],
                                                    v_sb[:, h // 4, H:T], ALU.mult))
                e.op(lambda v, h=h: v.tensor_tensor(attn_sb[:, h, H:T], t0_sb[:, h % 2, 0:H],
                                                    t0_sb[:, h % 2, H:T], ALU.add))
                e.op(lambda v, h=h: v.tensor_copy(attn_sb[:, h, 0:H], v_sb[:, h // 4, 0:H]),
                     label=("ao", i, h))

    @stg.engine("scalar")
    def _(e):
        for i in range(NT):
            e.wait(("d0", i))
            e.op(lambda s_: s_.activation(sm_sb[:, 4, :], sm_sb[:, 1, :], AF.Exp, bias=0.0, scale=1.0))
            e.wait(("d1", i))
            e.op(lambda s_: s_.activation(sm_sb[:, 5, :], sm_sb[:, 2, :], AF.Exp, bias=0.0, scale=1.0),
                 label=("e1", i))

    @stg.engine("gpsimd")
    def _(e):
        for i in range(NT):
            for fc in range(16):
                e.wait(("ao", i, fc))
                e.dma(st, t_attn(i, fc), s_attn[fc, :, i * T:(i + 1) * T], attn_sb[:, fc, :])
        e.wait_dma_all(st, t_attn(NT - 1, 15))

    stg.run()
    return ["s_attn"]



def build_stageH2(nc, io, es, gs=None):
    s_attn = io.d("s_attn", [16, 128, NT * T], BF16, "scr_in")
    WprojT = io.d("WprojT", [D, D], F32, "in")
    s_r1 = io.d("s_r1", [16, 128, NT * T], F32, "scr_out")

    E = es.enter_context
    wp_sb = E(nc.sbuf_tensor("H2w", [128, 16, 16, 128], BF16))
    wstripe = E(nc.sbuf_tensor("H2stripe", [128, 2, D], F32))
    at_sb = E(nc.sbuf_tensor("H2at", [128, 16, T], BF16))
    r1_sb = E(nc.sbuf_tensor("H2r1", [128, 4, T], F32))
    po = E(nc.psum_tensor("H2po", [128, 4, T], F32))
    stg = Stage(nc, es, "H2", gs)
    gs = stg.gs
    ld = gs.dq("q_sync")
    st = gs.dq("q_gpsimd")

    def t_stripe(j):
        return ("stripe", j)
    def t_at(i):
        return ("at", i)
    def t_r1(i, oc):
        return ("r1", i, oc)


    @stg.engine("sync")
    def _(e):
        for dc in range(16):
            if dc >= 2:
                e.wait(("wcast", dc - 2))
            e.dma(ld, t_stripe(dc), wstripe[:, dc % 2, :], WprojT[dc * 128:(dc + 1) * 128, :])
        for i in range(NT):
            if i >= 1:
                e.wait(("mo", i - 1, 15))
            e.dma(ld, t_at(i), at_sb[:, :, :],
                  s_attn.transpose([1, 0, 2])[:, :, i * T:(i + 1) * T])

    @stg.engine("tensor")
    def _(e):
        e.wait(("wcast", 15))
        for i in range(NT):
            e.wait_dma(ld, t_at(i))
            for oc in range(16):
                go = i * 16 + oc
                if go >= 4:
                    e.wait(("r1ev", go - 4))
                for ec in range(16):
                    e.unordered_op(lambda t, ec=ec, oc=oc, go=go: t.matmul(po[:, go % 4, :], wp_sb[:, ec, oc, :],
                                                                           at_sb[:, ec, :], start=(ec == 0), stop=(ec == 15)),
                                   label=("mo", i, oc) if ec == 15 else None)

    @stg.engine("vector")
    def _(e):
        for dc in range(16):
            e.wait_dma(ld, t_stripe(dc))
            e.op(lambda v, dc=dc: v.tensor_copy(wp_sb[:, dc, :, :],
                                                wstripe[:, dc % 2, :].rearrange("p (o m) -> p o m", m=128)),
                 label=("wcast", dc))
        for i in range(NT):
            for oc in range(16):
                go = i * 16 + oc
                e.wait(("mo", i, oc))
                if go >= 4:
                    e.wait_dma(st, t_r1((go - 4) // 16, (go - 4) % 16))
                e.op(lambda v, oc=oc, go=go: v.tensor_copy(r1_sb[:, go % 4, :], po[:, go % 4, :]),
                     label=("r1ev", go))

    @stg.engine("gpsimd")
    def _(e):
        for i in range(NT):
            for oc in range(16):
                go = i * 16 + oc
                e.wait(("r1ev", go))
                e.dma(st, t_r1(i, oc), s_r1[oc, :, i * T:(i + 1) * T], r1_sb[:, go % 4, :])
        e.wait_dma_all(st, t_r1(NT - 1, 15))

    stg.run()
    return ["s_r1"]



def build_stageI(nc, io, es, gs=None):
    s_xn1 = io.d("s_xn1", [16, 128, NT * T], BF16, "scr_in")
    s_lg = io.d("s_lg", [8, NT * T], F32, "scr_in")
    s_r1 = io.d("s_r1", [16, 128, NT * T], F32, "scr_in")
    s_x1 = io.d("s_x1", [NT * 4, 128, D], F32, "scr_in")
    s_web_ap = io.d("s_web", [8, 16, 16, 128, 128], BF16, "scr_in")
    b_moe = io.d("b_moe", [8, D], F32, "in")
    selM = io.d("selM", [8, 8, 128], BF16, "in")
    identf = io.d("identf", [128, 128], F32, "in")
    identb = io.d("identb", [128, 128], BF16, "in")
    out = io.d("out", [2, PPC, D], F32, "out")

    E = es.enter_context
    idf = E(nc.sbuf_tensor("Iidf", [128, 128], F32))
    idb = E(nc.sbuf_tensor("Iidb", [128, 128], BF16))
    selM_sb = E(nc.sbuf_tensor("IselM", [8, 8, 128], BF16))
    bm_f = E(nc.sbuf_tensor("Ibmf", [8, D], F32))
    bm_sb = E(nc.sbuf_tensor("Ibm", [8, 16, 128], BF16))
    we_sb = E(nc.sbuf_tensor("Iwe", [128, 3, 16, 128], BF16))
    xn1_sb = E(nc.sbuf_tensor("Ixn", [128, 16, T], BF16))
    lg_sb = E(nc.sbuf_tensor("Ilg", [8, T], F32))
    lt_sb = E(nc.sbuf_tensor("Ilt", [128, 4, 8], F32))
    et_sb = E(nc.sbuf_tensor("Iet", [128, 4, 8], F32))
    rm_sb = E(nc.sbuf_tensor("Irm", [128, 4, 4], F32))
    m_sb = E(nc.sbuf_tensor("Im", [128, 4, 8], BF16))
    mrows = E(nc.sbuf_tensor("Imrows", [8, T], BF16))
    mb_sb = E(nc.sbuf_tensor("Imb", [128, 2, T], BF16))
    z_sb = E(nc.sbuf_tensor("Iz", [128, 8, 16, T], BF16))
    r1_sb = E(nc.sbuf_tensor("Ir1", [128, 2, T], F32))
    sum1_sb = E(nc.sbuf_tensor("Isum1", [128, 2, T], F32))
    x1p_sb = E(nc.sbuf_tensor("Ix1p", [128, 2, 4, 128], F32))
    x2p_sb = E(nc.sbuf_tensor("Ix2p", [128, 2, 4, 128], F32))
    pltr = E(nc.psum_tensor("Ipltr", [128, 32], F32))
    pmr = E(nc.psum_tensor("Ipmr", [8, T], BF16))
    pb = E(nc.psum_tensor("Ipb", [128, 2, T], F32))
    pm = E(nc.psum_tensor("Ipm", [128, 2, T], F32))
    pxt = E(nc.psum_tensor("Ipxt", [128, 2, T], F32))
    stg = Stage(nc, es, "I", gs)
    gs = stg.gs
    ld = gs.dq("q_sync")
    st = gs.dq("q_gpsimd")
    so = gs.dq("q_scalar")
    wq = gs.dq("q_gpsimd")

    T_IDF = ("idf",)
    T_IDB = ("idb",)
    T_SELM = ("selm",)
    T_BM = ("bm",)
    def t_xn(i):
        return ("xn", i)
    def t_lg(i):
        return ("lg", i)
    def t_r1(i, oc):
        return ("r1", i, oc)
    def t_x1p(i, oc):
        return ("x1p", i, oc)
    def t_out(i, oc, tc):
        return ("out", i, oc, tc)
    def t_we(i, oc, e8):
        return ("we", i, oc, e8)


    @stg.engine("sync")
    def _(e):
        e.dma(ld, T_IDF, idf[:, :], identf)
        e.dma(ld, T_IDB, idb[:, :], identb)
        e.dma(ld, T_SELM, selM_sb[:, :, :], selM)
        e.dma(ld, T_BM, bm_f[:, :], b_moe)
        for i in range(NT):
            if i >= 1:
                e.wait(("mmb", (i - 1) * 128 + 127))
            e.dma(ld, t_xn(i), xn1_sb[:, :, :],
                  s_xn1.transpose([1, 0, 2])[:, :, i * T:(i + 1) * T])
            if i >= 1:
                e.wait(("lgread", i - 1))
            e.dma(ld, t_lg(i), lg_sb[:, :], s_lg[:, i * T:(i + 1) * T])
            for oc in range(16):
                go = i * 16 + oc
                if go >= 2:
                    e.wait(("s1", go - 2))
                e.dma(ld, t_r1(i, oc), r1_sb[:, go % 2, :], s_r1[oc, :, i * T:(i + 1) * T])
                if go >= 2:
                    e.wait(("x2", go - 2))
                e.dma(ld, t_x1p(i, oc), x1p_sb[:, go % 2, :, :],
                      s_x1[i * 4:(i + 1) * 4, :, oc * 128:(oc + 1) * 128].transpose([1, 0, 2]))

    @stg.engine("gpsimd")
    def _(e):
        for i in range(NT):
            for oc in range(16):
                for e8 in range(8):
                    bk = (i * 16 + oc) * 8 + e8
                    if bk >= 3:
                        e.wait(("mmb", bk - 3))
                    e.dma(wq, t_we(i, oc, e8), we_sb[:, bk % 3, :, :],
                          s_web_ap[e8, oc].transpose([1, 0, 2]))
        e.wait(("done",))

    @stg.engine("tensor")
    def _(e):
        e.wait_dma(ld, T_IDF)
        e.wait_dma(ld, T_IDB)
        e.wait_dma(ld, T_SELM)
        e.wait(("bmcast",))
        def emit_str(j1):
            e.wait(("s1", j1))
            if j1 >= 2:
                e.wait(("x2", j1 - 2))
            for tc in range(4):
                e.unordered_op(lambda t, tc=tc, j1=j1: t.transpose(
                    pxt[:, j1 % 2, tc * 128:(tc + 1) * 128], sum1_sb[:, j1 % 2, tc * 128:(tc + 1) * 128],
                    idf[:, :]))
            e.unordered_drain_op(lambda t: t.nop(), label=("str", j1))
        for i in range(NT):
            if i > 0:
                emit_str(i * 16 - 1)
            e.wait_dma(ld, t_lg(i))
            if i >= 1:
                e.wait(("lgread", i - 1))
            for tc in range(4):
                e.unordered_op(lambda t, tc=tc: t.transpose(pltr[:, tc * 8:(tc + 1) * 8],
                                                            lg_sb[0:8, tc * 128:(tc + 1) * 128],
                                                            idf[0:8, 0:8]))
            e.unordered_drain_op(lambda t: t.nop(), label=("ltr", i))
            if i >= 1:
                e.wait(("mrev", i - 1))
            for tc in range(4):
                e.wait(("mask", i, tc))
                e.unordered_op(lambda t, tc=tc: t.transpose(pmr[:, tc * 128:(tc + 1) * 128],
                                                            m_sb[:, tc, :], idb[:, :]))
            e.unordered_drain_op(lambda t: t.nop(), label=("mtr", i))
            e.wait_dma(ld, t_xn(i))
            for e8 in range(8):
                e.wait(("mrev", i))
                if i * 8 + e8 >= 2:
                    pz = i * 8 + e8 - 2
                    e.wait(("mbev", pz // 8, pz % 8))
                e.unordered_op(lambda t, e8=e8: t.matmul(pb[:, e8 % 2, :], selM_sb[:, e8, :], mrows[:, :],
                                                         start=True, stop=True),
                               label=("mbb", i, e8))
            for oc in range(16):
                go = i * 16 + oc
                if go >= 2:
                    e.wait(("s1", go - 2))
                e.wait(("z", i, 7))
                e.unordered_op(lambda t, oc=oc: t.matmul(pm[:, oc % 2, :], bm_sb[:, oc, :], mrows[:, :],
                                                         start=True, stop=False))
                for e8 in range(8):
                    bk = go * 8 + e8
                    e.wait_dma(wq, t_we(i, oc, e8))
                    for kc in range(16):
                        e.unordered_op(lambda t, kc=kc, e8=e8, oc=oc, bk=bk: t.matmul(
                            pm[:, oc % 2, :], we_sb[:, bk % 3, kc, :], z_sb[:, e8, kc, :],
                            start=False, stop=(e8 == 7 and kc == 15)),
                            label=("mmb", bk) if kc == 15 else None)
                if oc >= 1:
                    emit_str(go - 1)

        emit_str(NT * 16 - 1)

    @stg.engine("vector")
    def _(e):
        e.wait_dma(ld, T_BM)
        e.op(lambda v: v.tensor_copy(bm_sb[:, :, :], bm_f.rearrange("p (o m) -> p o m", m=128)),
             label=("bmcast",))
        for i in range(NT):
            e.wait(("ltr", i))
            e.op(lambda v: v.tensor_copy(lt_sb[:, :, :], pltr[:, :].rearrange("p (tc j) -> p tc j", j=8)),
                 label=("lgread", i))
            for tc in range(4):
                e.op(lambda v, tc=tc: v.reduce_max(rm_sb[:, tc, 0:1], lt_sb[:, tc, :], AX.X))
                e.op(lambda v, tc=tc: v.tensor_scalar(rm_sb[:, tc, 1:2], rm_sb[:, tc, 0:1], -1.0, None, ALU.mult),
                     label=("nm", i, tc))
                e.wait(("et", i, tc))
                e.op(lambda v, tc=tc: v.reduce_sum(rm_sb[:, tc, 2:3], et_sb[:, tc, :], AX.X))
                e.op(lambda v, tc=tc: v.reciprocal(rm_sb[:, tc, 3:4], rm_sb[:, tc, 2:3]))
                e.op(lambda v, tc=tc: v.tensor_scalar(m_sb[:, tc, :], et_sb[:, tc, :], 1.0, rm_sb[:, tc, 3:4],
                                                      ALU.is_ge, ALU.mult), label=("mask", i, tc))
            e.wait(("mtr", i))
            e.op(lambda v: v.tensor_copy(mrows[:, :], pmr[:, :]), label=("mrev", i))
            for e8 in range(8):
                e.wait(("mbb", i, e8))
                e.op(lambda v, e8=e8: v.tensor_copy(mb_sb[:, e8 % 2, :], pb[:, e8 % 2, :]),
                     label=("mbev", i, e8))
                if i >= 1 and e8 == 0:
                    e.wait(("mmb", (i - 1) * 128 + 127))
                for kc in range(16):
                    ins = e.op(lambda v, e8=e8, kc=kc: v.tensor_tensor(z_sb[:, e8, kc, :], xn1_sb[:, kc, :],
                                                                       mb_sb[:, e8 % 2, :], ALU.mult),
                               label=("z", i, e8) if kc == 15 else None)
            for oc in range(16):
                go = i * 16 + oc
                e.wait(("mmb", go * 8 + 7))
                e.wait_dma(ld, t_r1(i, oc))
                if go >= 2:
                    e.wait(("str", go - 2))
                e.op(lambda v, go=go: v.tensor_tensor(sum1_sb[:, go % 2, :], pm[:, go % 2, :],
                                                      r1_sb[:, go % 2, :], ALU.add), label=("s1", go))
                e.wait(("str", go))
                e.wait_dma(ld, t_x1p(i, oc))
                if go >= 2:
                    for tc in range(4):
                        e.wait_dma(so, t_out((go - 2) // 16, (go - 2) % 16, tc))
                e.op(lambda v, go=go: v.tensor_tensor(x2p_sb[:, go % 2, :, :],
                                                      pxt[:, go % 2, :].rearrange("p (tc m) -> p tc m", m=128),
                                                      x1p_sb[:, go % 2, :, :], ALU.add), label=("x2", go))

    @stg.engine("scalar")
    def _(e):
        for i in range(NT):
            for tc in range(4):
                e.wait(("nm", i, tc))
                e.op(lambda s_, tc=tc: s_.activation(et_sb[:, tc, :], lt_sb[:, tc, :], AF.Exp,
                                                     bias=rm_sb[:, tc, 1:2], scale=1.0), label=("et", i, tc))
            for oc in range(16):
                go = i * 16 + oc
                e.wait(("x2", go))
                for tc in range(4):
                    sq = tc // 2
                    p0 = i * 256 + (tc % 2) * 128
                    e.dma(so, t_out(i, oc, tc), out[sq, p0:p0 + 128, oc * 128:(oc + 1) * 128],
                          x2p_sb[:, go % 2, tc, :])
        e.wait_dma_all(so, t_out(NT - 1, 15, 3))
        e.op(lambda s_: s_.activation(et_sb[:, 0, :], et_sb[:, 0, :], AF.Identity, bias=0.0, scale=1.0),
             label=("done",))

    stg.run()
    return ["out"]



def _shard(x, c):
    return np.ascontiguousarray(x[:, c * PPC:(c + 1) * PPC, :])


def build_AG(nc, io, ges):
    totals = io.d("totals", [2, D], F32, "scr_out")
    ag = io.d("ag", [16, D], F32, "scr_out")
    ccsem = ges.enter_context(nc.semaphore("ccsem"))
    with nc.Block() as blk:
        @blk.gpsimd
        def _(g):
            g.collective_compute(
                "AllGather", ALU.bypass,
                replica_groups=[list(range(NCORES))],
                ins=[totals.opt()], outs=[ag.opt()],
            ).then_inc(ccsem, 1)
            g.wait_ge(ccsem, 1)


def _in_maps(inputs, w, c, shards):
    maps = []
    for ci in range(NCORES):
        maps.append({
            "x": shards[ci], "pmask": _prefix_mask(ci),
            "WkT": w["WkT"], "WvT": w["WvT"], "bk": w["bk"], "bv": w["bv"],
            "WqT": w["WqT"], "WoT": w["WoT"], "bq": w["bq"], "bo": w["bo"],
            "SU_ug": w["SU_ug"], "SU_dp": w["SU_dp"],
            "Wscale_ug": w["Wscale_ug"], "Wscale_dp": w["Wscale_dp"],
            "WupT": w["WupT"], "WgateT": w["WgateT"], "WdownT": w["WdownT"],
            "wg": w["wg"], "WqkvT": w["WqkvT"], "WprojT": w["WprojT"],
            "We_blk": w["We_blk"], "b_moe": w["b_moe"],
            "identb": c["identb"], "identf": c["identf"],
            "H128a": c["H128a"], "H128s": c["H128s"], "Hdd": c["Hdd"],
            "selS": c["selS"], "selB": c["selB"], "selM": c["selM"],
        })
    return maps


def kernel(**inputs):
    inputs = {k: np.asarray(v) for k, v in inputs.items()}
    w = _host_weights(inputs)
    c = _host_constants()
    x = inputs["L0_attn_x"]
    shards = [_shard(x, ci) for ci in range(NCORES)]

    nc = bass.Bass("TRN2", target_bir_lowering=False)
    io = IO(nc, fused=True)
    with contextlib.ExitStack() as ges:
        gs = GlobalSync(nc, ges)
        for builder in [build_stageA]:
            with contextlib.ExitStack() as es:
                builder(nc, io, es, gs)
        build_AG(nc, io, ges)
        for builder in [build_stageB1, build_stageB2, build_stageD, build_stageE,
                        build_stageF, build_stageG, build_stageH1, build_stageH2,
                        build_stageI]:
            with contextlib.ExitStack() as es:
                builder(nc, io, es, gs)
        res = run_bass_kernel_spmd(nc, _in_maps(inputs, w, c, shards),
                                   core_ids=list(range(NCORES)))
    return np.ascontiguousarray(
        np.concatenate([np.asarray(res.results[ci]["out"], np.float32)
                        for ci in range(NCORES)], axis=1))


def kernel_unfused(**inputs):
    inputs = {k: np.asarray(v) for k, v in inputs.items()}
    w = _host_weights(inputs)
    c = _host_constants()
    x = inputs["L0_attn_x"]
    shards = [_shard(x, ci) for ci in range(NCORES)]

    def launch(builder, per_core_maps):
        nc = bass.Bass("TRN2", target_bir_lowering=False)
        io = IO(nc, fused=False)
        with contextlib.ExitStack() as es:
            builder(nc, io, es)
            res = run_bass_kernel_spmd(nc, per_core_maps, core_ids=list(range(NCORES)))
        return res.results

    rA = launch(build_stageA, [
        {"x": shards[ci], "WkT": w["WkT"], "WvT": w["WvT"], "bk": w["bk"], "bv": w["bv"],
         "identb": c["identb"]} for ci in range(NCORES)])
    ag = np.concatenate([np.asarray(rA[ci]["totals"]) for ci in range(NCORES)], axis=0)

    rB1 = launch(build_stageB1, [
        {"s_xn": rA[ci]["s_xn"], "s_kv": rA[ci]["s_kv"], "ag": ag, "pmask": _prefix_mask(ci),
         "WqT": w["WqT"], "WoT": w["WoT"], "bq": w["bq"], "bo": w["bo"], "identf": c["identf"]}
        for ci in range(NCORES)])

    rB2 = launch(build_stageB2, [
        {"s_xn": rA[ci]["s_xn"], "SU_ug": w["SU_ug"], "H128a": c["H128a"]}
        for ci in range(NCORES)])

    rD = launch(build_stageD, [
        {"s_y": rB2[ci]["s_y"], "WupT": w["WupT"], "WgateT": w["WgateT"],
         "Wscale_ug": w["Wscale_ug"], "SU_dp": w["SU_dp"], "H128s": c["H128s"],
         "We_blk": w["We_blk"]}
        for ci in range(NCORES)])

    rE = launch(build_stageE, [
        {"s_wt": rD[ci]["s_wt"], "Hdd": c["Hdd"]} for ci in range(NCORES)])

    rF = launch(build_stageF, [
        {"s_yp": rE[ci]["s_yp"], "s_ratt": rB1[ci]["s_ratt"], "WdownT": w["WdownT"],
         "Wscale_dp": w["Wscale_dp"]} for ci in range(NCORES)])

    rG = launch(build_stageG, [
        {"x": shards[ci], "s_dh": rF[ci]["s_dh"], "wg": w["wg"], "identf": c["identf"]}
        for ci in range(NCORES)])

    rH1 = launch(build_stageH1, [
        {"s_xn1": rG[ci]["s_xn1"], "WqkvT": w["WqkvT"], "selS": c["selS"], "selB": c["selB"]}
        for ci in range(NCORES)])

    rH2 = launch(build_stageH2, [
        {"s_attn": rH1[ci]["s_attn"], "WprojT": w["WprojT"]} for ci in range(NCORES)])

    rI = launch(build_stageI, [
        {"s_xn1": rG[ci]["s_xn1"], "s_lg": rG[ci]["s_lg"], "s_r1": rH2[ci]["s_r1"],
         "s_x1": rG[ci]["s_x1"], "s_web": rD[ci]["s_web"], "b_moe": w["b_moe"],
         "selM": c["selM"], "identf": c["identf"], "identb": c["identb"]}
        for ci in range(NCORES)])

    return np.ascontiguousarray(
        np.concatenate([np.asarray(rI[ci]["out"], np.float32) for ci in range(NCORES)], axis=1))


# version 10
# speedup vs baseline: 1.0258x; 1.0132x over previous
import contextlib
import numpy as np
import ml_dtypes
import concourse.bass as bass
import concourse.mybir as mybir
from concourse.bass_utils import run_bass_kernel_spmd

F32 = mybir.dt.float32
BF16 = mybir.dt.bfloat16
AF = mybir.ActivationFunctionType
ALU = mybir.AluOpType
AX = mybir.AxisListType
BF = ml_dtypes.bfloat16

NCORES = 8
PPC = 2048
NT = 8
T = 512
H = 256
D = 2048
INTER = 8192
SCALE0 = float(2048 ** -0.5) * 0.0 + float(128 ** -0.5)
EPS = 1e-5


def _hadamard(n):
    Hm = np.array([[1.0]], dtype=np.float64)
    while Hm.shape[0] < n:
        Hm = np.block([[Hm, Hm], [Hm, -Hm]])
    return Hm.astype(np.float32)


def _host_constants():
    H128 = _hadamard(128)
    H64 = _hadamard(64)
    s2 = np.float32((1.0 / np.sqrt(np.float32(2048.0))) / 64.0)
    s8 = np.float32((1.0 / np.sqrt(np.float32(8192.0))) / 64.0)
    c = {}
    c["identb"] = np.eye(128, dtype=np.float32).astype(BF)
    c["identf"] = np.eye(128, dtype=np.float32)
    c["H128a"] = (H128 * s2).astype(BF)
    c["H128s"] = (H128 * s8).astype(BF)
    hdd = np.zeros((128, 128), np.float32)
    hdd[:64, :64] = H64
    hdd[64:, 64:] = H64
    c["Hdd"] = hdd.astype(BF)
    selS = np.zeros((128, 16, 16), np.float32)
    for h in range(16):
        selS[:, h, h] = SCALE0
    c["selS"] = selS.astype(BF)
    selB = np.zeros((16, 16, 128), np.float32)
    for h in range(16):
        selB[h, h, :] = 1.0
    c["selB"] = selB.astype(BF)
    selM = np.zeros((8, 8, 128), np.float32)
    for e in range(8):
        selM[e, e, :] = 1.0
    c["selM"] = selM.astype(BF)
    return c


def _host_weights(inp):
    w = {}
    w["WqT"] = np.ascontiguousarray(inp["L0_attn_Wq"].T)
    w["WkT"] = np.ascontiguousarray(inp["L0_attn_Wk"].T)
    w["WvT"] = np.ascontiguousarray(inp["L0_attn_Wv"].T)
    w["WoT"] = np.ascontiguousarray(inp["L0_attn_Wo"].T)
    w["bq"] = np.ascontiguousarray(inp["L0_attn_bq"])
    w["bk"] = np.ascontiguousarray(inp["L0_attn_bk"])
    w["bv"] = np.ascontiguousarray(inp["L0_attn_bv"])
    w["bo"] = np.ascontiguousarray(inp["L0_attn_bo"])
    w["WupT"] = np.ascontiguousarray(inp["L0_ffn_W_up"].T)
    w["WgateT"] = np.ascontiguousarray(inp["L0_ffn_W_gate"].T)
    w["WdownT"] = np.ascontiguousarray(inp["L0_ffn_W_down"].T)
    w["SU_ug"] = np.ascontiguousarray(inp["L0_ffn_SU_ug"])
    w["SU_dp"] = np.ascontiguousarray(inp["L0_ffn_SU_dp"])
    w["Wscale_ug"] = np.ascontiguousarray(inp["L0_ffn_Wscale_ug"])
    w["Wscale_dp"] = np.ascontiguousarray(inp["L0_ffn_Wscale_dp"])
    Wqkv = inp["L1_attn_weight_qkv"]
    order = []
    for h in range(16):
        g, kk = h // 4, h % 4
        order.extend(range(g * 768 + kk * 128, g * 768 + kk * 128 + 128))
    for g in range(4):
        order.extend(range(g * 768 + 512, g * 768 + 640))
    for g in range(4):
        order.extend(range(g * 768 + 640, g * 768 + 768))
    w["WqkvT"] = np.ascontiguousarray(Wqkv[np.array(order), :].T)
    w["WprojT"] = np.ascontiguousarray(inp["L1_attn_weight_proj"].T)
    w["wg"] = np.ascontiguousarray(inp["L1_ffn_wg"])
    We = inp["L1_ffn_W"]
    WeT = We.transpose(0, 2, 1)
    w["We_blk"] = np.ascontiguousarray(
        WeT.reshape(8, 16, 128, 16, 128).transpose(0, 3, 1, 2, 4))
    w["b_moe"] = np.ascontiguousarray(inp["L1_ffn_b"])
    return w


def _prefix_mask(core):
    m = np.zeros((16, 2), np.float32)
    for cc in range(core):
        m[2 * cc, 0] = 1.0
        m[2 * cc + 1, 1] = 1.0
    return m



class IO:

    def __init__(self, nc, fused):
        self.nc = nc
        self.fused = fused
        self.cache = {}

    def d(self, name, shape, dtype, role):
        if name in self.cache:
            return self.cache[name]
        if role.startswith("scr") and self.fused:
            t = self.nc.dram_tensor(name, shape, dtype)
        else:
            kind = {"in": "ExternalInput", "out": "ExternalOutput",
                    "scr_in": "ExternalInput", "scr_out": "ExternalOutput"}[role]
            t = self.nc.dram_tensor(name, shape, dtype, kind=kind)
        ap = t.ap()
        self.cache[name] = ap
        return ap


NQ = 8


def qtok(j, n=NQ):
    return j % n, 16 * (j // n + 1)


class DQ:

    def __init__(self, nc, es, name, n=NQ):
        self.sems = [es.enter_context(nc.semaphore(name + str(i))) for i in range(n)]
        self.n = n
        self.keymap = {}
        self.count = 0

    def alloc(self, key):
        if key in self.keymap:
            raise KeyError(f"duplicate DMA key {key}")
        self.keymap[key] = self.count
        self.count += 1
        return self.keymap[key]

    def j(self, key):
        return self.keymap[key]

    def inc(self, ins, j):
        ins.then_inc(self.sems[j % self.n], 16)

    def dma(self, eng, j, out, in_, **kw):
        if j >= self.n:
            self.wait_j(eng, j - self.n)
        ins = eng.dma_start(out=out, in_=in_, **kw)
        self.inc(ins, j)
        return ins

    def wait_j(self, eng, j):
        s, v = qtok(j, self.n)
        eng.wait_ge(self.sems[s], v)

    def wait_all_j(self, eng, j_hi):
        for s in range(self.n):
            j = j_hi - ((j_hi % self.n - s) % self.n)
            if j >= 0:
                eng.wait_ge(self.sems[s], 16 * (j // self.n + 1))


class Emitter:

    def __init__(self, stage, name, ordered):
        self.stage = stage
        self.name = name
        self.ordered = ordered
        self.n = 0
        self.m = 0
        self.eng = None
        self.sem = None
        self.lsem = None
        self.last_wait = 0

    def _label(self, label):
        self.m += 1
        if self.eng is None:
            assert label not in self.stage.labels, label
            self.stage.labels[label] = (self.name, self.m)

    def op(self, fn, label=None):
        self.n += 1
        if label is not None:
            self._label(label)
        if self.eng is None:
            return None
        if self.ordered and self.n - 1 > self.last_wait:
            self.eng.wait_ge(self.sem, self.n - 1)
        self.last_wait = max(self.last_wait, self.n - 1)
        ins = fn(self.eng)
        ins.then_inc(self.sem, 1)
        if label is not None:
            self.eng.drain().then_inc(self.lsem, 1)
        return ins

    def unordered_op(self, fn, label=None):
        if label is None:
            if self.eng is not None:
                fn(self.eng)
            return None
        self._label(label)
        if self.eng is None:
            return None
        ins = fn(self.eng)
        ins.then_inc(self.lsem, 1)
        return ins

    def unordered_drain_op(self, fn, label):
        self._label(label)
        if self.eng is None:
            return None
        fn(self.eng)
        self.eng.drain().then_inc(self.lsem, 1)
        return None

    def wait(self, label):
        if self.eng is None:
            return
        ename, pos = self.stage.labels[label]
        self.eng.wait_ge(self.stage.lsems[ename], pos)

    def wait_dma(self, dq, key):
        if self.eng is None:
            return
        dq.wait_j(self.eng, dq.j((self.stage.name,) + tuple(key)))

    def wait_dma_all(self, dq, key_hi):
        if self.eng is None:
            return
        dq.wait_all_j(self.eng, dq.j((self.stage.name,) + tuple(key_hi)))

    def dma(self, dq, key, out, in_, **kw):
        if self.eng is None:
            dq.alloc((self.stage.name,) + tuple(key))
            return
        dq.dma(self.eng, dq.j((self.stage.name,) + tuple(key)), out, in_, **kw)


class GlobalSync:

    ENGINES = [("sync", False), ("scalar", True), ("vector", True),
               ("tensor", False), ("gpsimd", True)]

    def __init__(self, nc, es):
        self.nc = nc
        self.es = es
        self.sems = {}
        self.lsems = {}
        self.counts = {}
        self.lcounts = {}
        for ename, _ in GlobalSync.ENGINES:
            self.sems[ename] = es.enter_context(nc.semaphore(f"g_{ename}"))
            self.lsems[ename] = es.enter_context(nc.semaphore(f"g_l_{ename}"))
            self.counts[ename] = 0
            self.lcounts[ename] = 0
        self.dqs = {}

    def dq(self, name):
        if name not in self.dqs:
            self.dqs[name] = DQ(self.nc, self.es, name)
        return self.dqs[name]


class Stage:

    def __init__(self, nc, es, name, gs=None):
        self.nc = nc
        self.name = name
        self.gs = gs if gs is not None else GlobalSync(nc, es)
        self.labels = {}
        self.sems = self.gs.sems
        self.lsems = self.gs.lsems
        self.progs = {}

    def engine(self, ename):
        def deco(fn):
            self.progs[ename] = fn
            return fn
        return deco

    def run(self):
        order = dict(GlobalSync.ENGINES)
        base_n = dict(self.gs.counts)
        base_m = dict(self.gs.lcounts)
        for ename, fn in self.progs.items():
            em = Emitter(self, ename, order[ename])
            em.n = base_n[ename]
            em.m = base_m[ename]
            em.last_wait = base_n[ename]
            fn(em)
            self.gs.counts[ename] = em.n
            self.gs.lcounts[ename] = em.m
        with self.nc.Block() as blk:
            for ename, fn in self.progs.items():
                em = Emitter(self, ename, order[ename])
                em.n = base_n[ename]
                em.m = base_m[ename]
                em.last_wait = base_n[ename]
                em.sem = self.sems[ename]
                em.lsem = self.lsems[ename]

                def section(eng, em=em, fn=fn):
                    em.eng = eng
                    fn(em)
                getattr(blk, ename)(section)



def build_stageA(nc, io, es, gs=None):
    x = io.d("x", [2, PPC, D], F32, "in")
    WkT = io.d("WkT", [D, D], F32, "in")
    WvT = io.d("WvT", [D, D], F32, "in")
    bk = io.d("bk", [D], F32, "in")
    bv = io.d("bv", [D], F32, "in")
    identb = io.d("identb", [128, 128], BF16, "in")
    s_xn = io.d("s_xn", [16, 128, NT * T], BF16, "scr_out")
    s_kv = io.d("s_kv", [16, 128, NT * T], F32, "scr_out")
    totals = io.d("totals", [2, D], F32, "scr_out")

    E = es.enter_context
    wk_sb = E(nc.sbuf_tensor("wk_sb", [128, 16, 16, 128], BF16))
    wv_sb = E(nc.sbuf_tensor("wv_sb", [128, 16, 16, 128], BF16))
    wstripe = E(nc.sbuf_tensor("wstripe", [128, 2, D], F32))
    bkv = E(nc.sbuf_tensor("bkv", [128, 16, 2], F32))
    idb = E(nc.sbuf_tensor("idb", [128, 128], BF16))
    xbuf = E(nc.sbuf_tensor("xbuf", [128, 2, D], F32))
    stat = E(nc.sbuf_tensor("stat", [128, 4, 8], F32))
    xntok = E(nc.sbuf_tensor("xntok", [128, 2, D], BF16))
    xnf = E(nc.sbuf_tensor("xnf", [128, 16, T], BF16))
    relu_sb = E(nc.sbuf_tensor("relu_sb", [128, T], F32))
    mn_sb = E(nc.sbuf_tensor("mn_sb", [128, T], F32))
    exp_sb = E(nc.sbuf_tensor("exp_sb", [128, T], F32))
    phi_sb = E(nc.sbuf_tensor("phi_sb", [128, T], F32))
    u_sb = E(nc.sbuf_tensor("u_sb", [128, T], F32))
    kvb = E(nc.sbuf_tensor("kvb", [128, 2, T], F32))
    carry = E(nc.sbuf_tensor("carry", [128, 16, 2], F32))
    zeros = E(nc.sbuf_tensor("zeros", [128, H], F32))
    epsc = E(nc.sbuf_tensor("epsc", [128, 1], F32))
    ptr = E(nc.psum_tensor("ptr", [128, 2, 2048], BF16))
    pk = E(nc.psum_tensor("pk", [128, 2, T], F32))
    pv = E(nc.psum_tensor("pv", [128, 2, T], F32))
    stg = Stage(nc, es, "A", gs)
    gs = stg.gs
    ld = gs.dq("q_sync")
    st = gs.dq("q_gpsimd")

    def t_stripe(j):
        return ("stripe", j)
    def t_bias(c, w_):
        return ("bias", c, w_)
    T_IDENT = ("ident",)
    def t_x(tg):
        return ("x", tg)
    def t_xn(i, fc):
        return ("xn", i, fc)
    def t_kv(i, ec):
        return ("kv", i, ec)
    def t_tot(ec, sq):
        return ("tot", ec, sq)


    @stg.engine("sync")
    def _(e):
        for wi, wsrc in enumerate([WkT, WvT]):
            for dc in range(16):
                j = wi * 16 + dc
                if j >= 2:
                    e.wait(("wcast", j - 2))
                e.dma(ld, t_stripe(j), wstripe[:, j % 2, :], wsrc[dc * 128:(dc + 1) * 128, :])
        for c in range(16):
            e.dma(ld, t_bias(c, 0), bkv[:, c, 0:1], bk[c * 128:(c + 1) * 128].rearrange("(p one) -> p one", one=1))
            e.dma(ld, t_bias(c, 1), bkv[:, c, 1:2], bv[c * 128:(c + 1) * 128].rearrange("(p one) -> p one", one=1))
        e.dma(ld, T_IDENT, idb[:, :], identb)
        for i in range(NT):
            for tc in range(4):
                tg = i * 4 + tc
                if tg >= 2:
                    e.wait(("apply", tg - 2))
                    e.wait(("rsum", tg - 2))
                sq = tc // 2
                p0 = i * 256 + (tc % 2) * 128
                e.dma(ld, t_x(tg), xbuf[:, tg % 2, :], x[sq, p0:p0 + 128, :])

    @stg.engine("vector")
    def _(e):
        e.op(lambda v: v.memset(zeros[:, :], 0.0))
        e.op(lambda v: v.memset(carry[:, :, :], 0.0))
        e.op(lambda v: v.memset(epsc[:, :], EPS))
        for wi, wdst in enumerate([wk_sb, wv_sb]):
            for dc in range(16):
                j = wi * 16 + dc
                e.wait_dma(ld, t_stripe(j))
                e.op(lambda v, wdst=wdst, dc=dc, j=j: v.tensor_copy(
                    wdst[:, dc, :, :],
                    wstripe[:, j % 2, :].rearrange("p (e m) -> p e m", m=128)),
                    label=("wcast", j))
        e.wait_dma_all(ld, T_IDENT)
        for i in range(NT):
            for tc in range(4):
                tg = i * 4 + tc
                e.wait_dma(ld, t_x(tg))
                e.op(lambda v, tc=tc, tg=tg: v.reduce_sum(stat[:, tc, 0:1], xbuf[:, tg % 2, :], AX.X),
                     label=("rsum", tg))
                e.wait(("sq", tg))
                e.op(lambda v, tc=tc: v.tensor_scalar(stat[:, tc, 2:3], stat[:, tc, 0:1], 1.0 / D, None, ALU.mult))
                e.op(lambda v, tc=tc: v.tensor_tensor(stat[:, tc, 7:8], stat[:, tc, 2:3], stat[:, tc, 2:3], ALU.mult))
                e.op(lambda v, tc=tc: v.tensor_scalar(stat[:, tc, 3:4], stat[:, tc, 1:2], 1.0 / D, None, ALU.mult))
                e.op(lambda v, tc=tc: v.tensor_tensor(stat[:, tc, 3:4], stat[:, tc, 3:4], stat[:, tc, 7:8], ALU.subtract),
                     label=("var", tg))
                e.wait(("sqrt", tg))
                e.op(lambda v, tc=tc: v.reciprocal(stat[:, tc, 5:6], stat[:, tc, 4:5]))
                e.op(lambda v, tc=tc: v.scalar_tensor_tensor(stat[:, tc, 6:7], stat[:, tc, 2:3], -1.0,
                                                             stat[:, tc, 5:6], ALU.mult, ALU.mult),
                     label=("rstd", tg))
            for tc in range(4):
                tg = i * 4 + tc
                e.wait(("tr", tg))
                if tc == 0 and i >= 1:
                    e.wait_dma_all(st, t_xn(i - 1, 15))
                e.op(lambda v, tc=tc, tg=tg: v.tensor_copy(
                    xnf[:, :, tc * 128:(tc + 1) * 128],
                    ptr[:, tg % 2, :].rearrange("p (c m) -> p c m", m=128)),
                    label=("evac", tg))
            for ec in range(16):
                g = i * 16 + ec
                e.wait(("mk", g))
                e.op(lambda v, ec=ec: v.tensor_scalar(relu_sb[:, :], pk[:, ec % 2, :], bkv[:, ec, 0:1], 0.0,
                                                      ALU.add, ALU.max), label=("relu", g))
                e.op(lambda v, ec=ec: v.tensor_scalar(mn_sb[:, :], pk[:, ec % 2, :], bkv[:, ec, 0:1], 0.0,
                                                      ALU.add, ALU.min), label=("mn", g))
                e.wait(("exp", g))
                e.op(lambda v: v.tensor_tensor(phi_sb[:, :], relu_sb[:, :], exp_sb[:, :], ALU.add),
                     label=("phi", g))
                e.wait(("mv", g))
                e.op(lambda v, ec=ec: v.scalar_tensor_tensor(u_sb[:, :], pv[:, ec % 2, :], bkv[:, ec, 1:2],
                                                             phi_sb[:, :], ALU.add, ALU.mult),
                     label=("u", g))
                if g >= 2:
                    e.wait_dma(st, t_kv((g - 2) // 16, (g - 2) % 16))
                e.op(lambda v, ec=ec, g=g: v.tensor_tensor_scan(kvb[:, g % 2, 0:H], u_sb[:, 0:H], zeros[:, :],
                                                                carry[:, ec, 0:1], ALU.add, ALU.add))
                e.op(lambda v, ec=ec, g=g: v.tensor_tensor_scan(kvb[:, g % 2, H:T], u_sb[:, H:T], zeros[:, :],
                                                                carry[:, ec, 1:2], ALU.add, ALU.add),
                     label=("scan", g))
                e.op(lambda v, ec=ec, g=g: v.tensor_copy(carry[:, ec, 0:1], kvb[:, g % 2, H - 1:H]))
                e.op(lambda v, ec=ec, g=g: v.tensor_copy(carry[:, ec, 1:2], kvb[:, g % 2, T - 1:T]),
                     label=("carry", g))

    @stg.engine("scalar")
    def _(e):
        e.wait_dma_all(ld, T_IDENT)
        for i in range(NT):
            for tc in range(4):
                tg = i * 4 + tc
                e.wait_dma(ld, t_x(tg))
                if tg >= 2:
                    e.wait(("tr", tg - 2))
                e.op(lambda s_, tc=tc, tg=tg: s_.activation(xntok[:, tg % 2, :], xbuf[:, tg % 2, :], AF.Square,
                                                            bias=0.0, scale=1.0, accum_out=stat[:, tc, 1:2]),
                     label=("sq", tg))
                e.wait(("var", tg))
                e.op(lambda s_, tc=tc: s_.activation(stat[:, tc, 4:5], stat[:, tc, 3:4], AF.Sqrt,
                                                     bias=epsc[:, 0:1], scale=1.0), label=("sqrt", tg))
                e.wait(("rstd", tg))
                e.op(lambda s_, tc=tc, tg=tg: s_.activation(xntok[:, tg % 2, :], xbuf[:, tg % 2, :], AF.Identity,
                                                            bias=stat[:, tc, 6:7], scale=stat[:, tc, 5:6]),
                     label=("apply", tg))
            for ec in range(16):
                g = i * 16 + ec
                if g >= 1:
                    e.wait(("phi", g - 1))
                e.wait(("mn", g))
                e.op(lambda s_: s_.activation(exp_sb[:, :], mn_sb[:, :], AF.Exp,
                                              bias=0.0, scale=1.0), label=("exp", g))

    @stg.engine("tensor")
    def _(e):
        e.wait_dma(ld, T_IDENT)
        e.wait(("wcast", 31))
        for i in range(NT):
            for tc in range(4):
                tg = i * 4 + tc
                e.wait(("apply", tg))
                if tg >= 2:
                    e.wait(("evac", tg - 2))
                for fc in range(15):
                    e.unordered_op(
                        lambda t, tg=tg, fc=fc: t.transpose(ptr[:, tg % 2, fc * 128:(fc + 1) * 128],
                                                            xntok[:, tg % 2, fc * 128:(fc + 1) * 128], idb[:, :]))
                e.unordered_drain_op(
                    lambda t, tg=tg: t.transpose(ptr[:, tg % 2, 15 * 128:16 * 128],
                                                 xntok[:, tg % 2, 15 * 128:16 * 128], idb[:, :]),
                    label=("tr", tg))
            e.wait(("evac", i * 4 + 3))
            for ec in range(16):
                g = i * 16 + ec
                if g >= 2:
                    e.wait(("relu", g - 2))
                    e.wait(("u", g - 2))
                for dc in range(16):
                    e.unordered_op(
                        lambda t, dc=dc, ec=ec: t.matmul(pk[:, ec % 2, :], wk_sb[:, dc, ec, :],
                                                         xnf[:, dc, :], start=(dc == 0), stop=(dc == 15)),
                        label=("mk", g) if dc == 15 else None)
                for dc in range(16):
                    e.unordered_op(
                        lambda t, dc=dc, ec=ec: t.matmul(pv[:, ec % 2, :], wv_sb[:, dc, ec, :],
                                                         xnf[:, dc, :], start=(dc == 0), stop=(dc == 15)),
                        label=("mv", g) if dc == 15 else None)

    @stg.engine("gpsimd")
    def _(e):
        for i in range(NT):
            e.wait(("evac", i * 4 + 3))
            for fc in range(16):
                e.dma(st, t_xn(i, fc), s_xn[fc, :, i * T:(i + 1) * T], xnf[:, fc, :])
            for ec in range(16):
                g = i * 16 + ec
                e.wait(("scan", g))
                e.dma(st, t_kv(i, ec), s_kv[ec, :, i * T:(i + 1) * T], kvb[:, g % 2, :])
        e.wait(("carry", NT * 16 - 1))
        for ec in range(16):
            for sq in range(2):
                e.dma(st, t_tot(ec, sq),
                      totals[sq, ec * 128:(ec + 1) * 128].rearrange("(p one) -> p one", one=1),
                      carry[:, ec, sq:sq + 1])
        e.wait_dma_all(st, t_tot(15, 1))

    stg.run()
    return ["s_xn", "s_kv", "totals"]



def build_stageB1(nc, io, es, gs=None):
    s_xn = io.d("s_xn", [16, 128, NT * T], BF16, "scr_in")
    s_kv = io.d("s_kv", [16, 128, NT * T], F32, "scr_in")
    ag = io.d("ag", [16, D], F32, "scr_in")
    pmask = io.d("pmask", [16, 2], F32, "in")
    WqT = io.d("WqT", [D, D], F32, "in")
    WoT = io.d("WoT", [D, D], F32, "in")
    bq = io.d("bq", [D], F32, "in")
    bo = io.d("bo", [D], F32, "in")
    identf = io.d("identf", [128, 128], F32, "in")
    s_ratt = io.d("s_ratt", [16, 128, NT * T], F32, "scr_out")

    E = es.enter_context
    wq_sb = E(nc.sbuf_tensor("wq_sb", [128, 16, 16, 128], BF16))
    wo_sb = E(nc.sbuf_tensor("wo_sb", [128, 16, 16, 128], BF16))
    wstripe = E(nc.sbuf_tensor("B1stripe", [128, 1, D], F32))
    bqo = E(nc.sbuf_tensor("bqo", [128, 16, 2], F32))
    idf = E(nc.sbuf_tensor("B1idf", [128, 128], F32))
    ag_sb = E(nc.sbuf_tensor("ag_sb", [16, D], F32))
    pm_sb = E(nc.sbuf_tensor("pm_sb", [16, 2], F32))
    prow = E(nc.sbuf_tensor("prow", [2, D], F32))
    prefix = E(nc.sbuf_tensor("prefix", [128, 16, 2], F32))
    xn_sb = E(nc.sbuf_tensor("B1xn", [128, 16, T], BF16))
    kv_sb = E(nc.sbuf_tensor("B1kv", [128, 2, T], F32))
    kvp_sb = E(nc.sbuf_tensor("B1kvp", [128, T], F32))
    relu_sb = E(nc.sbuf_tensor("B1relu", [128, T], F32))
    mn_sb = E(nc.sbuf_tensor("B1mn", [128, T], F32))
    exp_sb = E(nc.sbuf_tensor("B1exp", [128, T], F32))
    phi_sb = E(nc.sbuf_tensor("B1phi", [128, T], F32))
    ctx_sb = E(nc.sbuf_tensor("B1ctx", [128, 16, T], BF16))
    ratt_sb = E(nc.sbuf_tensor("B1ratt", [128, 2, T], F32))
    pq = E(nc.psum_tensor("B1pq", [128, 3, T], F32))
    po = E(nc.psum_tensor("B1po", [128, 3, T], F32))
    pp = E(nc.psum_tensor("B1pp", [2, T], F32))
    ppt = E(nc.psum_tensor("B1ppt", [128, 32], F32))
    stg = Stage(nc, es, "B1", gs)
    gs = stg.gs
    ld = gs.dq("q_sync")
    st = gs.dq("q_gpsimd")

    def t_stripe(j):
        return ("stripe", j)
    def t_bias(c, w_):
        return ("bias", c, w_)
    T_IDF = ("idf",)
    T_AG = ("ag",)
    T_PM = ("pm",)
    def t_xn(i):
        return ("xn", i)
    def t_kv(i, ec):
        return ("kv", i, ec)
    def t_ratt(i, oc):
        return ("ratt", i, oc)


    @stg.engine("sync")
    def _(e):
        for wi, wsrc in enumerate([WqT, WoT]):
            for dc in range(16):
                j = wi * 16 + dc
                if j >= 1:
                    e.wait(("wcast", j - 1))
                e.dma(ld, t_stripe(j), wstripe[:, 0, :], wsrc[dc * 128:(dc + 1) * 128, :])
        for c in range(16):
            e.dma(ld, t_bias(c, 0), bqo[:, c, 0:1], bq[c * 128:(c + 1) * 128].rearrange("(p one) -> p one", one=1))
            e.dma(ld, t_bias(c, 1), bqo[:, c, 1:2], bo[c * 128:(c + 1) * 128].rearrange("(p one) -> p one", one=1))
        e.dma(ld, T_IDF, idf[:, :], identf)
        e.dma(ld, T_AG, ag_sb[:, :], ag)
        e.dma(ld, T_PM, pm_sb[:, :], pmask)
        for i in range(NT):
            if i >= 1:
                e.wait(("ctxdone", i - 1))
            e.dma(ld, t_xn(i), xn_sb[:, :, :],
                  s_xn.transpose([1, 0, 2])[:, :, i * T:(i + 1) * T])
            for ec in range(16):
                g = i * 16 + ec
                if g >= 2:
                    e.wait(("kvp", g - 2))
                e.dma(ld, t_kv(i, ec), kv_sb[:, g % 2, :], s_kv[ec, :, i * T:(i + 1) * T])

    @stg.engine("tensor")
    def _(e):
        e.wait_dma(ld, T_AG)
        e.wait_dma(ld, T_PM)
        e.wait_dma(ld, T_IDF)
        for ng in range(4):
            e.unordered_op(lambda t, ng=ng: t.matmul(pp[:, :], pm_sb[:, :], ag_sb[:, ng * 512:(ng + 1) * 512],
                                                     start=True, stop=True),
                           label=("ppm", ng))
            e.wait(("ppev", ng))
        for fc in range(16):
            e.unordered_op(lambda t, fc=fc: t.transpose(ppt[:, fc * 2:(fc + 1) * 2],
                                                        prow[:, fc * 128:(fc + 1) * 128], idf[0:2, 0:2]))
        e.unordered_drain_op(lambda t: t.nop(), label=("ptr",))
        e.wait(("wcast", 31))
        for i in range(NT):
            e.wait_dma(ld, t_xn(i))
            for ec in range(16):
                g = i * 16 + ec
                if g >= 3:
                    e.wait(("mn", g - 3))
                for dc in range(16):
                    e.unordered_op(lambda t, dc=dc, ec=ec, g=g: t.matmul(pq[:, g % 3, :], wq_sb[:, dc, ec, :],
                                                                         xn_sb[:, dc, :], start=(dc == 0), stop=(dc == 15)),
                                   label=("mq", g) if dc == 15 else None)
            e.wait(("ctx", i, 15))
            for oc in range(16):
                go = i * 16 + oc
                if go >= 3:
                    e.wait(("rattev", go - 3))
                for ec in range(16):
                    e.unordered_op(lambda t, ec=ec, oc=oc, go=go: t.matmul(po[:, go % 3, :], wo_sb[:, ec, oc, :],
                                                                           ctx_sb[:, ec, :], start=(ec == 0), stop=(ec == 15)),
                                   label=("mo", i, oc) if ec == 15 else None)

    @stg.engine("vector")
    def _(e):
        for wi, wdst in enumerate([wq_sb, wo_sb]):
            for dc in range(16):
                j = wi * 16 + dc
                e.wait_dma(ld, t_stripe(j))
                e.op(lambda v, wdst=wdst, dc=dc, j=j: v.tensor_copy(
                    wdst[:, dc, :, :],
                    wstripe[:, 0, :].rearrange("p (e m) -> p e m", m=128)),
                    label=("wcast", j))
        for ng in range(4):
            e.wait(("ppm", ng))
            e.op(lambda v, ng=ng: v.tensor_copy(prow[:, ng * 512:(ng + 1) * 512], pp[:, :]),
                 label=("ppev", ng))
        e.wait(("ptr",))
        e.op(lambda v: v.tensor_copy(prefix[:, :, :], ppt[:, :].rearrange("p (c s) -> p c s", s=2)),
             label=("pfx",))
        e.wait_dma_all(ld, T_PM)
        for i in range(NT):
            for ec in range(16):
                g = i * 16 + ec
                e.wait(("mq", g))
                e.op(lambda v, ec=ec, g=g: v.tensor_scalar(relu_sb[:, :], pq[:, g % 3, :], bqo[:, ec, 0:1], 0.0,
                                                           ALU.add, ALU.max), label=("relu", g))
                e.op(lambda v, ec=ec, g=g: v.tensor_scalar(mn_sb[:, :], pq[:, g % 3, :], bqo[:, ec, 0:1], 0.0,
                                                           ALU.add, ALU.min), label=("mn", g))
                e.wait(("exp", g))
                e.op(lambda v: v.tensor_tensor(phi_sb[:, :], relu_sb[:, :], exp_sb[:, :], ALU.add))
                e.wait_dma(ld, t_kv(i, ec))
                e.op(lambda v, ec=ec, g=g: v.tensor_scalar(kvp_sb[:, 0:H], kv_sb[:, g % 2, 0:H],
                                                           prefix[:, ec, 0:1], SCALE0, ALU.add, ALU.mult))
                e.op(lambda v, ec=ec, g=g: v.tensor_scalar(kvp_sb[:, H:T], kv_sb[:, g % 2, H:T],
                                                           prefix[:, ec, 1:2], SCALE0, ALU.add, ALU.mult),
                     label=("kvp", g))
                e.op(lambda v, ec=ec: v.tensor_tensor(ctx_sb[:, ec, :], phi_sb[:, :], kvp_sb[:, :], ALU.mult),
                     label=("ctx", i, ec))
            e.op(lambda v: v.engine_nop(), label=("ctxdone", i))

    @stg.engine("scalar")
    def _(e):
        e.wait_dma_all(ld, T_PM)
        for i in range(NT):
            for ec in range(16):
                g = i * 16 + ec
                e.wait(("mn", g))
                e.op(lambda s_: s_.activation(exp_sb[:, :], mn_sb[:, :], AF.Exp, bias=0.0, scale=1.0),
                     label=("exp", g))
            for oc in range(16):
                go = i * 16 + oc
                e.wait(("mo", i, oc))
                if go >= 2:
                    e.wait_dma(st, t_ratt((go - 2) // 16, (go - 2) % 16))
                e.op(lambda s_, oc=oc, go=go: s_.activation(ratt_sb[:, oc % 2, :], po[:, go % 3, :], AF.Identity,
                                                            bias=bqo[:, oc, 1:2], scale=1.0),
                     label=("rattev", go))

    @stg.engine("gpsimd")
    def _(e):
        for i in range(NT):
            for oc in range(16):
                go = i * 16 + oc
                e.wait(("rattev", go))
                e.dma(st, t_ratt(i, oc), s_ratt[oc, :, i * T:(i + 1) * T], ratt_sb[:, oc % 2, :])
        e.wait_dma_all(st, t_ratt(NT - 1, 15))

    stg.run()
    return ["s_ratt"]



def build_stageB2(nc, io, es, gs=None):
    s_xn = io.d("s_xn", [16, 128, NT * T], BF16, "scr_in")
    SU_ug = io.d("SU_ug", [D], F32, "in")
    H128a = io.d("H128a", [128, 128], BF16, "in")
    s_y = io.d("s_y", [16, 128, NT * T], BF16, "scr_out")

    E = es.enter_context
    h_sb = E(nc.sbuf_tensor("B2h", [128, 128], BF16))
    su_sb = E(nc.sbuf_tensor("B2su", [128, 16], F32))
    xn_sb = E(nc.sbuf_tensor("B2xn", [128, 2, 16, T], BF16))
    z_sb = E(nc.sbuf_tensor("B2z", [128, 2, T], BF16))
    y0 = E(nc.sbuf_tensor("B2y0", [128, 16, T], BF16))
    y1 = E(nc.sbuf_tensor("B2y1", [128, 16, T], BF16))
    py = E(nc.psum_tensor("B2py", [128, 2, T], F32))
    stg = Stage(nc, es, "B2", gs)
    gs = stg.gs
    ld = gs.dq("q_sync")
    st = gs.dq("q_gpsimd")

    T_H = ("h",)
    def t_su(c):
        return ("su", c)
    def t_xn(i):
        return ("xn", i)
    def t_y(i, fc):
        return ("y", i, fc)


    @stg.engine("sync")
    def _(e):
        e.dma(ld, T_H, h_sb[:, :], H128a)
        for c in range(16):
            e.dma(ld, t_su(c), su_sb[:, c:c + 1], SU_ug[c * 128:(c + 1) * 128].rearrange("(p one) -> p one", one=1))
        for i in range(NT):
            if i >= 2:
                e.wait(("z", (i - 2) * 16 + 15))
            e.dma(ld, t_xn(i), xn_sb[:, i % 2, :, :],
                  s_xn.transpose([1, 0, 2])[:, :, i * T:(i + 1) * T])

    @stg.engine("vector")
    def _(e):
        e.wait_dma_all(ld, t_su(15))
        for i in range(NT):
            e.wait_dma(ld, t_xn(i))
            for fc in range(16):
                g = i * 16 + fc
                if g >= 2:
                    e.wait(("my", g - 2))
                e.op(lambda v, fc=fc, i=i, g=g: v.tensor_scalar(z_sb[:, g % 2, :], xn_sb[:, i % 2, fc, :],
                                                                su_sb[:, fc:fc + 1], None, ALU.mult),
                     label=("z", g))
                e.wait(("my", g))
                if g >= 16 and fc == 0:
                    e.wait_dma_all(st, t_y(i - 1, 15))
                e.op(lambda v, fc=fc, g=g: v.tensor_copy(y0[:, fc, :], py[:, g % 2, :]),
                     label=("ev", g))
            for si, b in enumerate((1, 2, 4, 8)):
                src, dst = (y0, y1) if si % 2 == 0 else (y1, y0)
                for c in range(16):
                    if c & b:
                        e.op(lambda v, c=c, b=b, src=src, dst=dst: v.tensor_tensor(
                            dst[:, c, :], src[:, c ^ b, :], src[:, c, :], ALU.subtract))
                    else:
                        e.op(lambda v, c=c, b=b, src=src, dst=dst: v.tensor_tensor(
                            dst[:, c, :], src[:, c, :], src[:, c ^ b, :], ALU.add))
            e.op(lambda v: v.engine_nop(), label=("ydone", i))

    @stg.engine("tensor")
    def _(e):
        e.wait_dma(ld, T_H)
        for i in range(NT):
            for fc in range(16):
                g = i * 16 + fc
                e.wait(("z", g))
                if g >= 2:
                    e.wait(("ev", g - 2))
                e.unordered_op(lambda t, g=g: t.matmul(py[:, g % 2, :], h_sb[:, :], z_sb[:, g % 2, :],
                                                       start=True, stop=True),
                               label=("my", g))

    @stg.engine("gpsimd")
    def _(e):
        for i in range(NT):
            e.wait(("ydone", i))
            for fc in range(16):
                e.dma(st, t_y(i, fc), s_y[fc, :, i * T:(i + 1) * T], y0[:, fc, :])
        e.wait_dma_all(st, t_y(NT - 1, 15))

    stg.run()
    return ["s_y"]



def build_stageD(nc, io, es, gs=None):
    s_y = io.d("s_y", [16, 128, NT * T], BF16, "scr_in")
    WupT = io.d("WupT", [D, INTER], F32, "in")
    WgateT = io.d("WgateT", [D, INTER], F32, "in")
    Wscale_ug = io.d("Wscale_ug", [2 * INTER], F32, "in")
    SU_dp = io.d("SU_dp", [INTER], F32, "in")
    H128s = io.d("H128s", [128, 128], BF16, "in")
    We_blk = io.d("We_blk", [8, 16, 16, 128, 128], F32, "in")
    s_wt = io.d("s_wt", [64, 128, NT * T], BF16, "scr_out")
    s_web = io.d("s_web", [8, 16, 16, 128, 128], BF16, "scr_out")

    E = es.enter_context
    wup_sb = E(nc.sbuf_tensor("Dwup", [128, 16, 16, 128], BF16))
    wgt_sb = E(nc.sbuf_tensor("Dwgt", [128, 16, 16, 128], BF16))
    wstripe = E(nc.sbuf_tensor("Dstripe", [128, 2, D], F32))
    h_sb = E(nc.sbuf_tensor("Dh", [128, 128], BF16))
    wsu_sb = E(nc.sbuf_tensor("Dwsu", [128, 64], F32))
    wsg_sb = E(nc.sbuf_tensor("Dwsg", [128, 64], F32))
    sud_sb = E(nc.sbuf_tensor("Dsud", [128, 64], F32))
    y_sb = E(nc.sbuf_tensor("Dy", [128, 2, 16, T], BF16))
    g_sb = E(nc.sbuf_tensor("Dg", [128, 2, T], F32))
    w_sb = E(nc.sbuf_tensor("Dw", [128, 2, T], BF16))
    wt_sb = E(nc.sbuf_tensor("Dwt", [128, 2, T], BF16))
    wei_f = E(nc.sbuf_tensor("Dweif", [128, 1, 16, 128], F32))
    web_b = E(nc.sbuf_tensor("Dwebb", [128, 1, 16, 128], BF16))
    pup = E(nc.psum_tensor("Dpup", [128, 3, T], F32))
    pgt = E(nc.psum_tensor("Dpgt", [128, 3, T], F32))
    ph = E(nc.psum_tensor("Dph", [128, 2, T], F32))
    stg = Stage(nc, es, "D", gs)
    gs = stg.gs
    ld = gs.dq("q_sync")
    st = gs.dq("q_gpsimd")

    def t_wsu(c):
        return ("wsu", c)
    def t_wsg(c):
        return ("wsg", c)
    def t_sud(c):
        return ("sud", c)
    T_H = ("h",)
    def t_wcast(b):
        return ("wecast", b)
    def t_webst(b):
        return ("webst", b)
    def t_stripe(q, j):
        return ("stripe", q, j)
    def t_y(q, i):
        return ("y", q, i)
    def t_wt(q, i, ic):
        return ("wt", q, i, ic)


    @stg.engine("sync")
    def _(e):
        for c in range(64):
            e.dma(ld, t_wsu(c), wsu_sb[:, c:c + 1], Wscale_ug[c * 128:(c + 1) * 128].rearrange("(p one) -> p one", one=1))
        for c in range(64):
            e.dma(ld, t_wsg(c), wsg_sb[:, c:c + 1], Wscale_ug[INTER + c * 128:INTER + (c + 1) * 128].rearrange("(p one) -> p one", one=1))
        for c in range(64):
            e.dma(ld, t_sud(c), sud_sb[:, c:c + 1], SU_dp[c * 128:(c + 1) * 128].rearrange("(p one) -> p one", one=1))
        e.dma(ld, T_H, h_sb[:, :], H128s)
        for q in range(4):
            for wi, wsrc in enumerate([WupT, WgateT]):
                for dc in range(16):
                    j = wi * 16 + dc
                    if q * 32 + j >= 2:
                        e.wait(("wcast", q * 32 + j - 2))
                    e.dma(ld, t_stripe(q, j), wstripe[:, j % 2, :],
                          wsrc[dc * 128:(dc + 1) * 128, q * D:(q + 1) * D])
            for i in range(NT):
                gt = q * NT + i
                if gt >= 2:
                    e.wait(("mgi", (gt - 2) * 16 + 15))
                e.dma(ld, t_y(q, i), y_sb[:, gt % 2, :, :],
                      s_y.transpose([1, 0, 2])[:, :, i * T:(i + 1) * T])
                for bb in range(4):
                    b = gt * 4 + bb
                    e8b, ocb = b // 16, b % 16
                    if b >= 1:
                        e.wait(("wecast", b - 1))
                    e.dma(ld, t_wcast(b), wei_f[:, 0, :, :], We_blk[e8b, ocb].transpose([1, 0, 2]))

    @stg.engine("vector")
    def _(e):
        e.wait_dma_all(ld, t_sud(63))
        e.op(lambda v: v.tensor_scalar(wsg_sb[:, :], wsg_sb[:, :], 64.0, None, ALU.mult))
        e.op(lambda v: v.tensor_scalar(wsu_sb[:, :], wsu_sb[:, :], 64.0, None, ALU.mult))
        e.op(lambda v: v.tensor_tensor(wsu_sb[:, :], wsu_sb[:, :], sud_sb[:, :], ALU.mult),
             label=("scales",))
        for q in range(4):
            for wi, wdst in enumerate([wup_sb, wgt_sb]):
                for dc in range(16):
                    j = wi * 16 + dc
                    e.wait_dma(ld, t_stripe(q, j))
                    e.op(lambda v, wdst=wdst, dc=dc, j=j: v.tensor_copy(
                        wdst[:, dc, :, :],
                        wstripe[:, j % 2, :].rearrange("p (i m) -> p i m", m=128)),
                        label=("wcast", q * 32 + j))
            for i in range(NT):
                for ic in range(16):
                    gq = (q * NT + i) * 16 + ic
                    ig = q * 16 + ic
                    e.wait(("mu", gq))
                    e.wait(("g", gq))
                    if gq >= 2:
                        e.wait(("mh", gq - 2))
                    e.op(lambda v, gq=gq, ig=ig: v.scalar_tensor_tensor(
                        w_sb[:, gq % 2, :], pup[:, gq % 3, :], wsu_sb[:, ig:ig + 1],
                        g_sb[:, gq % 2, :], ALU.mult, ALU.mult), label=("w", gq))
                    e.wait(("mh", gq))
                    if gq >= 2:
                        pg = gq - 2
                        e.wait_dma(st, t_wt(pg // (NT * 16), (pg // 16) % NT, pg % 16))
                    e.op(lambda v, gq=gq: v.tensor_copy(wt_sb[:, gq % 2, :], ph[:, gq % 2, :]),
                         label=("wtev", gq))
                    if ic in (3, 7, 11, 15):
                        gt_ = q * NT + i
                        b = gt_ * 4 + (ic // 4)
                        e.wait_dma(ld, t_wcast(b))
                        if b >= 1:
                            e.wait_dma(st, t_webst(b - 1))
                        e.op(lambda v, b=b: v.tensor_copy(web_b[:, 0, :, :], wei_f[:, 0, :, :]),
                             label=("wecast", b))

    @stg.engine("scalar")
    def _(e):
        e.wait(("scales",))
        for q in range(4):
            for i in range(NT):
                for ic in range(16):
                    gq = (q * NT + i) * 16 + ic
                    ig = q * 16 + ic
                    e.wait(("mgi", gq))
                    if gq >= 2:
                        e.wait(("w", gq - 2))
                    e.op(lambda s_, gq=gq, ig=ig: s_.activation(g_sb[:, gq % 2, :], pgt[:, gq % 3, :],
                                                                AF.Silu, bias=0.0, scale=wsg_sb[:, ig:ig + 1]),
                         label=("g", gq))

    @stg.engine("tensor")
    def _(e):
        def emit_mh(j2):
            e.wait(("w", j2))
            if j2 >= 2:
                e.wait(("wtev", j2 - 2))
            e.unordered_op(lambda t, j2=j2: t.matmul(ph[:, j2 % 2, :], h_sb[:, :], w_sb[:, j2 % 2, :],
                                                     start=True, stop=True),
                           label=("mh", j2))
        e.wait_dma(ld, T_H)
        for q in range(4):
            if q > 0:
                emit_mh(q * 128 - 2)
                emit_mh(q * 128 - 1)
            e.wait(("wcast", q * 32 + 31))
            for i in range(NT):
                gt = q * NT + i
                e.wait_dma(ld, t_y(q, i))
                for ic in range(16):
                    gq = gt * 16 + ic
                    if gq >= 3:
                        e.wait(("w", gq - 3))
                    for dc in range(16):
                        e.unordered_op(lambda t, dc=dc, ic=ic, gt=gt, gq=gq: t.matmul(
                            pup[:, gq % 3, :], wup_sb[:, dc, ic, :], y_sb[:, gt % 2, dc, :],
                            start=(dc == 0), stop=(dc == 15)),
                            label=("mu", gq) if dc == 15 else None)
                    if gq >= 3:
                        e.wait(("g", gq - 3))
                    for dc in range(16):
                        lab = None
                        if dc == 15:
                            lab = ("mgi", gq)
                        e.unordered_op(lambda t, dc=dc, ic=ic, gt=gt, gq=gq: t.matmul(
                            pgt[:, gq % 3, :], wgt_sb[:, dc, ic, :], y_sb[:, gt % 2, dc, :],
                            start=(dc == 0), stop=(dc == 15)), label=lab)
                    if gq - 2 >= q * 128:
                        emit_mh(gq - 2)

        emit_mh(NT * 64 - 2)
        emit_mh(NT * 64 - 1)

    @stg.engine("gpsimd")
    def _(e):
        for q in range(4):
            for i in range(NT):
                for ic in range(16):
                    gq = (q * NT + i) * 16 + ic
                    ig = q * 16 + ic
                    e.wait(("wtev", gq))
                    e.dma(st, t_wt(q, i, ic), s_wt[ig, :, i * T:(i + 1) * T], wt_sb[:, gq % 2, :])
                    if ic in (3, 7, 11, 15):
                        b = (q * NT + i) * 4 + (ic // 4)
                        e8b, ocb = b // 16, b % 16
                        e.wait(("wecast", b))
                        e.dma(st, t_webst(b), s_web[e8b, ocb].transpose([1, 0, 2]), web_b[:, 0, :, :])
        e.wait_dma_all(st, t_wt(3, NT - 1, 15))
        e.wait_dma_all(st, t_webst(127))

    stg.run()
    return ["s_wt"]



def build_stageE(nc, io, es, gs=None):
    s_wt = io.d("s_wt", [64, 128, NT * T], BF16, "scr_in")
    Hdd = io.d("Hdd", [128, 128], BF16, "in")
    s_yp = io.d("s_yp", [64, 128, NT * T], BF16, "scr_out")

    E = es.enter_context
    h_sb = E(nc.sbuf_tensor("Eh", [128, 128], BF16))
    wt_in = E(nc.sbuf_tensor("Ewt", [128, 2, 64, T], BF16))
    stage4 = E(nc.sbuf_tensor("Est", [128, 16, T], BF16))
    ph = E(nc.psum_tensor("Eph", [128, 2, T], F32))
    stg = Stage(nc, es, "E", gs)
    gs = stg.gs
    ld = gs.dq("q_sync")
    st = gs.dq("q_gpsimd")

    T_H = ("h",)
    def t_in(i, a):
        return ("in", i, a)
    def t_out(i, grp, a):
        return ("out", i, grp, a)


    @stg.engine("sync")
    def _(e):
        e.dma(ld, T_H, h_sb[:, :], Hdd)
        for i in range(NT):
            if i >= 2:
                e.wait(("em", (i - 2) * 64 + 63))
            for a in range(2):
                e.dma(ld, t_in(i, a), wt_in[a * 64:(a + 1) * 64, i % 2, :, :],
                      s_wt[:, a::2, i * T:(i + 1) * T])

    @stg.engine("tensor")
    def _(e):
        e.wait_dma(ld, T_H)
        for i in range(NT):
            e.wait_dma(ld, t_in(i, 0))
            e.wait_dma(ld, t_in(i, 1))
            for p2 in range(64):
                g = i * 64 + p2
                if g >= 2:
                    e.wait(("eev", g - 2))
                e.unordered_op(lambda t, i=i, p2=p2, g=g: t.matmul(ph[:, g % 2, :], h_sb[:, :],
                                                                  wt_in[:, i % 2, p2, :],
                                                                  start=True, stop=True),
                               label=("em", g))

    @stg.engine("vector")
    def _(e):
        for i in range(NT):
            for p2 in range(64):
                g = i * 64 + p2
                e.wait(("em", g))
                grp = p2 // 4
                if grp >= 4 and p2 % 4 == 0:
                    for a in range(2):
                        e.wait_dma(st, t_out(i, grp - 4, a))
                elif i >= 1 and grp < 4 and p2 % 4 == 0:
                    for a in range(2):
                        e.wait_dma(st, t_out(i - 1, 12 + grp, a))
                e.op(lambda v, g=g, p2=p2: v.tensor_copy(stage4[:, p2 % 16, :], ph[:, g % 2, :]),
                     label=("eev", g))

    @stg.engine("gpsimd")
    def _(e):
        for i in range(NT):
            for grp in range(16):
                e.wait(("eev", i * 64 + grp * 4 + 3))
                for a in range(2):
                    s0 = (grp % 4) * 4
                    e.dma(st, t_out(i, grp, a),
                          s_yp[:, 8 * grp + a::2, i * T:(i + 1) * T][:, 0:4, :],
                          stage4[a * 64:(a + 1) * 64, s0:s0 + 4, :])
        e.wait_dma_all(st, t_out(NT - 1, 15, 1))

    stg.run()
    return ["s_yp"]



def build_stageF(nc, io, es, gs=None):
    s_yp = io.d("s_yp", [64, 128, NT * T], BF16, "scr_in")
    s_ratt = io.d("s_ratt", [16, 128, NT * T], F32, "scr_in")
    WdownT = io.d("WdownT", [INTER, D], F32, "in")
    Wscale_dp = io.d("Wscale_dp", [D], F32, "in")
    s_dh = io.d("s_dh", [16, 128, NT * T], F32, "scr_out")

    E = es.enter_context
    wd_sb = E(nc.sbuf_tensor("Fwd", [128, 64, 8, 128], BF16))
    wstripe = E(nc.sbuf_tensor("Fstripe", [128, 1, 1024], F32))
    wsd_sb = E(nc.sbuf_tensor("Fwsd", [128, 16], F32))
    yp_sb = E(nc.sbuf_tensor("Fyp", [128, 64, T], BF16))
    ratt_sb = E(nc.sbuf_tensor("Fratt", [128, 2, T], F32))
    dd_sb = E(nc.sbuf_tensor("Fdd", [128, 2, T], F32))
    pd = E(nc.psum_tensor("Fpd", [128, 4, T], F32))
    stg = Stage(nc, es, "F", gs)
    gs = stg.gs
    ld = gs.dq("q_sync")
    st = gs.dq("q_gpsimd")

    def t_wsd(c):
        return ("wsd", c)
    def t_stripe(oh, j):
        return ("stripe", oh, j)
    def t_yp(oh, i, gch):
        return ("yp", oh, i, gch)
    def t_ratt(oh, i, oc):
        return ("ratt", oh, i, oc)
    def t_dh(oh, i, oc):
        return ("dh", oh, i, oc)


    @stg.engine("sync")
    def _(e):
        for c in range(16):
            e.dma(ld, t_wsd(c), wsd_sb[:, c:c + 1], Wscale_dp[c * 128:(c + 1) * 128].rearrange("(p one) -> p one", one=1))
        for oh in range(2):
            for j in range(64):
                if oh * 64 + j >= 1:
                    e.wait(("wcast", oh * 64 + j - 1))
                e.dma(ld, t_stripe(oh, j), wstripe[:, 0, :],
                      WdownT[j * 128:(j + 1) * 128, oh * 1024:(oh + 1) * 1024])
            for i in range(NT):
                gt = oh * NT + i
                if gt >= 1:
                    e.wait(("md", gt - 1, 7))
                for gch in range(8):
                    e.dma(ld, t_yp(oh, i, gch), yp_sb[:, gch * 8:(gch + 1) * 8, :],
                          s_yp.transpose([1, 0, 2])[:, gch * 8:(gch + 1) * 8, i * T:(i + 1) * T])
                for oc in range(8):
                    g8 = gt * 8 + oc
                    if g8 >= 2:
                        e.wait(("dd", g8 - 2))
                    e.dma(ld, t_ratt(oh, i, oc), ratt_sb[:, g8 % 2, :],
                          s_ratt[oh * 8 + oc, :, i * T:(i + 1) * T])

    @stg.engine("vector")
    def _(e):
        e.wait_dma_all(ld, t_wsd(15))
        e.op(lambda v: v.tensor_scalar(wsd_sb[:, :], wsd_sb[:, :], 64.0, None, ALU.mult),
             label=("scales",))
        for oh in range(2):
            for j in range(64):
                e.wait_dma(ld, t_stripe(oh, j))
                e.op(lambda v, j=j: v.tensor_copy(
                    wd_sb[:, j, :, :],
                    wstripe[:, 0, :].rearrange("p (o m) -> p o m", m=128)),
                    label=("wcast", oh * 64 + j))
            for i in range(NT):
                for oc in range(8):
                    gt = oh * NT + i
                    g8 = gt * 8 + oc
                    ocg = oh * 8 + oc
                    e.wait(("md", gt, oc))
                    e.wait_dma(ld, t_ratt(oh, i, oc))
                    if g8 >= 2:
                        pg = g8 - 2
                        e.wait_dma(st, t_dh(pg // (NT * 8), (pg // 8) % NT, pg % 8))
                    e.op(lambda v, g8=g8, ocg=ocg: v.scalar_tensor_tensor(
                        dd_sb[:, g8 % 2, :], pd[:, g8 % 4, :], wsd_sb[:, ocg:ocg + 1],
                        ratt_sb[:, g8 % 2, :], ALU.mult, ALU.add), label=("dd", g8))

    @stg.engine("tensor")
    def _(e):
        for oh in range(2):
            e.wait(("wcast", oh * 64 + 63))
            e.wait(("scales",))
            for i in range(NT):
                gt = oh * NT + i
                for oc in range(8):
                    g8 = gt * 8 + oc
                    if g8 >= 4:
                        e.wait(("dd", g8 - 4))
                    for fc in range(64):
                        if oc == 0 and fc % 8 == 0:
                            e.wait_dma(ld, t_yp(oh, i, fc // 8))
                        e.unordered_op(lambda t, fc=fc, oc=oc, g8=g8: t.matmul(
                            pd[:, g8 % 4, :], wd_sb[:, fc, oc, :], yp_sb[:, fc, :],
                            start=(fc == 0), stop=(fc == 63)),
                            label=("md", gt, oc) if fc == 63 else None)

    @stg.engine("gpsimd")
    def _(e):
        for oh in range(2):
            for i in range(NT):
                for oc in range(8):
                    g8 = (oh * NT + i) * 8 + oc
                    e.wait(("dd", g8))
                    e.dma(st, t_dh(oh, i, oc), s_dh[oh * 8 + oc, :, i * T:(i + 1) * T], dd_sb[:, g8 % 2, :])
        e.wait_dma_all(st, t_dh(1, NT - 1, 7))

    stg.run()
    return ["s_dh"]



def build_stageG(nc, io, es, gs=None):
    x = io.d("x", [2, PPC, D], F32, "in")
    s_dh = io.d("s_dh", [16, 128, NT * T], F32, "scr_in")
    wg = io.d("wg", [D, 8], F32, "in")
    identf = io.d("identf", [128, 128], F32, "in")
    s_x1 = io.d("s_x1", [NT * 4, 128, D], F32, "scr_out")
    s_xn1 = io.d("s_xn1", [16, 128, NT * T], BF16, "scr_out")
    s_lg = io.d("s_lg", [8, NT * T], F32, "scr_out")

    E = es.enter_context
    idf = E(nc.sbuf_tensor("Gidf", [128, 128], F32))
    wg_sb = E(nc.sbuf_tensor("Gwg", [128, 16, 8], F32))
    dh_sb = E(nc.sbuf_tensor("Gdh", [128, 16, T], F32))
    x_sb = E(nc.sbuf_tensor("Gx", [128, 4, D], F32))
    x1_sb = E(nc.sbuf_tensor("Gx1", [128, 4, D], F32))
    xn1tok = E(nc.sbuf_tensor("Gxn1t", [128, 2, D], F32))
    xn1f = E(nc.sbuf_tensor("Gxn1f", [128, 16, T], BF16))
    xf32 = E(nc.sbuf_tensor("Gxf32", [128, 2, 4, 128], F32))
    junk = E(nc.sbuf_tensor("Gjunk", [128, D], F32))
    stat = E(nc.sbuf_tensor("Gstat", [128, 4, 8], F32))
    lg_sb = E(nc.sbuf_tensor("Glg", [8, T], F32))
    epsc = E(nc.sbuf_tensor("Geps", [128, 1], F32))
    pt = E(nc.psum_tensor("Gpt", [128, 2, T], F32))
    pxt = E(nc.psum_tensor("Gpxt", [128, 2, T], F32))
    pl = E(nc.psum_tensor("Gpl", [8, T], F32))
    stg = Stage(nc, es, "G", gs)
    gs = stg.gs
    ld = gs.dq("q_sync")
    st = gs.dq("q_gpsimd")

    T_IDF = ("idf",)
    def t_wg(dc):
        return ("wg", dc)
    def t_dh(i):
        return ("dh", i)
    def t_x(i, tc):
        return ("x", i, tc)
    def t_x1(i, tc):
        return ("x1", i, tc)
    def t_xn1(i, fc):
        return ("xn1", i, fc)
    def t_lg(i):
        return ("lg", i)


    @stg.engine("sync")
    def _(e):
        e.dma(ld, T_IDF, idf[:, :], identf)
        for dc in range(16):
            e.dma(ld, t_wg(dc), wg_sb[:, dc, :], wg[dc * 128:(dc + 1) * 128, :])
        for i in range(NT):
            if i >= 1:
                e.wait(("gtr", i - 1, 15))
            e.dma(ld, t_dh(i), dh_sb[:, :, :],
                  s_dh.transpose([1, 0, 2])[:, :, i * T:(i + 1) * T])
            if i >= 1:
                e.wait(("x1", i - 1, 15))
            for tc in range(4):
                sq = tc // 2
                p0 = i * 256 + (tc % 2) * 128
                e.dma(ld, t_x(i, tc), x_sb[:, tc, :], x[sq, p0:p0 + 128, :])

    @stg.engine("tensor")
    def _(e):
        e.wait_dma(ld, T_IDF)
        for i in range(NT):
            e.wait_dma(ld, t_dh(i))
            for fc in range(16):
                if i * 16 + fc >= 2:
                    pfc = i * 16 + fc - 2
                    e.wait(("x1", pfc // 16, pfc % 16))
                for tc in range(4):
                    e.unordered_op(lambda t, fc=fc, tc=tc: t.transpose(
                        pt[:, fc % 2, tc * 128:(tc + 1) * 128],
                        dh_sb[:, fc, tc * 128:(tc + 1) * 128], idf[:, :]))
                e.unordered_drain_op(lambda t: t.nop(), label=("gtr", i, fc))
            for tc in range(4):
                e.wait(("apply", i, tc))
                for fcg in range(4):
                    gx = (i * 4 + tc) * 4 + fcg
                    if gx >= 2:
                        e.wait(("xev", gx - 2))
                    for j in range(4):
                        fc = fcg * 4 + j
                        e.unordered_op(lambda t, tc=tc, fc=fc, j=j, fcg=fcg, i=i: t.transpose(
                            pxt[:, fcg % 2, j * 128:(j + 1) * 128],
                            xn1tok[:, (i * 4 + tc) % 2, fc * 128:(fc + 1) * 128], idf[:, :]))
                    e.unordered_drain_op(lambda t: t.nop(), label=("xtr", gx))
                    e.wait(("xev", gx))
                    if gx == 0 and i >= 1:
                        e.wait(("lgev", i - 1))
                    for j in range(4):
                        fc = fcg * 4 + j
                        e.unordered_op(lambda t, tc=tc, fc=fc, j=j, gx=gx: t.matmul(
                            pl[:, tc * 128:(tc + 1) * 128], wg_sb[:, fc, :],
                            xf32[:, gx % 2, j, :],
                            start=(tc == 0 and fc == 0), stop=(tc == 3 and fc == 15)),
                            label=("mlg", i) if (tc == 3 and fc == 15) else None)

    @stg.engine("vector")
    def _(e):
        e.op(lambda v: v.memset(epsc[:, :], EPS))
        for i in range(NT):
            for fc in range(16):
                e.wait(("gtr", i, fc))
                e.wait_dma_all(ld, t_x(i, 3))
                if i >= 1 and fc == 0:
                    e.wait_dma_all(st, t_x1(i - 1, 3))
                e.op(lambda v, fc=fc: v.tensor_tensor(
                    x1_sb[:, :, fc * 128:(fc + 1) * 128],
                    pt[:, fc % 2, :].rearrange("p (tc m) -> p tc m", m=128),
                    x_sb[:, :, fc * 128:(fc + 1) * 128], ALU.add), label=("x1", i, fc))
            for tc in range(4):
                tg = i * 4 + tc
                e.op(lambda v, tc=tc: v.reduce_sum(stat[:, tc, 0:1], x1_sb[:, tc, :], AX.X))
                e.wait(("sq", i, tc))
                e.op(lambda v, tc=tc: v.tensor_scalar(stat[:, tc, 2:3], stat[:, tc, 0:1], 1.0 / D, None, ALU.mult))
                e.op(lambda v, tc=tc: v.tensor_tensor(stat[:, tc, 7:8], stat[:, tc, 2:3], stat[:, tc, 2:3], ALU.mult))
                e.op(lambda v, tc=tc: v.tensor_scalar(stat[:, tc, 3:4], stat[:, tc, 1:2], 1.0 / D, None, ALU.mult))
                e.op(lambda v, tc=tc: v.tensor_tensor(stat[:, tc, 3:4], stat[:, tc, 3:4], stat[:, tc, 7:8], ALU.subtract),
                     label=("var", i, tc))
                e.wait(("sqrt", i, tc))
                e.op(lambda v, tc=tc: v.reciprocal(stat[:, tc, 5:6], stat[:, tc, 4:5]))
                e.op(lambda v, tc=tc: v.scalar_tensor_tensor(stat[:, tc, 6:7], stat[:, tc, 2:3], -1.0,
                                                             stat[:, tc, 5:6], ALU.mult, ALU.mult),
                     label=("rstd", i, tc))
                for fcg in range(4):
                    gx = (i * 4 + tc) * 4 + fcg
                    e.wait(("xtr", gx))
                    e.op(lambda v, gx=gx: v.tensor_copy(xf32[:, gx % 2, :, :],
                                                        pxt[:, gx % 2, :].rearrange("p (j m) -> p j m", m=128)))
                    if i >= 1 and tc == 0 and fcg == 0:
                        e.wait_dma_all(st, t_xn1(i - 1, 15))
                    e.op(lambda v, gx=gx, tc=tc, fcg=fcg: v.tensor_copy(
                        xn1f[:, fcg * 4:(fcg + 1) * 4, tc * 128:(tc + 1) * 128],
                        pxt[:, gx % 2, :].rearrange("p (j m) -> p j m", m=128)),
                        label=("xev", gx))
            e.wait(("mlg", i))
            if i >= 1:
                e.wait_dma(st, t_lg(i - 1))
            e.op(lambda v: v.tensor_copy(lg_sb[:, :], pl[:, :]), label=("lgev", i))

    @stg.engine("scalar")
    def _(e):
        for i in range(NT):
            for tc in range(4):
                tg = i * 4 + tc
                e.wait(("x1", i, 15))
                if tg >= 2:
                    e.wait(("xtr", (tg - 2) * 4 + 3))
                e.op(lambda s_, tc=tc: s_.activation(junk[:, :], x1_sb[:, tc, :], AF.Square,
                                                     bias=0.0, scale=1.0, accum_out=stat[:, tc, 1:2]),
                     label=("sq", i, tc))
                e.wait(("var", i, tc))
                e.op(lambda s_, tc=tc: s_.activation(stat[:, tc, 4:5], stat[:, tc, 3:4], AF.Sqrt,
                                                     bias=epsc[:, 0:1], scale=1.0), label=("sqrt", i, tc))
                e.wait(("rstd", i, tc))
                e.op(lambda s_, tc=tc, tg=tg: s_.activation(xn1tok[:, tg % 2, :], x1_sb[:, tc, :], AF.Identity,
                                                            bias=stat[:, tc, 6:7], scale=stat[:, tc, 5:6]),
                     label=("apply", i, tc))

    @stg.engine("gpsimd")
    def _(e):
        for i in range(NT):
            for tc in range(4):
                e.wait(("apply", i, tc))
                e.dma(st, t_x1(i, tc), s_x1[i * 4 + tc, :, :], x1_sb[:, tc, :])
            for fc in range(16):
                e.wait(("xev", (i * 4 + 3) * 4 + 3))
                e.dma(st, t_xn1(i, fc), s_xn1[fc, :, i * T:(i + 1) * T], xn1f[:, fc, :])
            e.wait(("lgev", i))
            e.dma(st, t_lg(i), s_lg[:, i * T:(i + 1) * T], lg_sb[:, :])
        e.wait_dma_all(st, t_lg(NT - 1))

    stg.run()
    return ["s_x1", "s_xn1", "s_lg"]



def build_stageH1(nc, io, es, gs=None):
    s_xn1 = io.d("s_xn1", [16, 128, NT * T], BF16, "scr_in")
    WqkvT = io.d("WqkvT", [D, 3072], F32, "in")
    selS = io.d("selS", [128, 16, 16], BF16, "in")
    selB = io.d("selB", [16, 16, 128], BF16, "in")
    s_attn = io.d("s_attn", [16, 128, NT * T], BF16, "scr_out")

    E = es.enter_context
    wq_sb = E(nc.sbuf_tensor("H1w", [128, 16, 24, 128], BF16))
    wstripe = E(nc.sbuf_tensor("H1stripe", [128, 2, 3072], F32))
    selS_sb = E(nc.sbuf_tensor("H1selS", [128, 16, 16], BF16))
    selB_sb = E(nc.sbuf_tensor("H1selB", [16, 16, 128], BF16))
    xn1_sb = E(nc.sbuf_tensor("H1xn", [128, 16, T], BF16))
    q_sb = E(nc.sbuf_tensor("H1q", [128, 16, T], BF16))
    k_sb = E(nc.sbuf_tensor("H1k", [128, 4, T], BF16))
    v_sb = E(nc.sbuf_tensor("H1v", [128, 4, T], BF16))
    pr_sb = E(nc.sbuf_tensor("H1pr", [128, 2, T], BF16))
    s_sb = E(nc.sbuf_tensor("H1s", [16, T], F32))
    sm_sb = E(nc.sbuf_tensor("H1sm", [16, 6, H], F32))
    wts_sb = E(nc.sbuf_tensor("H1wts", [16, T], BF16))
    wb_sb = E(nc.sbuf_tensor("H1wb", [128, 2, T], BF16))
    t0_sb = E(nc.sbuf_tensor("H1t0", [128, 2, T], BF16))
    attn_sb = E(nc.sbuf_tensor("H1attn", [128, 16, T], BF16))
    pq = E(nc.psum_tensor("H1pq", [128, 2, T], F32))
    ps = E(nc.psum_tensor("H1ps", [16, T], F32))
    pw = E(nc.psum_tensor("H1pw", [128, 2, T], F32))
    stg = Stage(nc, es, "H1", gs)
    gs = stg.gs
    ld = gs.dq("q_sync")
    st = gs.dq("q_gpsimd")

    def t_stripe(j):
        return ("stripe", j)
    T_SELS = ("sels",)
    T_SELB = ("selb",)
    def t_xn(i):
        return ("xn", i)
    def t_attn(i, fc):
        return ("attn", i, fc)


    @stg.engine("sync")
    def _(e):
        for dc in range(16):
            if dc >= 2:
                e.wait(("wcast", dc - 2))
            e.dma(ld, t_stripe(dc), wstripe[:, dc % 2, :], WqkvT[dc * 128:(dc + 1) * 128, :])
        e.dma(ld, T_SELS, selS_sb[:, :, :], selS)
        e.dma(ld, T_SELB, selB_sb[:, :, :], selB)
        for i in range(NT):
            if i >= 1:
                e.wait(("mq", i - 1, 23))
            e.dma(ld, t_xn(i), xn1_sb[:, :, :],
                  s_xn1.transpose([1, 0, 2])[:, :, i * T:(i + 1) * T])

    @stg.engine("tensor")
    def _(e):
        e.wait_dma(ld, T_SELS)
        e.wait_dma(ld, T_SELB)
        e.wait(("wcast", 15))
        for i in range(NT):
            e.wait_dma(ld, t_xn(i))
            for j in range(24):
                gj = i * 24 + j
                if gj >= 2:
                    e.wait(("qev", gj - 2))
                for dc in range(16):
                    e.unordered_op(lambda t, dc=dc, j=j: t.matmul(pq[:, j % 2, :], wq_sb[:, dc, j, :],
                                                                  xn1_sb[:, dc, :], start=(dc == 0), stop=(dc == 15)),
                                   label=("mq", i, j) if dc == 15 else None)
            if i >= 1:
                e.wait(("sev", i - 1))
            for h in range(16):
                e.wait(("pr", i, h))
                e.unordered_op(lambda t, h=h: t.matmul(ps[:, 0:H], selS_sb[:, h, :], pr_sb[:, h % 2, 0:H],
                                                       start=(h == 0), stop=False))
                e.unordered_op(lambda t, h=h: t.matmul(ps[:, H:T], selS_sb[:, h, :], pr_sb[:, h % 2, H:T],
                                                       start=False, stop=(h == 15)),
                               label=("msh", i, h))
            for h in range(16):
                e.wait(("wts", i))
                if i * 16 + h >= 2:
                    ph_ = i * 16 + h - 2
                    e.wait(("ao", ph_ // 16, ph_ % 16))
                e.unordered_op(lambda t, h=h: t.matmul(pw[:, h % 2, :], selB_sb[:, h, :], wts_sb[:, :],
                                                       start=True, stop=True),
                               label=("mw", i, h))

    @stg.engine("vector")
    def _(e):
        for dc in range(16):
            e.wait_dma(ld, t_stripe(dc))
            e.op(lambda v, dc=dc: v.tensor_copy(wq_sb[:, dc, :, :],
                                                wstripe[:, dc % 2, :].rearrange("p (j m) -> p j m", m=128)),
                 label=("wcast", dc))
        for i in range(NT):
            for j in range(24):
                gj = i * 24 + j
                e.wait(("mq", i, j))
                dst = (lambda j: q_sb[:, j, :] if j < 16 else (k_sb[:, j - 16, :] if j < 20 else v_sb[:, j - 20, :]))
                e.op(lambda v, j=j: v.tensor_copy(dst(j), pq[:, j % 2, :]), label=("qev", gj))
            for h in range(16):
                gh = i * 16 + h
                if gh >= 2:
                    e.wait(("msh", (gh - 2) // 16, (gh - 2) % 16))
                e.op(lambda v, h=h: v.tensor_tensor(pr_sb[:, h % 2, 0:H], q_sb[:, h, H:T],
                                                    k_sb[:, h // 4, 0:H], ALU.mult))
                e.op(lambda v, h=h: v.tensor_tensor(pr_sb[:, h % 2, H:T], q_sb[:, h, H:T],
                                                    k_sb[:, h // 4, H:T], ALU.mult), label=("pr", i, h))
            e.wait(("msh", i, 15))
            e.op(lambda v: v.tensor_copy(s_sb[:, :], ps[:, :]), label=("sev", i))
            e.op(lambda v: v.tensor_tensor(sm_sb[:, 0, :], s_sb[:, 0:H], s_sb[:, H:T], ALU.max))
            e.op(lambda v: v.tensor_tensor(sm_sb[:, 1, :], s_sb[:, 0:H], sm_sb[:, 0, :], ALU.subtract),
                 label=("d0", i))
            e.op(lambda v: v.tensor_tensor(sm_sb[:, 2, :], s_sb[:, H:T], sm_sb[:, 0, :], ALU.subtract),
                 label=("d1", i))
            e.wait(("e1", i))
            e.op(lambda v: v.tensor_tensor(sm_sb[:, 0, :], sm_sb[:, 4, :], sm_sb[:, 5, :], ALU.add))
            e.op(lambda v: v.reciprocal(sm_sb[:, 3, :], sm_sb[:, 0, :]))
            e.op(lambda v: v.tensor_tensor(wts_sb[:, 0:H], sm_sb[:, 4, :], sm_sb[:, 3, :], ALU.mult))
            e.op(lambda v: v.tensor_tensor(wts_sb[:, H:T], sm_sb[:, 5, :], sm_sb[:, 3, :], ALU.mult),
                 label=("wts", i))
            for h in range(16):
                gh = i * 16 + h
                e.wait(("mw", i, h))
                if i >= 1 and h == 0:
                    e.wait_dma_all(st, t_attn(i - 1, 15))
                e.op(lambda v, h=h: v.tensor_copy(wb_sb[:, h % 2, :], pw[:, h % 2, :]))
                e.op(lambda v, h=h: v.tensor_tensor(t0_sb[:, h % 2, 0:H], wb_sb[:, h % 2, 0:H],
                                                    v_sb[:, h // 4, 0:H], ALU.mult))
                e.op(lambda v, h=h: v.tensor_tensor(t0_sb[:, h % 2, H:T], wb_sb[:, h % 2, H:T],
                                                    v_sb[:, h // 4, H:T], ALU.mult))
                e.op(lambda v, h=h: v.tensor_tensor(attn_sb[:, h, H:T], t0_sb[:, h % 2, 0:H],
                                                    t0_sb[:, h % 2, H:T], ALU.add))
                e.op(lambda v, h=h: v.tensor_copy(attn_sb[:, h, 0:H], v_sb[:, h // 4, 0:H]),
                     label=("ao", i, h))

    @stg.engine("scalar")
    def _(e):
        for i in range(NT):
            e.wait(("d0", i))
            e.op(lambda s_: s_.activation(sm_sb[:, 4, :], sm_sb[:, 1, :], AF.Exp, bias=0.0, scale=1.0))
            e.wait(("d1", i))
            e.op(lambda s_: s_.activation(sm_sb[:, 5, :], sm_sb[:, 2, :], AF.Exp, bias=0.0, scale=1.0),
                 label=("e1", i))

    @stg.engine("gpsimd")
    def _(e):
        for i in range(NT):
            for fc in range(16):
                e.wait(("ao", i, fc))
                e.dma(st, t_attn(i, fc), s_attn[fc, :, i * T:(i + 1) * T], attn_sb[:, fc, :])
        e.wait_dma_all(st, t_attn(NT - 1, 15))

    stg.run()
    return ["s_attn"]



def build_stageH2(nc, io, es, gs=None):
    s_attn = io.d("s_attn", [16, 128, NT * T], BF16, "scr_in")
    WprojT = io.d("WprojT", [D, D], F32, "in")
    s_r1 = io.d("s_r1", [16, 128, NT * T], F32, "scr_out")

    E = es.enter_context
    wp_sb = E(nc.sbuf_tensor("H2w", [128, 16, 16, 128], BF16))
    wstripe = E(nc.sbuf_tensor("H2stripe", [128, 2, D], F32))
    at_sb = E(nc.sbuf_tensor("H2at", [128, 16, T], BF16))
    r1_sb = E(nc.sbuf_tensor("H2r1", [128, 4, T], F32))
    po = E(nc.psum_tensor("H2po", [128, 4, T], F32))
    stg = Stage(nc, es, "H2", gs)
    gs = stg.gs
    ld = gs.dq("q_sync")
    st = gs.dq("q_gpsimd")

    def t_stripe(j):
        return ("stripe", j)
    def t_at(i):
        return ("at", i)
    def t_r1(i, oc):
        return ("r1", i, oc)


    @stg.engine("sync")
    def _(e):
        for dc in range(16):
            if dc >= 2:
                e.wait(("wcast", dc - 2))
            e.dma(ld, t_stripe(dc), wstripe[:, dc % 2, :], WprojT[dc * 128:(dc + 1) * 128, :])
        for i in range(NT):
            if i >= 1:
                e.wait(("mo", i - 1, 15))
            e.dma(ld, t_at(i), at_sb[:, :, :],
                  s_attn.transpose([1, 0, 2])[:, :, i * T:(i + 1) * T])

    @stg.engine("tensor")
    def _(e):
        e.wait(("wcast", 15))
        for i in range(NT):
            e.wait_dma(ld, t_at(i))
            for oc in range(16):
                go = i * 16 + oc
                if go >= 4:
                    e.wait(("r1ev", go - 4))
                for ec in range(16):
                    e.unordered_op(lambda t, ec=ec, oc=oc, go=go: t.matmul(po[:, go % 4, :], wp_sb[:, ec, oc, :],
                                                                           at_sb[:, ec, :], start=(ec == 0), stop=(ec == 15)),
                                   label=("mo", i, oc) if ec == 15 else None)

    @stg.engine("vector")
    def _(e):
        for dc in range(16):
            e.wait_dma(ld, t_stripe(dc))
            e.op(lambda v, dc=dc: v.tensor_copy(wp_sb[:, dc, :, :],
                                                wstripe[:, dc % 2, :].rearrange("p (o m) -> p o m", m=128)),
                 label=("wcast", dc))
        for i in range(NT):
            for oc in range(16):
                go = i * 16 + oc
                e.wait(("mo", i, oc))
                if go >= 4:
                    e.wait_dma(st, t_r1((go - 4) // 16, (go - 4) % 16))
                e.op(lambda v, oc=oc, go=go: v.tensor_copy(r1_sb[:, go % 4, :], po[:, go % 4, :]),
                     label=("r1ev", go))

    @stg.engine("gpsimd")
    def _(e):
        for i in range(NT):
            for oc in range(16):
                go = i * 16 + oc
                e.wait(("r1ev", go))
                e.dma(st, t_r1(i, oc), s_r1[oc, :, i * T:(i + 1) * T], r1_sb[:, go % 4, :])
        e.wait_dma_all(st, t_r1(NT - 1, 15))

    stg.run()
    return ["s_r1"]



def build_stageI(nc, io, es, gs=None):
    s_xn1 = io.d("s_xn1", [16, 128, NT * T], BF16, "scr_in")
    s_lg = io.d("s_lg", [8, NT * T], F32, "scr_in")
    s_r1 = io.d("s_r1", [16, 128, NT * T], F32, "scr_in")
    s_x1 = io.d("s_x1", [NT * 4, 128, D], F32, "scr_in")
    s_web_ap = io.d("s_web", [8, 16, 16, 128, 128], BF16, "scr_in")
    b_moe = io.d("b_moe", [8, D], F32, "in")
    selM = io.d("selM", [8, 8, 128], BF16, "in")
    identf = io.d("identf", [128, 128], F32, "in")
    identb = io.d("identb", [128, 128], BF16, "in")
    out = io.d("out", [2, PPC, D], F32, "out")

    E = es.enter_context
    idf = E(nc.sbuf_tensor("Iidf", [128, 128], F32))
    idb = E(nc.sbuf_tensor("Iidb", [128, 128], BF16))
    selM_sb = E(nc.sbuf_tensor("IselM", [8, 8, 128], BF16))
    bm_f = E(nc.sbuf_tensor("Ibmf", [8, D], F32))
    bm_sb = E(nc.sbuf_tensor("Ibm", [8, 16, 128], BF16))
    we_sb = E(nc.sbuf_tensor("Iwe", [128, 3, 16, 128], BF16))
    xn1_sb = E(nc.sbuf_tensor("Ixn", [128, 16, T], BF16))
    lg_sb = E(nc.sbuf_tensor("Ilg", [8, T], F32))
    lt_sb = E(nc.sbuf_tensor("Ilt", [128, 4, 8], F32))
    et_sb = E(nc.sbuf_tensor("Iet", [128, 4, 8], F32))
    rm_sb = E(nc.sbuf_tensor("Irm", [128, 4, 4], F32))
    m_sb = E(nc.sbuf_tensor("Im", [128, 4, 8], BF16))
    mrows = E(nc.sbuf_tensor("Imrows", [8, T], BF16))
    mb_sb = E(nc.sbuf_tensor("Imb", [128, 2, T], BF16))
    z_sb = E(nc.sbuf_tensor("Iz", [128, 8, 16, T], BF16))
    r1_sb = E(nc.sbuf_tensor("Ir1", [128, 2, T], F32))
    sum1_sb = E(nc.sbuf_tensor("Isum1", [128, 2, T], F32))
    x1p_sb = E(nc.sbuf_tensor("Ix1p", [128, 2, 4, 128], F32))
    x2p_sb = E(nc.sbuf_tensor("Ix2p", [128, 2, 4, 128], F32))
    pltr = E(nc.psum_tensor("Ipltr", [128, 32], F32))
    pmr = E(nc.psum_tensor("Ipmr", [8, T], BF16))
    pb = E(nc.psum_tensor("Ipb", [128, 2, T], F32))
    pm = E(nc.psum_tensor("Ipm", [128, 2, T], F32))
    pxt = E(nc.psum_tensor("Ipxt", [128, 2, T], F32))
    stg = Stage(nc, es, "I", gs)
    gs = stg.gs
    ld = gs.dq("q_sync")
    st = gs.dq("q_gpsimd")
    so = gs.dq("q_scalar")
    wq = gs.dq("q_gpsimd")

    T_IDF = ("idf",)
    T_IDB = ("idb",)
    T_SELM = ("selm",)
    T_BM = ("bm",)
    def t_xn(i):
        return ("xn", i)
    def t_lg(i):
        return ("lg", i)
    def t_r1(i, oc):
        return ("r1", i, oc)
    def t_x1p(i, oc):
        return ("x1p", i, oc)
    def t_out(i, oc, tc):
        return ("out", i, oc, tc)
    def t_we(i, oc, e8):
        return ("we", i, oc, e8)


    @stg.engine("sync")
    def _(e):
        e.dma(ld, T_IDF, idf[:, :], identf)
        e.dma(ld, T_IDB, idb[:, :], identb)
        e.dma(ld, T_SELM, selM_sb[:, :, :], selM)
        e.dma(ld, T_BM, bm_f[:, :], b_moe)
        for i in range(NT):
            if i >= 1:
                e.wait(("mmb", (i - 1) * 128 + 127))
            e.dma(ld, t_xn(i), xn1_sb[:, :, :],
                  s_xn1.transpose([1, 0, 2])[:, :, i * T:(i + 1) * T])
            if i >= 1:
                e.wait(("lgread", i - 1))
            e.dma(ld, t_lg(i), lg_sb[:, :], s_lg[:, i * T:(i + 1) * T])
            for oc in range(16):
                go = i * 16 + oc
                if go >= 2:
                    e.wait(("s1", go - 2))
                e.dma(ld, t_r1(i, oc), r1_sb[:, go % 2, :], s_r1[oc, :, i * T:(i + 1) * T])
                if go >= 2:
                    e.wait(("x2", go - 2))
                e.dma(ld, t_x1p(i, oc), x1p_sb[:, go % 2, :, :],
                      s_x1[i * 4:(i + 1) * 4, :, oc * 128:(oc + 1) * 128].transpose([1, 0, 2]))

    @stg.engine("gpsimd")
    def _(e):
        for i in range(NT):
            for oc in range(16):
                for e8 in range(8):
                    bk = (i * 16 + oc) * 8 + e8
                    if bk >= 3:
                        e.wait(("mmb", bk - 3))
                    e.dma(wq, t_we(i, oc, e8), we_sb[:, bk % 3, :, :],
                          s_web_ap[e8, oc].transpose([1, 0, 2]))
        e.wait(("done",))

    @stg.engine("tensor")
    def _(e):
        e.wait_dma(ld, T_IDF)
        e.wait_dma(ld, T_IDB)
        e.wait_dma(ld, T_SELM)
        e.wait(("bmcast",))
        def emit_str(j1):
            e.wait(("s1", j1))
            if j1 >= 2:
                e.wait(("x2", j1 - 2))
            for tc in range(4):
                e.unordered_op(lambda t, tc=tc, j1=j1: t.transpose(
                    pxt[:, j1 % 2, tc * 128:(tc + 1) * 128], sum1_sb[:, j1 % 2, tc * 128:(tc + 1) * 128],
                    idf[:, :]))
            e.unordered_drain_op(lambda t: t.nop(), label=("str", j1))
        for i in range(NT):
            if i > 0:
                emit_str(i * 16 - 1)
            e.wait_dma(ld, t_lg(i))
            if i >= 1:
                e.wait(("lgread", i - 1))
            for tc in range(4):
                e.unordered_op(lambda t, tc=tc: t.transpose(pltr[:, tc * 8:(tc + 1) * 8],
                                                            lg_sb[0:8, tc * 128:(tc + 1) * 128],
                                                            idf[0:8, 0:8]))
            e.unordered_drain_op(lambda t: t.nop(), label=("ltr", i))
            if i >= 1:
                e.wait(("mrev", i - 1))
            for tc in range(4):
                e.wait(("mask", i, tc))
                e.unordered_op(lambda t, tc=tc: t.transpose(pmr[:, tc * 128:(tc + 1) * 128],
                                                            m_sb[:, tc, :], idb[:, :]))
            e.unordered_drain_op(lambda t: t.nop(), label=("mtr", i))
            e.wait_dma(ld, t_xn(i))
            for e8 in range(8):
                e.wait(("mrev", i))
                if i * 8 + e8 >= 2:
                    pz = i * 8 + e8 - 2
                    e.wait(("mbev", pz // 8, pz % 8))
                e.unordered_op(lambda t, e8=e8: t.matmul(pb[:, e8 % 2, :], selM_sb[:, e8, :], mrows[:, :],
                                                         start=True, stop=True),
                               label=("mbb", i, e8))
            for oc in range(16):
                go = i * 16 + oc
                if go >= 2:
                    e.wait(("s1", go - 2))
                e.wait(("z", i, 7))
                e.unordered_op(lambda t, oc=oc: t.matmul(pm[:, oc % 2, :], bm_sb[:, oc, :], mrows[:, :],
                                                         start=True, stop=False))
                for e8 in range(8):
                    bk = go * 8 + e8
                    e.wait_dma(wq, t_we(i, oc, e8))
                    for kc in range(16):
                        e.unordered_op(lambda t, kc=kc, e8=e8, oc=oc, bk=bk: t.matmul(
                            pm[:, oc % 2, :], we_sb[:, bk % 3, kc, :], z_sb[:, e8, kc, :],
                            start=False, stop=(e8 == 7 and kc == 15)),
                            label=("mmb", bk) if kc == 15 else None)
                if oc >= 1:
                    emit_str(go - 1)

        emit_str(NT * 16 - 1)

    @stg.engine("vector")
    def _(e):
        e.wait_dma(ld, T_BM)
        e.op(lambda v: v.tensor_copy(bm_sb[:, :, :], bm_f.rearrange("p (o m) -> p o m", m=128)),
             label=("bmcast",))
        for i in range(NT):
            e.wait(("ltr", i))
            e.op(lambda v: v.tensor_copy(lt_sb[:, :, :], pltr[:, :].rearrange("p (tc j) -> p tc j", j=8)),
                 label=("lgread", i))
            for tc in range(4):
                e.op(lambda v, tc=tc: v.reduce_max(rm_sb[:, tc, 0:1], lt_sb[:, tc, :], AX.X))
                e.op(lambda v, tc=tc: v.tensor_scalar(rm_sb[:, tc, 1:2], rm_sb[:, tc, 0:1], -1.0, None, ALU.mult),
                     label=("nm", i, tc))
                e.wait(("et", i, tc))
                e.op(lambda v, tc=tc: v.reduce_sum(rm_sb[:, tc, 2:3], et_sb[:, tc, :], AX.X))
                e.op(lambda v, tc=tc: v.reciprocal(rm_sb[:, tc, 3:4], rm_sb[:, tc, 2:3]))
                e.op(lambda v, tc=tc: v.tensor_scalar(m_sb[:, tc, :], et_sb[:, tc, :], 1.0, rm_sb[:, tc, 3:4],
                                                      ALU.is_ge, ALU.mult), label=("mask", i, tc))
            e.wait(("mtr", i))
            e.op(lambda v: v.tensor_copy(mrows[:, :], pmr[:, :]), label=("mrev", i))
            for e8 in range(8):
                e.wait(("mbb", i, e8))
                e.op(lambda v, e8=e8: v.tensor_copy(mb_sb[:, e8 % 2, :], pb[:, e8 % 2, :]),
                     label=("mbev", i, e8))
                if i >= 1 and e8 == 0:
                    e.wait(("mmb", (i - 1) * 128 + 127))
                for kc in range(16):
                    ins = e.op(lambda v, e8=e8, kc=kc: v.tensor_tensor(z_sb[:, e8, kc, :], xn1_sb[:, kc, :],
                                                                       mb_sb[:, e8 % 2, :], ALU.mult),
                               label=("z", i, e8) if kc == 15 else None)
            for oc in range(16):
                go = i * 16 + oc
                e.wait(("mmb", go * 8 + 7))
                e.wait_dma(ld, t_r1(i, oc))
                if go >= 2:
                    e.wait(("str", go - 2))
                e.op(lambda v, go=go: v.tensor_tensor(sum1_sb[:, go % 2, :], pm[:, go % 2, :],
                                                      r1_sb[:, go % 2, :], ALU.add), label=("s1", go))
                e.wait(("str", go))
                e.wait_dma(ld, t_x1p(i, oc))
                if go >= 2:
                    for tc in range(4):
                        e.wait_dma(so, t_out((go - 2) // 16, (go - 2) % 16, tc))
                e.op(lambda v, go=go: v.tensor_tensor(x2p_sb[:, go % 2, :, :],
                                                      pxt[:, go % 2, :].rearrange("p (tc m) -> p tc m", m=128),
                                                      x1p_sb[:, go % 2, :, :], ALU.add), label=("x2", go))

    @stg.engine("scalar")
    def _(e):
        for i in range(NT):
            for tc in range(4):
                e.wait(("nm", i, tc))
                e.op(lambda s_, tc=tc: s_.activation(et_sb[:, tc, :], lt_sb[:, tc, :], AF.Exp,
                                                     bias=rm_sb[:, tc, 1:2], scale=1.0), label=("et", i, tc))
            for oc in range(16):
                go = i * 16 + oc
                e.wait(("x2", go))
                for tc in range(4):
                    sq = tc // 2
                    p0 = i * 256 + (tc % 2) * 128
                    e.dma(so, t_out(i, oc, tc), out[sq, p0:p0 + 128, oc * 128:(oc + 1) * 128],
                          x2p_sb[:, go % 2, tc, :])
        e.wait_dma_all(so, t_out(NT - 1, 15, 3))
        e.op(lambda s_: s_.activation(et_sb[:, 0, :], et_sb[:, 0, :], AF.Identity, bias=0.0, scale=1.0),
             label=("done",))

    stg.run()
    return ["out"]



def _shard(x, c):
    return np.ascontiguousarray(x[:, c * PPC:(c + 1) * PPC, :])


def build_AG(nc, io, ges):
    totals = io.d("totals", [2, D], F32, "scr_out")
    ag = io.d("ag", [16, D], F32, "scr_out")
    ccsem = ges.enter_context(nc.semaphore("ccsem"))
    with nc.Block() as blk:
        @blk.gpsimd
        def _(g):
            g.collective_compute(
                "AllGather", ALU.bypass,
                replica_groups=[list(range(NCORES))],
                ins=[totals.opt()], outs=[ag.opt()],
            ).then_inc(ccsem, 1)
            g.wait_ge(ccsem, 1)


def _in_maps(inputs, w, c, shards):
    maps = []
    for ci in range(NCORES):
        maps.append({
            "x": shards[ci], "pmask": _prefix_mask(ci),
            "WkT": w["WkT"], "WvT": w["WvT"], "bk": w["bk"], "bv": w["bv"],
            "WqT": w["WqT"], "WoT": w["WoT"], "bq": w["bq"], "bo": w["bo"],
            "SU_ug": w["SU_ug"], "SU_dp": w["SU_dp"],
            "Wscale_ug": w["Wscale_ug"], "Wscale_dp": w["Wscale_dp"],
            "WupT": w["WupT"], "WgateT": w["WgateT"], "WdownT": w["WdownT"],
            "wg": w["wg"], "WqkvT": w["WqkvT"], "WprojT": w["WprojT"],
            "We_blk": w["We_blk"], "b_moe": w["b_moe"],
            "identb": c["identb"], "identf": c["identf"],
            "H128a": c["H128a"], "H128s": c["H128s"], "Hdd": c["Hdd"],
            "selS": c["selS"], "selB": c["selB"], "selM": c["selM"],
        })
    return maps


def kernel(**inputs):
    inputs = {k: np.asarray(v) for k, v in inputs.items()}
    w = _host_weights(inputs)
    c = _host_constants()
    x = inputs["L0_attn_x"]
    shards = [_shard(x, ci) for ci in range(NCORES)]

    nc = bass.Bass("TRN2", target_bir_lowering=False)
    io = IO(nc, fused=True)
    with contextlib.ExitStack() as ges:
        gs = GlobalSync(nc, ges)
        for builder in [build_stageA]:
            with contextlib.ExitStack() as es:
                builder(nc, io, es, gs)
        build_AG(nc, io, ges)
        for builder in [build_stageB1, build_stageB2, build_stageD, build_stageE,
                        build_stageF, build_stageG, build_stageH1, build_stageH2,
                        build_stageI]:
            with contextlib.ExitStack() as es:
                builder(nc, io, es, gs)
        res = run_bass_kernel_spmd(nc, _in_maps(inputs, w, c, shards),
                                   core_ids=list(range(NCORES)))
    return np.ascontiguousarray(
        np.concatenate([np.asarray(res.results[ci]["out"], np.float32)
                        for ci in range(NCORES)], axis=1))


def kernel_unfused(**inputs):
    inputs = {k: np.asarray(v) for k, v in inputs.items()}
    w = _host_weights(inputs)
    c = _host_constants()
    x = inputs["L0_attn_x"]
    shards = [_shard(x, ci) for ci in range(NCORES)]

    def launch(builder, per_core_maps):
        nc = bass.Bass("TRN2", target_bir_lowering=False)
        io = IO(nc, fused=False)
        with contextlib.ExitStack() as es:
            builder(nc, io, es)
            res = run_bass_kernel_spmd(nc, per_core_maps, core_ids=list(range(NCORES)))
        return res.results

    rA = launch(build_stageA, [
        {"x": shards[ci], "WkT": w["WkT"], "WvT": w["WvT"], "bk": w["bk"], "bv": w["bv"],
         "identb": c["identb"]} for ci in range(NCORES)])
    ag = np.concatenate([np.asarray(rA[ci]["totals"]) for ci in range(NCORES)], axis=0)

    rB1 = launch(build_stageB1, [
        {"s_xn": rA[ci]["s_xn"], "s_kv": rA[ci]["s_kv"], "ag": ag, "pmask": _prefix_mask(ci),
         "WqT": w["WqT"], "WoT": w["WoT"], "bq": w["bq"], "bo": w["bo"], "identf": c["identf"]}
        for ci in range(NCORES)])

    rB2 = launch(build_stageB2, [
        {"s_xn": rA[ci]["s_xn"], "SU_ug": w["SU_ug"], "H128a": c["H128a"]}
        for ci in range(NCORES)])

    rD = launch(build_stageD, [
        {"s_y": rB2[ci]["s_y"], "WupT": w["WupT"], "WgateT": w["WgateT"],
         "Wscale_ug": w["Wscale_ug"], "SU_dp": w["SU_dp"], "H128s": c["H128s"],
         "We_blk": w["We_blk"]}
        for ci in range(NCORES)])

    rE = launch(build_stageE, [
        {"s_wt": rD[ci]["s_wt"], "Hdd": c["Hdd"]} for ci in range(NCORES)])

    rF = launch(build_stageF, [
        {"s_yp": rE[ci]["s_yp"], "s_ratt": rB1[ci]["s_ratt"], "WdownT": w["WdownT"],
         "Wscale_dp": w["Wscale_dp"]} for ci in range(NCORES)])

    rG = launch(build_stageG, [
        {"x": shards[ci], "s_dh": rF[ci]["s_dh"], "wg": w["wg"], "identf": c["identf"]}
        for ci in range(NCORES)])

    rH1 = launch(build_stageH1, [
        {"s_xn1": rG[ci]["s_xn1"], "WqkvT": w["WqkvT"], "selS": c["selS"], "selB": c["selB"]}
        for ci in range(NCORES)])

    rH2 = launch(build_stageH2, [
        {"s_attn": rH1[ci]["s_attn"], "WprojT": w["WprojT"]} for ci in range(NCORES)])

    rI = launch(build_stageI, [
        {"s_xn1": rG[ci]["s_xn1"], "s_lg": rG[ci]["s_lg"], "s_r1": rH2[ci]["s_r1"],
         "s_x1": rG[ci]["s_x1"], "s_web": rD[ci]["s_web"], "b_moe": w["b_moe"],
         "selM": c["selM"], "identf": c["identf"], "identb": c["identb"]}
        for ci in range(NCORES)])

    return np.ascontiguousarray(
        np.concatenate([np.asarray(rI[ci]["out"], np.float32) for ci in range(NCORES)], axis=1))
